# version 26
# baseline (speedup 1.0000x reference)
import contextlib
import numpy as np
import concourse.bass as bass
import concourse.mybir as mybir

F32 = mybir.dt.float32
BF16 = mybir.dt.bfloat16
AF = mybir.ActivationFunctionType
ALU = mybir.AluOpType
AX = mybir.AxisListType

ENGINES = ("pe", "act", "dve", "pool", "sp")
DMAQ = {"qs": "sp", "qa": "act", "qp": "pool"}
NLANES = 4
LANES = [f"{q}{l}" for q in DMAQ for l in range(NLANES)]


class Buf:
    __slots__ = ("name", "w", "r")

    def __init__(self, name):
        self.name = name
        self.w = {}
        self.r = {}


class TB:
    __slots__ = ("t", "buf")

    def __init__(self, t, buf):
        self.t = t
        self.buf = buf

    def __getitem__(self, key):
        return self.t[key]


PREP_SEMS = ("pool",) + tuple(f"qp{l}" for l in range(NLANES))
QS_SEMS = tuple(f"qs{l}" for l in range(NLANES))
HARD_FLUSH_TAGS = set()


class Prog:
    def barrier_partial(self):
        self.barrier(exclude=PREP_SEMS)

    def barrier_no_qs(self):
        self.barrier(exclude=PREP_SEMS + QS_SEMS)

    def __init__(self, nc):
        self.nc = nc
        self.stack = contextlib.ExitStack()
        self.sems = {}
        for name in list(ENGINES) + LANES:
            self.sems[name] = self.stack.enter_context(nc.semaphore(f"sem_{name}"))
        self.counts = {name: 0 for name in list(ENGINES) + LANES}
        self.qrr = {q: 0 for q in DMAQ}
        self.last_wait = {}
        self.ops = {e: [] for e in ENGINES}
        self.n_ops = 0
        self.pb = [self.sbuf_like_psum(i) for i in range(8)]

    def sbuf_like_psum(self, i):
        t = self.stack.enter_context(self.nc.psum_tensor(f"pb{i}", [128, 512], F32))
        return TB(t, Buf(f"pb{i}"))

    def sbuf(self, name, shape, dtype, stack=None, **kw):
        st = stack if stack is not None else self.stack
        self.uid = getattr(self, "uid", 0) + 1
        name = f"{name}_u{self.uid}"
        t = st.enter_context(self.nc.sbuf_tensor(name, list(shape), dtype, **kw))
        return TB(t, Buf(name))

    def dram(self, name, shape, dtype, kind="Internal"):
        t = self.nc.dram_tensor(name, list(shape), dtype, kind=kind)
        return TB(t.ap(), Buf(name))

    def _wait_list(self, eng_stream, own_sem, reads, writes):
        need = {}
        def add(s, v):
            if need.get(s, 0) < v:
                need[s] = v
        for b in reads:
            for s, v in b.w.items():
                add(s, v)
        for b in writes:
            for s, v in b.w.items():
                if s != own_sem or s in ENGINES:
                    add(s, v)
            for s, v in b.r.items():
                add(s, v)
        waits = []
        for s, v in need.items():
            if s == "pe" and eng_stream == "pe":
                continue
            assert v <= self.counts[s] * (16 if s not in ENGINES else 1), \
                f"wait on future token {s}={v}"
            key = (eng_stream, s)
            if self.last_wait.get(key, 0) >= v:
                continue
            self.last_wait[key] = v
            waits.append((self.sems[s], v))
        return waits

    def emit(self, eng, fn, reads=(), writes=(), inc=1):
        assert eng in ENGINES
        waits = self._wait_list(eng, eng, reads, writes)
        self.counts[eng] += inc
        tok = (eng, self.counts[eng])
        self.ops[eng].append((waits, fn, self.sems[eng], inc))
        self.n_ops += 1
        for b in reads:
            b.r[tok[0]] = tok[1]
        for b in writes:
            b.w = {tok[0]: tok[1]}
            b.r = {}
        return tok

    def dma(self, q, fn, reads=(), writes=()):
        eng = DMAQ[q]
        lane = f"{q}{self.qrr[q]}"
        self.qrr[q] = (self.qrr[q] + 1) % NLANES
        waits = self._wait_list(eng, lane, reads, writes)
        prev = self.counts[lane] * 16
        key = (eng, lane)
        if prev > 0 and self.last_wait.get(key, 0) < prev:
            self.last_wait[key] = prev
            waits.append((self.sems[lane], prev))
        self.counts[lane] += 1
        tok = (lane, self.counts[lane] * 16)
        self.ops[eng].append((waits, fn, self.sems[lane], 16))
        self.n_ops += 1
        for b in reads:
            b.r[tok[0]] = tok[1]
        for b in writes:
            b.w = {tok[0]: tok[1]}
            b.r = {}
        return tok

    def barrier(self, exclude=()):
        snapshot = [(s, self.counts[s] * (16 if s not in ENGINES else 1))
                    for s in list(ENGINES) + LANES if s not in exclude]
        for e in ENGINES:
            if e in exclude:
                continue
            waits = []
            for s, v in snapshot:
                if v == 0:
                    continue
                key = (e, s)
                if self.last_wait.get(key, 0) >= v:
                    continue
                self.last_wait[key] = v
                waits.append((self.sems[s], v))
            if waits:
                self.ops[e].append((waits, None, None, 0))

    def flush(self, name=None, final=False):
        if not final and name not in HARD_FLUSH_TAGS:
            return
        ops = self.ops
        self.ops = {e: [] for e in ENGINES}
        live = [e for e in ENGINES if ops[e]]
        if not live:
            return

        def runner(oplist):
            def run(eng):
                for waits, fn, sem, amt in oplist:
                    for s, v in waits:
                        eng.wait_ge(s, v)
                    if fn is not None:
                        inst = fn(eng)
                        if sem is not None:
                            assert inst is not None, "op fn must return its last instruction"
                            inst.then_inc(sem, amt)
            return run

        with self.nc.Block(name, no_gpsimd_drain=True) as block:
            for e in live:
                getattr(block, {"pe": "tensor", "act": "scalar", "dve": "vector",
                                "pool": "gpsimd", "sp": "sync"}[e])(runner(ops[e]))

    def finish(self):
        self.flush(final=True)
        self.stack.close()


"""Block builders for the mk_seq_preln_d2 megakernel. Feature-major dataflow:
activations [128 ch-part, S tok] per 128-channel tile (tile c = channels c*128..c*128+127,
stored at cols [c*S, (c+1)*S)); matmul lhsT = W[in,out] tiles, rhs = actT,
out psum [out_ch, tok]."""
import contextlib
import numpy as np

D = 1024
NCT = D // 128
HID = 4096
NHT = HID // 128
LN_EPS = 1e-5



def emit_vecs_to_cols(P, vec_aps, dst, NT, ident, bank, stack, tag="", q="qs"):
    nvec = len(vec_aps)
    assert nvec <= 32
    rows = P.sbuf(f"vrows{tag}", [nvec, NT * 128], F32, stack=stack)
    for i, v in enumerate(vec_aps):
        P.dma(q, lambda e, i=i, v=v: e.dma_start(rows.t[i:i + 1, :], v.unsqueeze(0)),
              writes=[rows.buf])
    for t in range(NT):
        P.emit("pe", lambda e, t=t: e.transpose(
            bank.t[0:128, 0:nvec], rows.t[0:nvec, t * 128:(t + 1) * 128],
            ident.t[0:nvec, 0:nvec]),
            reads=[rows.buf, ident.buf], writes=[bank.buf])
        P.emit("dve", lambda e, t=t: e.tensor_copy(
            dst.t[:, t:t + (nvec - 1) * NT + 1:NT], bank.t[0:128, 0:nvec]),
            reads=[bank.buf], writes=[dst.buf])


def make_ln_aux(P, Tc, stack, banks):
    return {
        "rows": P.sbuf("lnrows", [1, 3 * Tc], F32, stack=stack),
        "castbuf": P.sbuf("lncast", [128, 2 * Tc], BF16, stack=stack),
        "sqbuf": P.sbuf("lnsq", [128, 2 * Tc], BF16, stack=stack),
        "tmpf": P.sbuf("lntmp", [128, Tc], F32, stack=stack),
        "banks": banks,
    }


def emit_ln_chunk(P, xc, lnc, Tc, aux, cmn, scale_bias=None):
    onesb, onesrow = cmn["onesb"], cmn["onesrow"]
    rows, castb, sqb = aux["rows"], aux["castbuf"], aux["sqbuf"]
    bm, bq, bmb, brb = aux["banks"]
    for c in range(NCT):
        sl = slice(c * Tc, (c + 1) * Tc)
        dsl = slice((c % 2) * Tc, (c % 2) * Tc + Tc)
        P.emit("act", lambda e, sl=sl, dsl=dsl: e.activation(castb.t[:, dsl], xc.t[:, sl], AF.Copy),
               reads=[xc.buf], writes=[castb.buf])
        P.emit("act", lambda e, sl=sl, dsl=dsl: e.activation(sqb.t[:, dsl], xc.t[:, sl], AF.Square),
               reads=[xc.buf], writes=[sqb.buf])
        P.emit("pe", lambda e, c=c, dsl=dsl: e.matmul(
            bm.t[0:1, 0:Tc], onesb.t[:], castb.t[:, dsl], start=(c == 0), stop=(c == NCT - 1)),
            reads=[castb.buf, onesb.buf], writes=[bm.buf])
        P.emit("pe", lambda e, c=c, dsl=dsl: e.matmul(
            bq.t[0:1, 0:Tc], onesb.t[:], sqb.t[:, dsl], start=(c == 0), stop=(c == NCT - 1)),
            reads=[sqb.buf, onesb.buf], writes=[bq.buf])
    mrow = rows.t[0:1, 0:Tc]
    qrow = rows.t[0:1, Tc:2 * Tc]
    vrow = rows.t[0:1, 2 * Tc:3 * Tc]
    P.emit("dve", lambda e: e.tensor_scalar_mul(mrow, bm.t[0:1, 0:Tc], 1.0 / D),
           reads=[bm.buf], writes=[rows.buf])
    P.emit("dve", lambda e: e.tensor_scalar_mul(qrow, bq.t[0:1, 0:Tc], 1.0 / D),
           reads=[bq.buf], writes=[rows.buf])
    P.emit("dve", lambda e: e.tensor_mul(vrow, mrow, mrow), reads=[rows.buf], writes=[rows.buf])
    P.emit("dve", lambda e: e.tensor_sub(vrow, qrow, vrow), reads=[rows.buf], writes=[rows.buf])
    P.emit("act", lambda e: e.activation(vrow, vrow, AF.Sqrt, bias=cmn["eps1"].t[0:1, 0:1]),
           reads=[rows.buf, cmn["eps1"].buf], writes=[rows.buf])
    P.emit("dve", lambda e: e.reciprocal(vrow, vrow), reads=[rows.buf], writes=[rows.buf])
    P.emit("pe", lambda e: e.matmul(bmb.t[0:128, 0:Tc], onesrow.t[:], mrow, start=True, stop=True),
           reads=[rows.buf, onesrow.buf], writes=[bmb.buf])
    P.emit("pe", lambda e: e.matmul(brb.t[0:128, 0:Tc], onesrow.t[:], vrow, start=True, stop=True),
           reads=[rows.buf, onesrow.buf], writes=[brb.buf])
    tmpf = aux["tmpf"]
    for c in range(NCT):
        sl = slice(c * Tc, (c + 1) * Tc)
        P.emit("dve", lambda e, sl=sl: e.tensor_sub(tmpf.t[:, 0:Tc], xc.t[:, sl], bmb.t[0:128, 0:Tc]),
               reads=[xc.buf, bmb.buf], writes=[tmpf.buf])
        P.emit("dve", lambda e, sl=sl: e.tensor_mul(lnc.t[:, sl], tmpf.t[:, 0:Tc], brb.t[0:128, 0:Tc]),
               reads=[tmpf.buf, brb.buf], writes=[lnc.buf])


def emit_weight_cast(P, src_ap, dst_tb, O, I, wt, qload="qs", ecast="act", qstore="qa", cw=None):
    cw = cw or I
    wf, wb = wt["f"], wt["b"]
    fB = wt.setdefault("fb", [Buf("wcf0"), Buf("wcf1")])
    bB = wt.setdefault("bb", [Buf("wcb0"), Buf("wcb1")])
    k = 0
    for r in range(O // 128):
        for cc in range(I // cw):
            h = (k % 2) * cw
            fb, bb = fB[k % 2], bB[k % 2]
            k += 1
            P.dma(qload, lambda e, r=r, cc=cc, h=h: e.dma_start(
                wf.t[:, h:h + cw], src_ap[r * 128:(r + 1) * 128, cc * cw:(cc + 1) * cw]),
                writes=[fb])
            if ecast in ("pool", "dve"):
                P.emit(ecast, lambda e, h=h: e.tensor_copy(wb.t[:, h:h + cw], wf.t[:, h:h + cw]),
                       reads=[fb], writes=[bb])
            else:
                P.emit(ecast, lambda e, h=h: e.activation(wb.t[:, h:h + cw], wf.t[:, h:h + cw], AF.Copy),
                       reads=[fb], writes=[bb])
            P.dma(qstore, lambda e, r=r, cc=cc, h=h: e.dma_start(
                dst_tb.t[r * 128:(r + 1) * 128, cc * cw:(cc + 1) * cw], wb.t[:, h:h + cw]),
                reads=[bb], writes=[dst_tb.buf])


def emit_weight_transpose(P, src_tb, dst_tb, O, I, tt, qload="qs", qstore="qs", ow=2048):
    tB = tt.setdefault("bufs", [Buf("wtt0"), Buf("wtt1")])
    ttt = tt["t"]
    ow = min(ow, O)
    nslot = tt.get("nslot", 2)
    k = 0
    for ci in range(I // 128):
        for oc in range(O // ow):
            h = (k % nslot) * ow
            tb = tB[k % nslot]
            k += 1
            P.dma(qload, lambda e, ci=ci, oc=oc, h=h: e.dma_start(
                ttt.t[:, h:h + ow],
                src_tb.t[oc * ow:(oc + 1) * ow, ci * 128:(ci + 1) * 128], transpose=True),
                reads=[src_tb.buf], writes=[tb])
            P.dma(qstore, lambda e, ci=ci, oc=oc, h=h: e.dma_start(
                dst_tb.t[ci * 128:(ci + 1) * 128, oc * ow:(oc + 1) * ow], ttt.t[:, h:h + ow]),
                reads=[tb], writes=[dst_tb.buf])


def make_common(P, stack, ident_ap, q="qs"):
    cmn = {}
    cmn["ident"] = P.sbuf("identC", [128, 128], F32, stack=stack)
    P.dma(q, lambda e: e.dma_start(cmn["ident"].t[:], ident_ap), writes=[cmn["ident"].buf])
    cmn["onesb"] = P.sbuf("onesbC", [128, 1], BF16, stack=stack)
    cmn["onesrow"] = P.sbuf("onesrowC", [1, 128], F32, stack=stack)
    P.emit("dve", lambda e: e.memset(cmn["onesb"].t[:], 1.0), writes=[cmn["onesb"].buf])
    P.emit("dve", lambda e: e.memset(cmn["onesrow"].t[:], 1.0), writes=[cmn["onesrow"].buf])
    cmn["eps1"] = P.sbuf("eps1C", [1, 1], F32, stack=stack)
    P.emit("dve", lambda e: e.memset(cmn["eps1"].t[:], LN_EPS), writes=[cmn["eps1"].buf])
    return cmn



def make_prep_bufs(P, cw=1024, stack=None):
    return {"f": P.sbuf("wprep_f", [128, 2 * cw], F32, stack=stack),
            "b": P.sbuf("wprep_b", [128, 2 * cw], BF16, stack=stack)}


def emit_l1ffn_cast(P, w, wt, qload="qp", ecast="pool", qstore="qp", cw=1024):
    if wt is None:
        wt = make_prep_bufs(P, cw)
    bf = {}
    for nm, O, I in (("w1", HID, D), ("w2", HID, D), ("h1", HID, HID),
                     ("h2", HID, HID), ("w3", D, HID)):
        bf[nm] = P.dram(nm + "bf", [O, I], BF16)
        emit_weight_cast(P, w[nm + "_w"], bf[nm], O, I, wt, qload, ecast, qstore, cw)
    return bf


def emit_l1ffn_transpose(P, bf, tt, qload="qs", qstore="qs", ow=2048):
    if tt is None:
        tt = {"t": P.sbuf("wprep_t", [128, 2 * ow], BF16)}
    pre = {}
    for nm, O, I in (("w1", HID, D), ("w2", HID, D), ("h1", HID, HID),
                     ("h2", HID, HID), ("w3", D, HID)):
        pre[nm + "T"] = P.dram(nm + "T", [I, O], BF16)
        emit_weight_transpose(P, bf[nm], pre[nm + "T"], O, I, tt, qload, qstore, ow)
    return pre


def build_l1ffn(P, S, xin_ap, w, out_kind, out_ap, outbuf, prep=True):
    Tc = min(512, S)
    ntc = S // Tc
    WQ = 512

    if prep is True:
        pre = emit_l1ffn_cast(P, w, None)
        pre = emit_l1ffn_transpose(P, pre, None)
    else:
        pre = prep
    w1T, w2T, h1T, h2T, w3T = (pre[k] for k in ("w1T", "w2T", "h1T", "h2T", "w3T"))

    with contextlib.ExitStack() as ph:
        cmn = make_common(P, ph, w["ident"])
        scol = P.sbuf("scol", [128, 4 * NHT], F32, stack=ph)
        s3col = P.sbuf("s3col", [128, NCT], F32, stack=ph)
        with contextlib.ExitStack() as vph:
            emit_vecs_to_cols(P, [w["w1_s"], w["w2_s"], w["h1_s"], w["h2_s"]], scol, NHT,
                              cmn["ident"], P.pb[7], vph, tag="s4")
            emit_vecs_to_cols(P, [w["w3_s"]], s3col, NCT, cmn["ident"], P.pb[7], vph, tag="s3")
            P.barrier_partial()
            P.flush("l1ffn_scales")

        xc = P.sbuf("xc", [128, NCT * Tc], F32, stack=ph)
        lnc = P.sbuf("lnc", [128, NCT * Tc], BF16, stack=ph)
        av = P.sbuf("av", [128, NHT * Tc], BF16, stack=ph)
        bv = P.sbuf("bv", [128, NHT * Tc], BF16, stack=ph)
        WQA = 256
        wa = P.sbuf("wa", [128, 2 * NCT * WQA], BF16, stack=ph)
        wb = P.sbuf("wb", [128, 2 * NCT * WQA], BF16, stack=ph)
        wh = P.sbuf("wh", [128, 2 * NHT * WQ], BF16, stack=ph)
        t1f = P.sbuf("t1f", [128, Tc], F32, stack=ph)
        xo = P.sbuf("xo", [128, Tc], F32, stack=ph)
        ytile = P.sbuf("ytile", [128, 2 * 128], F32, stack=ph)
        aux = make_ln_aux(P, Tc, ph, (P.pb[0], P.pb[1], P.pb[2], P.pb[3]))
        waB = [Buf("waA"), Buf("waB")]
        wbB = [Buf("wbA"), Buf("wbB")]
        whB = [Buf("whA"), Buf("whB")]

        def win_load(q, srcT, dst_t, half_off, nin_t, wnd, bufW, wq=WQ):
            src = srcT.t.rearrange("(c p) o -> p c o", p=128)[:, :, wnd * wq:(wnd + 1) * wq]
            dst = dst_t[:, half_off:half_off + nin_t * wq].rearrange("p (c o) -> p c o", c=nin_t)
            P.dma(q, lambda e, s=src, d=dst: e.dma_start(d, s), reads=[srcT.buf], writes=[bufW])

        def hid_stage(inp, srcT, nin_t, nout_t, epilogue):
            for wnd in range(nout_t * 128 // WQ):
                half = (wnd % 2) * NHT * WQ
                bufW = whB[wnd % 2]
                win_load("qs", srcT, wh.t, half, nin_t, wnd, bufW)
                for ot in range(WQ // 128):
                    og = wnd * (WQ // 128) + ot
                    bk = P.pb[4 + (og % 2)]
                    def mm(e, half=half, ot=ot, bk=bk, nin_t=nin_t, inp=inp):
                        i = None
                        for ci in range(nin_t):
                            i = e.matmul(bk.t[:, 0:Tc],
                                         wh.t[:, half + ci * WQ + ot * 128: half + ci * WQ + ot * 128 + 128],
                                         inp.t[:, ci * Tc:(ci + 1) * Tc],
                                         start=(ci == 0), stop=(ci == nin_t - 1))
                        return i
                    P.emit("pe", mm, reads=[bufW, inp.buf], writes=[bk.buf])
                    epilogue(og, bk)

        for tci in range(ntc):
            t0 = tci * Tc
            for c in range(NCT):
                P.dma("qa", lambda e, c=c, t0=t0: e.dma_start(
                    xc.t[:, c * Tc:(c + 1) * Tc], xin_ap[:, c * S + t0: c * S + t0 + Tc]),
                    writes=[xc.buf])
            emit_ln_chunk(P, xc, lnc, Tc, aux, cmn)

            for wnd in range(HID // WQA):
                half = (wnd % 2) * NCT * WQA
                bufA, bufB = waB[wnd % 2], wbB[wnd % 2]
                win_load("qs", w1T, wa.t, half, NCT, wnd, bufA, WQA)
                win_load("qa", w2T, wb.t, half, NCT, wnd, bufB, WQA)
                for ho in range(WQA // 128):
                    hog = wnd * (WQA // 128) + ho
                    bA, bB = P.pb[4], P.pb[5]
                    def mmw(e, wtile, half, ho, bk):
                        i = None
                        for ci in range(NCT):
                            i = e.matmul(bk.t[:, 0:Tc],
                                         wtile[:, half + ci * WQA + ho * 128: half + ci * WQA + ho * 128 + 128],
                                         lnc.t[:, ci * Tc:(ci + 1) * Tc],
                                         start=(ci == 0), stop=(ci == NCT - 1))
                        return i
                    P.emit("pe", lambda e, half=half, ho=ho: mmw(e, wa.t, half, ho, bA),
                           reads=[bufA, lnc.buf], writes=[bA.buf])
                    P.emit("act", lambda e, hog=hog: e.activation(
                        t1f.t[:, 0:Tc], bA.t[:, 0:Tc], AF.Silu,
                        scale=scol.t[:, 0 * NHT + hog:0 * NHT + hog + 1]),
                        reads=[bA.buf, scol.buf], writes=[t1f.buf])
                    P.emit("pe", lambda e, half=half, ho=ho: mmw(e, wb.t, half, ho, bB),
                           reads=[bufB, lnc.buf], writes=[bB.buf])
                    P.emit("dve", lambda e, hog=hog: e.scalar_tensor_tensor(
                        av.t[:, hog * Tc:(hog + 1) * Tc], bB.t[:, 0:Tc],
                        scol.t[:, 1 * NHT + hog:1 * NHT + hog + 1], t1f.t[:, 0:Tc],
                        ALU.mult, ALU.mult),
                        reads=[bB.buf, scol.buf, t1f.buf], writes=[av.buf])

            def ep_gelu(dst, sidx):
                def ep(og, bk):
                    P.emit("act", lambda e, og=og, bk=bk: e.activation(
                        dst.t[:, og * Tc:(og + 1) * Tc], bk.t[:, 0:Tc], AF.Gelu,
                        scale=scol.t[:, sidx + og:sidx + og + 1]),
                        reads=[bk.buf, scol.buf], writes=[dst.buf])
                return ep

            hid_stage(av, h1T, NHT, NHT, ep_gelu(bv, 2 * NHT))
            hid_stage(bv, h2T, NHT, NHT, ep_gelu(av, 3 * NHT))

            def ep_out(og, bk):
                P.emit("dve", lambda e, og=og, bk=bk: e.scalar_tensor_tensor(
                    xo.t[:, 0:Tc], bk.t[:, 0:Tc], s3col.t[:, og:og + 1],
                    xc.t[:, og * Tc:(og + 1) * Tc], ALU.mult, ALU.add),
                    reads=[bk.buf, s3col.buf, xc.buf], writes=[xo.buf])
                if out_kind == "feat":
                    P.dma("qa", lambda e, og=og, t0=t0: e.dma_start(
                        out_ap[:, og * S + t0: og * S + t0 + Tc], xo.t[:, 0:Tc]),
                        reads=[xo.buf], writes=[outbuf])
                else:
                    for tb in range(Tc // 128):
                        yh = (tb % 2) * 128
                        P.emit("pe", lambda e, tb=tb: e.transpose(
                            P.pb[6].t[0:128, 0:128], xo.t[:, tb * 128:(tb + 1) * 128],
                            cmn["ident"].t[:]),
                            reads=[xo.buf, cmn["ident"].buf], writes=[P.pb[6].buf])
                        P.emit("dve", lambda e, yh=yh: e.tensor_copy(
                            ytile.t[:, yh:yh + 128], P.pb[6].t[0:128, 0:128]),
                            reads=[P.pb[6].buf], writes=[ytile.buf])
                        P.dma("qa", lambda e, og=og, tb=tb, yh=yh, t0=t0: e.dma_start(
                            out_ap[t0 + tb * 128: t0 + (tb + 1) * 128, og * 128:(og + 1) * 128],
                            ytile.t[:, yh:yh + 128]),
                            reads=[ytile.buf], writes=[outbuf])

            hid_stage(av, w3T, NHT, NCT, ep_out)
        P.barrier()
        P.flush("l1ffn_main")



def emit_l0ffn_prep(P, w, wt, qload="qp", ecast="pool", qstore="qp", cw=1024):
    NE = 8
    W1b = P.dram("W1b", [NE, D, HID], BF16)
    W2b = P.dram("W2b", [NE, HID, D], BF16)
    W1e = [TB(W1b.t[e], Buf(f"W1b{e}")) for e in range(NE)]
    W2e = [TB(W2b.t[e], Buf(f"W2b{e}")) for e in range(NE)]
    if wt is None:
        wt = make_prep_bufs(P, cw)
    for e in range(NE):
        emit_weight_cast(P, w["W1"][e], W1e[e], D, HID, wt, qload, ecast, qstore, cw)
        emit_weight_cast(P, w["W2"][e], W2e[e], HID, D, wt, qload, ecast, qstore, cw)
    return (W1e, W2e)


def build_l0ffn(P, S, xin_ap, w, out_ap, outbuf, prep=True):
    Tc = min(512, S)
    ntc = S // Tc
    NE = 8
    if prep is True:
        pre = emit_l0ffn_prep(P, w, None)
    else:
        pre = prep
    W1e, W2e = pre

    with contextlib.ExitStack() as ph:
        cmn = make_common(P, ph, w["ident"])
        b1col = P.sbuf("b1col", [128, NE * NHT], F32, stack=ph)
        with contextlib.ExitStack() as vph:
            emit_vecs_to_cols(P, [w["b1"][e] for e in range(NE)], b1col, NHT,
                              cmn["ident"], P.pb[7], vph, tag="b1")
            P.barrier_partial()
            P.flush("l0ffn_b1")
        b2sb = P.sbuf("b2sb", [NE, D], F32, stack=ph)
        P.dma("qs", lambda e: e.dma_start(b2sb.t[:], w["b2"]), writes=[b2sb.buf])
        bgcol = P.sbuf("bgcol", [NE, 1], F32, stack=ph)
        P.dma("qs", lambda e: e.dma_start(bgcol.t[:], w["bg"].unsqueeze(1)), writes=[bgcol.buf])
        wgsb = P.sbuf("wgsb", [128, NCT * NE], BF16, stack=ph)
        wgf = P.sbuf("wgf", [128, NCT * NE], F32, stack=ph)
        for c in range(NCT):
            P.dma("qs", lambda e, c=c: e.dma_start(
                wgf.t[:, c * NE:(c + 1) * NE], w["Wg"][c * 128:(c + 1) * 128, :]),
                writes=[wgf.buf])
        P.emit("act", lambda e: e.activation(wgsb.t[:], wgf.t[:], AF.Copy),
               reads=[wgf.buf], writes=[wgsb.buf])
        sel8 = P.sbuf("sel8sb", [NE, NE * 128], F32, stack=ph)
        P.dma("qs", lambda e: e.dma_start(sel8.t[:], w["sel8"]), writes=[sel8.buf])

        xc = P.sbuf("xc0", [128, NCT * Tc], F32, stack=ph)
        lnc = P.sbuf("lnc0", [128, NCT * Tc], BF16, stack=ph)
        hbuf = P.sbuf("hbuf", [128, NHT * Tc], BF16, stack=ph)
        acc = P.sbuf("acc0", [128, NCT * Tc], F32, stack=ph)
        w1win = P.sbuf("w1win", [128, 2 * NCT * 512], BF16, stack=ph)
        w2win = P.sbuf("w2win", [128, 2 * NHT * 256], BF16, stack=ph)
        lgsb = P.sbuf("lgsb", [NE, Tc], F32, stack=ph)
        ohw = P.sbuf("ohw", [128, 33], F32, stack=ph)
        elbuf = P.sbuf("elbuf", [128, NE], F32, stack=ph)
        rmaxn = P.sbuf("rmaxn", [128, 1], F32, stack=ph)
        maskT = P.sbuf("maskT", [NE, Tc], F32, stack=ph)
        wrow = P.sbuf("wrow", [1, Tc], F32, stack=ph)
        t1f = P.sbuf("t1f0", [128, 2 * Tc], F32, stack=ph)
        tmpo = P.sbuf("tmpo", [128, 2 * Tc], F32, stack=ph)
        aux = make_ln_aux(P, Tc, ph, (P.pb[0], P.pb[1], P.pb[2], P.pb[3]))
        P.emit("dve", lambda e: e.memset(ohw.t[:], 0.0), writes=[ohw.buf])
        w1B = [Buf("w1A"), Buf("w1B")]
        w2B = [Buf("w2A"), Buf("w2B")]
        t1B = [Buf("t1A"), Buf("t1B")]
        tmB = [Buf("tmA"), Buf("tmB")]

        for tci in range(ntc):
            t0 = tci * Tc
            for c in range(NCT):
                P.dma("qs", lambda e, c=c, t0=t0: e.dma_start(
                    xc.t[:, c * Tc:(c + 1) * Tc], xin_ap[:, c * S + t0: c * S + t0 + Tc]),
                    writes=[xc.buf])
            emit_ln_chunk(P, xc, lnc, Tc, aux, cmn)

            def gmm(e):
                i = None
                for c in range(NCT):
                    i = e.matmul(P.pb[4].t[0:NE, 0:Tc], wgsb.t[:, c * NE:(c + 1) * NE],
                                 lnc.t[:, c * Tc:(c + 1) * Tc], start=(c == 0), stop=(c == NCT - 1))
                return i
            P.emit("pe", gmm, reads=[wgsb.buf, lnc.buf], writes=[P.pb[4].buf])
            P.emit("dve", lambda e: e.tensor_scalar_add(lgsb.t[:], P.pb[4].t[0:NE, 0:Tc],
                                                        bgcol.t[:]),
                   reads=[P.pb[4].buf, bgcol.buf], writes=[lgsb.buf])
            for tt in range(Tc // 128):
                P.emit("pe", lambda e, tt=tt: e.transpose(
                    P.pb[5].t[0:128, 0:NE], lgsb.t[:, tt * 128:(tt + 1) * 128],
                    cmn["ident"].t[0:NE, 0:NE]),
                    reads=[lgsb.buf, cmn["ident"].buf], writes=[P.pb[5].buf])
                P.emit("dve", lambda e: e.tensor_reduce(
                    rmaxn.t[:], P.pb[5].t[0:128, 0:NE], AX.X, ALU.max, negate=True),
                    reads=[P.pb[5].buf], writes=[rmaxn.buf])
                P.emit("act", lambda e: e.activation(elbuf.t[:], P.pb[5].t[0:128, 0:NE],
                                                     AF.Exp, bias=rmaxn.t[:]),
                       reads=[P.pb[5].buf, rmaxn.buf], writes=[elbuf.buf])
                P.emit("dve", lambda e: e.tensor_scalar(
                    ohw.t[:, 0:NE], P.pb[5].t[0:128, 0:NE], rmaxn.t[:], 0.0,
                    ALU.add, ALU.is_ge),
                    reads=[P.pb[5].buf, rmaxn.buf], writes=[ohw.buf])
                P.emit("dve", lambda e: e.tensor_reduce(
                    ohw.t[:, 32:33], elbuf.t[:], AX.X, ALU.add),
                    reads=[elbuf.buf], writes=[ohw.buf])
                P.emit("dve", lambda e: e.reciprocal(ohw.t[:, 32:33], ohw.t[:, 32:33]),
                       reads=[ohw.buf], writes=[ohw.buf])
                P.emit("pe", lambda e: e.transpose(
                    P.pb[6].t[0:33, 0:128], ohw.t[:], cmn["ident"].t[:]),
                    reads=[ohw.buf, cmn["ident"].buf], writes=[P.pb[6].buf])
                P.emit("dve", lambda e, tt=tt: e.tensor_copy(
                    maskT.t[:, tt * 128:(tt + 1) * 128], P.pb[6].t[0:NE, 0:128]),
                    reads=[P.pb[6].buf], writes=[maskT.buf])
                P.emit("dve", lambda e, tt=tt: e.tensor_copy(
                    wrow.t[0:1, tt * 128:(tt + 1) * 128], P.pb[6].t[32:33, 0:128]),
                    reads=[P.pb[6].buf], writes=[wrow.buf])

            for e_i in range(NE):
                P.emit("pe", lambda e, e_i=e_i: e.matmul(
                    P.pb[2].t[:, 0:Tc], sel8.t[:, e_i * 128:(e_i + 1) * 128],
                    maskT.t[0:NE, 0:Tc], start=True, stop=True),
                    reads=[sel8.buf, maskT.buf], writes=[P.pb[2].buf])
                for hw in range(HID // 512):
                    half = (hw % 2) * NCT * 512
                    bw = w1B[hw % 2]
                    src = W1e[e_i].t.rearrange("(c p) o -> p c o", p=128)[:, :, hw * 512:(hw + 1) * 512]
                    dst = w1win.t[:, half:half + NCT * 512].rearrange("p (c o) -> p c o", c=NCT)
                    P.dma("qs", lambda e, s=src, d=dst: e.dma_start(d, s),
                          reads=[W1e[e_i].buf], writes=[bw])
                    for ho in range(4):
                        hog = hw * 4 + ho
                        bk = P.pb[4 + (ho % 2)]
                        def mm1(e, half=half, ho=ho, bk=bk):
                            i = None
                            for ci in range(NCT):
                                i = e.matmul(bk.t[:, 0:Tc],
                                             w1win.t[:, half + ci * 512 + ho * 128: half + ci * 512 + ho * 128 + 128],
                                             lnc.t[:, ci * Tc:(ci + 1) * Tc],
                                             start=(ci == 0), stop=(ci == NCT - 1))
                            return i
                        P.emit("pe", mm1, reads=[bw, lnc.buf], writes=[bk.buf])
                        tof = (hog % 2) * Tc
                        tb1 = t1B[hog % 2]
                        P.emit("act", lambda e, e_i=e_i, hog=hog, bk=bk, tof=tof: e.activation(
                            t1f.t[:, tof:tof + Tc], bk.t[:, 0:Tc], AF.Silu,
                            bias=b1col.t[:, e_i * NHT + hog:e_i * NHT + hog + 1]),
                            reads=[bk.buf, b1col.buf], writes=[tb1])
                        P.emit("dve", lambda e, hog=hog, tof=tof: e.tensor_mul(
                            hbuf.t[:, hog * Tc:(hog + 1) * Tc], t1f.t[:, tof:tof + Tc], P.pb[2].t[:, 0:Tc]),
                            reads=[tb1, P.pb[2].buf], writes=[hbuf.buf])
                for ow in range(D // 256):
                    w2h = (ow % 2) * NHT * 256
                    w2b_ = w2B[ow % 2]
                    src = W2e[e_i].t.rearrange("(c p) o -> p c o", p=128)[:, :, ow * 256:(ow + 1) * 256]
                    dst = w2win.t[:, w2h:w2h + NHT * 256].rearrange("p (c o) -> p c o", c=NHT)
                    P.dma("qa", lambda e, s=src, d=dst: e.dma_start(d, s),
                          reads=[W2e[e_i].buf], writes=[w2b_])
                    for oc in range(2):
                        og = ow * 2 + oc
                        bk = P.pb[4 + (oc % 2)]
                        def mm2(e, oc=oc, bk=bk, w2h=w2h):
                            i = None
                            for ci in range(NHT):
                                i = e.matmul(bk.t[:, 0:Tc],
                                             w2win.t[:, w2h + ci * 256 + oc * 128: w2h + ci * 256 + oc * 128 + 128],
                                             hbuf.t[:, ci * Tc:(ci + 1) * Tc],
                                             start=(ci == 0), stop=(ci == NHT - 1))
                            return i
                        P.emit("pe", mm2, reads=[w2b_, hbuf.buf], writes=[bk.buf])
                        if e_i == 0:
                            P.emit("dve", lambda e, og=og, bk=bk: e.tensor_copy(
                                acc.t[:, og * Tc:(og + 1) * Tc], bk.t[:, 0:Tc]),
                                reads=[bk.buf], writes=[acc.buf])
                        else:
                            P.emit("dve", lambda e, og=og, bk=bk: e.tensor_add(
                                acc.t[:, og * Tc:(og + 1) * Tc], acc.t[:, og * Tc:(og + 1) * Tc],
                                bk.t[:, 0:Tc]),
                                reads=[bk.buf, acc.buf], writes=[acc.buf])

            P.emit("pe", lambda e: e.matmul(P.pb[3].t[:, 0:Tc], cmn["onesrow"].t[:],
                                            wrow.t[0:1, 0:Tc], start=True, stop=True),
                   reads=[cmn["onesrow"].buf, wrow.buf], writes=[P.pb[3].buf])
            for og in range(NCT):
                bk = P.pb[4 + (og % 2)]
                tof = (og % 2) * Tc
                tb = tmB[og % 2]
                P.emit("pe", lambda e, og=og, bk=bk: e.matmul(
                    bk.t[:, 0:Tc], b2sb.t[:, og * 128:(og + 1) * 128], maskT.t[0:NE, 0:Tc],
                    start=True, stop=True),
                    reads=[b2sb.buf, maskT.buf], writes=[bk.buf])
                P.emit("dve", lambda e, og=og, bk=bk, tof=tof: e.tensor_add(
                    tmpo.t[:, tof:tof + Tc], acc.t[:, og * Tc:(og + 1) * Tc], bk.t[:, 0:Tc]),
                    reads=[acc.buf, bk.buf], writes=[tb])
                P.emit("dve", lambda e, tof=tof: e.tensor_mul(tmpo.t[:, tof:tof + Tc], tmpo.t[:, tof:tof + Tc],
                                                     P.pb[3].t[:, 0:Tc]),
                       reads=[tb, P.pb[3].buf], writes=[tb])
                P.emit("dve", lambda e, og=og, tof=tof: e.tensor_add(
                    tmpo.t[:, tof:tof + Tc], tmpo.t[:, tof:tof + Tc], xc.t[:, og * Tc:(og + 1) * Tc]),
                    reads=[tb, xc.buf], writes=[tb])
                P.dma("qa", lambda e, og=og, t0=t0, tof=tof: e.dma_start(
                    out_ap[:, og * S + t0: og * S + t0 + Tc], tmpo.t[:, tof:tof + Tc]),
                    reads=[tb], writes=[outbuf])
        P.barrier()
        P.flush("l0ffn_main")



def emit_pe_weight_load_T(P, src_ap, O, I, dst, cmn, wf, stack, tag="", qload="qs"):
    nO, nI = O // 128, I // 128
    for r in range(nO):
        P.dma(qload, lambda e, r=r: e.dma_start(wf.t[:, (r % 2) * I:(r % 2) * I + I],
                                                src_ap[r * 128:(r + 1) * 128, :]),
              writes=[wf.buf])
        for ci in range(nI):
            P.emit("pe", lambda e, r=r, ci=ci: e.transpose(
                P.pb[7].t[0:128, 0:128],
                wf.t[:, (r % 2) * I + ci * 128:(r % 2) * I + ci * 128 + 128],
                cmn["ident"].t[:]),
                reads=[wf.buf, cmn["ident"].buf], writes=[P.pb[7].buf])
            P.emit("act", lambda e, r=r, ci=ci: e.activation(
                dst.t[:, ci * O + r * 128: ci * O + r * 128 + 128],
                P.pb[7].t[0:128, 0:128], AF.Copy),
                reads=[P.pb[7].buf], writes=[dst.buf])


def build_l1attn(P, S, xin_ap, w, out_ap, outbuf):
    H, DH = 16, 64
    DQL = 512
    NQT = DQL // 128
    lnp_d = P.dram("lnp", [128, NCT * S], F32)

    with contextlib.ExitStack() as ph:
        cmn = make_common(P, ph, w["ident"], q="qa")
        attnT = P.sbuf("attnT", [128, NCT * S], BF16, stack=ph)

        with contextlib.ExitStack() as bc:
            cqT = P.sbuf("cqT", [128, NQT * S], BF16, stack=bc)
            ckvT = P.sbuf("ckvT", [128, NQT * S], BF16, stack=bc)

            with contextlib.ExitStack() as pa:
                wdqT = P.sbuf("wdqT", [128, NCT * DQL], BF16, stack=pa)
                wdkT = P.sbuf("wdkT", [128, NCT * DQL], BF16, stack=pa)
                with contextlib.ExitStack() as pwa:
                    wfA = P.sbuf("wfA", [128, 2 * 1024], F32, stack=pwa)
                    emit_pe_weight_load_T(P, w["w_down_q"], DQL, D, wdqT, cmn, wfA, pwa, qload="qa")
                    emit_pe_weight_load_T(P, w["w_down_kv"], DQL, D, wdkT, cmn, wfA, pwa, qload="qa")
                    P.barrier_no_qs()
                    P.flush("l1a_wd")
                sbcol = P.sbuf("sbcol", [128, 2 * NCT], F32, stack=pa)
                with contextlib.ExitStack() as vph:
                    emit_vecs_to_cols(P, [w["ln_scale"], w["ln_bias"]], sbcol, NCT,
                                      cmn["ident"], P.pb[7], vph, tag="sb", q="qa")
                    P.barrier_no_qs()
                    P.flush("l1a_sb")
                Tc = min(512, S)
                ntc = S // Tc
                xc = P.sbuf("xcA", [128, NCT * Tc], F32, stack=pa)
                lnc = P.sbuf("lncA", [128, NCT * Tc], BF16, stack=pa)
                lnpc = P.sbuf("lnpcA", [128, Tc], F32, stack=pa)
                aux = make_ln_aux(P, Tc, pa, (P.pb[0], P.pb[1], P.pb[2], P.pb[3]))
                rows2, cb2, sq2 = aux["rows"], aux["castbuf"], aux["sqbuf"]
                lnpf = P.sbuf("lnpf", [128, NCT * Tc], F32, stack=pa)

                for tci in range(ntc):
                    t0 = tci * Tc
                    for c in range(NCT):
                        P.dma("qa", lambda e, c=c, t0=t0: e.dma_start(
                            xc.t[:, c * Tc:(c + 1) * Tc], xin_ap[:, c * S + t0: c * S + t0 + Tc]),
                            writes=[xc.buf])
                    onesb, onesrow = cmn["onesb"], cmn["onesrow"]
                    rows, castb, sqb = aux["rows"], aux["castbuf"], aux["sqbuf"]
                    bm, bq, bmb, brb = aux["banks"]
                    for c in range(NCT):
                        sl = slice(c * Tc, (c + 1) * Tc)
                        dsl = slice((c % 2) * Tc, (c % 2) * Tc + Tc)
                        P.emit("act", lambda e, sl=sl, dsl=dsl: e.activation(castb.t[:, dsl], xc.t[:, sl], AF.Copy),
                               reads=[xc.buf], writes=[castb.buf])
                        P.emit("act", lambda e, sl=sl, dsl=dsl: e.activation(sqb.t[:, dsl], xc.t[:, sl], AF.Square),
                               reads=[xc.buf], writes=[sqb.buf])
                        P.emit("pe", lambda e, c=c, dsl=dsl: e.matmul(
                            bm.t[0:1, 0:Tc], onesb.t[:], castb.t[:, dsl], start=(c == 0), stop=(c == NCT - 1)),
                            reads=[castb.buf, onesb.buf], writes=[bm.buf])
                        P.emit("pe", lambda e, c=c, dsl=dsl: e.matmul(
                            bq.t[0:1, 0:Tc], onesb.t[:], sqb.t[:, dsl], start=(c == 0), stop=(c == NCT - 1)),
                            reads=[sqb.buf, onesb.buf], writes=[bq.buf])
                    mrow = rows.t[0:1, 0:Tc]; qrow = rows.t[0:1, Tc:2 * Tc]; vrow = rows.t[0:1, 2 * Tc:3 * Tc]
                    P.emit("dve", lambda e: e.tensor_scalar_mul(mrow, bm.t[0:1, 0:Tc], 1.0 / D),
                           reads=[bm.buf], writes=[rows.buf])
                    P.emit("dve", lambda e: e.tensor_scalar_mul(qrow, bq.t[0:1, 0:Tc], 1.0 / D),
                           reads=[bq.buf], writes=[rows.buf])
                    P.emit("dve", lambda e: e.tensor_mul(vrow, mrow, mrow), reads=[rows.buf], writes=[rows.buf])
                    P.emit("dve", lambda e: e.tensor_sub(vrow, qrow, vrow), reads=[rows.buf], writes=[rows.buf])
                    P.emit("act", lambda e: e.activation(vrow, vrow, AF.Sqrt, bias=cmn["eps1"].t[0:1, 0:1]),
                           reads=[rows.buf, cmn["eps1"].buf], writes=[rows.buf])
                    P.emit("dve", lambda e: e.reciprocal(vrow, vrow), reads=[rows.buf], writes=[rows.buf])
                    P.emit("pe", lambda e: e.matmul(bmb.t[0:128, 0:Tc], onesrow.t[:], mrow, start=True, stop=True),
                           reads=[rows.buf, onesrow.buf], writes=[bmb.buf])
                    P.emit("pe", lambda e: e.matmul(brb.t[0:128, 0:Tc], onesrow.t[:], vrow, start=True, stop=True),
                           reads=[rows.buf, onesrow.buf], writes=[brb.buf])
                    for c in range(NCT):
                        sl = slice(c * Tc, (c + 1) * Tc)
                        P.emit("dve", lambda e, sl=sl: e.tensor_sub(lnpc.t[:, 0:Tc], xc.t[:, sl], bmb.t[0:128, 0:Tc]),
                               reads=[xc.buf, bmb.buf], writes=[lnpc.buf])
                        P.emit("dve", lambda e, sl=sl: e.tensor_mul(lnpf.t[:, sl], lnpc.t[:, 0:Tc], brb.t[0:128, 0:Tc]),
                               reads=[lnpc.buf, brb.buf], writes=[lnpf.buf])
                        P.dma("qa", lambda e, c=c, t0=t0, sl=sl: e.dma_start(
                            lnp_d.t[:, c * S + t0: c * S + t0 + Tc], lnpf.t[:, sl]),
                            reads=[lnpf.buf], writes=[lnp_d.buf])
                    for c in range(NCT):
                        sl = slice(c * Tc, (c + 1) * Tc)
                        dsl = slice((c % 2) * Tc, (c % 2) * Tc + Tc)
                        P.emit("act", lambda e, sl=sl, dsl=dsl: e.activation(cb2.t[:, dsl], lnpf.t[:, sl], AF.Copy),
                               reads=[lnpf.buf], writes=[cb2.buf])
                        P.emit("act", lambda e, sl=sl, dsl=dsl: e.activation(sq2.t[:, dsl], lnpf.t[:, sl], AF.Square),
                               reads=[lnpf.buf], writes=[sq2.buf])
                        P.emit("pe", lambda e, c=c, dsl=dsl: e.matmul(
                            bm.t[0:1, 0:Tc], onesb.t[:], cb2.t[:, dsl], start=(c == 0), stop=(c == NCT - 1)),
                            reads=[cb2.buf, onesb.buf], writes=[bm.buf])
                        P.emit("pe", lambda e, c=c, dsl=dsl: e.matmul(
                            bq.t[0:1, 0:Tc], onesb.t[:], sq2.t[:, dsl], start=(c == 0), stop=(c == NCT - 1)),
                            reads=[sq2.buf, onesb.buf], writes=[bq.buf])
                    mrow2 = rows2.t[0:1, 0:Tc]; qrow2 = rows2.t[0:1, Tc:2 * Tc]; vrow2 = rows2.t[0:1, 2 * Tc:3 * Tc]
                    P.emit("dve", lambda e: e.tensor_scalar_mul(mrow2, bm.t[0:1, 0:Tc], 1.0 / D),
                           reads=[bm.buf], writes=[rows2.buf])
                    P.emit("dve", lambda e: e.tensor_scalar_mul(qrow2, bq.t[0:1, 0:Tc], 1.0 / D),
                           reads=[bq.buf], writes=[rows2.buf])
                    P.emit("dve", lambda e: e.tensor_mul(vrow2, mrow2, mrow2), reads=[rows2.buf], writes=[rows2.buf])
                    P.emit("dve", lambda e: e.tensor_sub(vrow2, qrow2, vrow2), reads=[rows2.buf], writes=[rows2.buf])
                    P.emit("act", lambda e: e.activation(vrow2, vrow2, AF.Sqrt, bias=cmn["eps1"].t[0:1, 0:1]),
                           reads=[rows2.buf, cmn["eps1"].buf], writes=[rows2.buf])
                    P.emit("dve", lambda e: e.reciprocal(vrow2, vrow2), reads=[rows2.buf], writes=[rows2.buf])
                    P.emit("pe", lambda e: e.matmul(bmb.t[0:128, 0:Tc], onesrow.t[:], mrow2, start=True, stop=True),
                           reads=[rows2.buf, onesrow.buf], writes=[bmb.buf])
                    P.emit("pe", lambda e: e.matmul(brb.t[0:128, 0:Tc], onesrow.t[:], vrow2, start=True, stop=True),
                           reads=[rows2.buf, onesrow.buf], writes=[brb.buf])
                    tmpf = aux["tmpf"]
                    for c in range(NCT):
                        sl = slice(c * Tc, (c + 1) * Tc)
                        P.emit("dve", lambda e, sl=sl: e.tensor_sub(tmpf.t[:, 0:Tc], lnpf.t[:, sl], bmb.t[0:128, 0:Tc]),
                               reads=[lnpf.buf, bmb.buf], writes=[tmpf.buf])
                        P.emit("dve", lambda e, sl=sl: e.tensor_mul(tmpf.t[:, 0:Tc], tmpf.t[:, 0:Tc], brb.t[0:128, 0:Tc]),
                               reads=[tmpf.buf, brb.buf], writes=[tmpf.buf])
                        P.emit("dve", lambda e, c=c, sl=sl: e.tensor_scalar(
                            lnc.t[:, sl], tmpf.t[:, 0:Tc],
                            sbcol.t[:, 0 * NCT + c:0 * NCT + c + 1],
                            sbcol.t[:, 1 * NCT + c:1 * NCT + c + 1],
                            ALU.mult, ALU.add),
                            reads=[tmpf.buf, sbcol.buf], writes=[lnc.buf])
                    for proj, wT, dstT in (("q", wdqT, cqT), ("kv", wdkT, ckvT)):
                        for oc in range(NQT):
                            bk = P.pb[4 + (oc % 2)]
                            def mmd(e, wT=wT, oc=oc, bk=bk):
                                i = None
                                for ci in range(NCT):
                                    i = e.matmul(bk.t[:, 0:Tc],
                                                 wT.t[:, ci * DQL + oc * 128: ci * DQL + oc * 128 + 128],
                                                 lnc.t[:, ci * Tc:(ci + 1) * Tc],
                                                 start=(ci == 0), stop=(ci == NCT - 1))
                                return i
                            P.emit("pe", mmd, reads=[wT.buf, lnc.buf], writes=[bk.buf])
                            P.emit("act", lambda e, dstT=dstT, oc=oc, t0=t0, bk=bk: e.activation(
                                dstT.t[:, oc * S + t0: oc * S + t0 + Tc], bk.t[:, 0:Tc], AF.Copy),
                                reads=[bk.buf], writes=[dstT.buf])
                P.barrier_no_qs()
                P.flush("l1a_A")

            with contextlib.ExitStack() as pc:
                wuqT = P.sbuf("wuqT", [128, NQT * 1024], BF16, stack=pc)
                wukT = P.sbuf("wukT", [128, NQT * 1024], BF16, stack=pc)
                wuvT = P.sbuf("wuvT", [128, NQT * 1024], BF16, stack=pc)
                with contextlib.ExitStack() as pw:
                    wfC = P.sbuf("wfC", [128, 2 * 512], F32, stack=pw)
                    emit_pe_weight_load_T(P, w["w_up_q"], 1024, DQL, wuqT, cmn, wfC, pw, qload="qa")
                    emit_pe_weight_load_T(P, w["w_up_k"], 1024, DQL, wukT, cmn, wfC, pw, qload="qa")
                    emit_pe_weight_load_T(P, w["w_up_v"], 1024, DQL, wuvT, cmn, wfC, pw, qload="qa")
                    P.barrier_no_qs()
                    P.flush("l1a_wu")
                ropeA = P.sbuf("ropeAsb", [64, 512], F32, stack=pc)
                ropeB = P.sbuf("ropeBsb", [64, 512], F32, stack=pc)
                swap64 = P.sbuf("swap64sb", [64, 64], BF16, stack=pc)
                swf = P.sbuf("swf", [64, 64], F32, stack=pc)
                P.dma("qa", lambda e: e.dma_start(swf.t[:], w["swap64"]), writes=[swf.buf])
                P.emit("act", lambda e: e.activation(swap64.t[:], swf.t[:], AF.Copy),
                       reads=[swf.buf], writes=[swap64.buf])
                qTh = P.sbuf("qTh", [64, 2 * S], BF16, stack=pc)
                kTh = P.sbuf("kTh", [64, 2 * S], BF16, stack=pc)
                NT = S // 128
                nic = S // 512
                vON = P.sbuf("vON", [128, 2 * NT * 66], BF16, stack=pc)
                qbf = P.sbuf("qbf", [64, 512], BF16, stack=pc)
                rt1 = P.sbuf("rt1", [64, 512], F32, stack=pc)
                rt2 = rb_shared = None
                expT = P.sbuf("expT", [128, 3 * 512], BF16, stack=pc)
                rb = P.sbuf("rb", [64, 512], F32, stack=pc)
                rt2 = rb
                qB = [Buf("qThA"), Buf("qThB")]
                kB = [Buf("kThA"), Buf("kThB")]
                vB = [Buf("vONA"), Buf("vONB")]

                def emit_build(h):
                    sl = (h % 2) * S
                    vsl = (h % 2) * NT * 66
                    for icx in range(nic):
                        for which, wupT, dstT, dB in (("q", wuqT, qTh, qB), ("k", wukT, kTh, kB)):
                            i0 = icx * 512
                            if which == "q":
                                P.dma("qa", lambda e, i0=i0: e.dma_start(
                                    ropeA.t[:], w["ropeA"][:, i0:i0 + 512]),
                                    writes=[ropeA.buf])
                                P.dma("qa", lambda e, i0=i0: e.dma_start(
                                    ropeB.t[:], w["ropeB"][:, i0:i0 + 512]),
                                    writes=[ropeB.buf])
                            bk = P.pb[1]
                            def mmq(e, wupT=wupT, h=h, i0=i0, bk=bk):
                                i = None
                                for ci in range(NQT):
                                    i = e.matmul(bk.t[0:64, 0:512],
                                                 wupT.t[:, ci * 1024 + h * 64: ci * 1024 + h * 64 + 64],
                                                 (cqT if wupT is wuqT else ckvT).t[:, ci * S + i0: ci * S + i0 + 512],
                                                 start=(ci == 0), stop=(ci == NQT - 1))
                                return i
                            P.emit("pe", mmq, reads=[wupT.buf, (cqT if which == "q" else ckvT).buf],
                                   writes=[bk.buf])
                            sc = 1.0 if which == "q" else 0.125
                            P.emit("act", lambda e, bk=bk, sc=sc: e.activation(
                                qbf.t[:], bk.t[0:64, 0:512], AF.Copy, scale=sc),
                                reads=[bk.buf], writes=[qbf.buf])
                            P.emit("pe", lambda e: e.matmul(
                                P.pb[2].t[0:64, 0:512], swap64.t[:], qbf.t[:], start=True, stop=True),
                                reads=[swap64.buf, qbf.buf], writes=[P.pb[2].buf])
                            P.emit("dve", lambda e: e.tensor_mul(
                                rt1.t[:], qbf.t[:], ropeA.t[:]),
                                reads=[qbf.buf, ropeA.buf], writes=[rt1.buf])
                            P.emit("dve", lambda e: e.tensor_mul(
                                rt2.t[:], P.pb[2].t[0:64, 0:512], ropeB.t[:]),
                                reads=[P.pb[2].buf, ropeB.buf], writes=[rt2.buf])
                            P.emit("dve", lambda e, dstT=dstT, i0=i0, sl=sl, dB=dB, h=h: e.tensor_add(
                                dstT.t[:, sl + i0: sl + i0 + 512], rt1.t[:], rt2.t[:]),
                                reads=[rt1.buf, rt2.buf], writes=[dB[h % 2]])
                    P.emit("dve", lambda e, vsl=vsl: e.memset(vON.t[:, vsl:vsl + NT * 66], 1.0),
                           writes=[vB[h % 2]])
                    for tt in range(NT):
                        bk = P.pb[1]
                        def mmv(e, h=h, tt=tt, bk=bk):
                            i = None
                            for ci in range(NQT):
                                i = e.matmul(bk.t[:, 0:64],
                                             ckvT.t[:, ci * S + tt * 128: ci * S + tt * 128 + 128],
                                             wuvT.t[:, ci * 1024 + h * 64: ci * 1024 + h * 64 + 64],
                                             start=(ci == 0), stop=(ci == NQT - 1))
                            return i
                        P.emit("pe", mmv, reads=[ckvT.buf, wuvT.buf], writes=[bk.buf])
                        P.emit("act", lambda e, tt=tt, bk=bk, vsl=vsl, h=h: e.activation(
                            vON.t[:, vsl + tt * 66: vsl + tt * 66 + 64], bk.t[:, 0:64], AF.Copy),
                            reads=[bk.buf], writes=[vB[h % 2]])

                emit_build(0)
                for h in range(H):
                    c_att = h // 2
                    r_att = (h % 2) * 64
                    sl = (h % 2) * S
                    vsl = (h % 2) * NT * 66
                    expB = [Buf("expA"), Buf("expB"), Buf("expC")]
                    LA = 2
                    for icx in range(nic):
                        i0 = icx * 512
                        def em_s(J, i0=i0, sl=sl, h=h):
                            bs = P.pb[5 + (J % 3)]
                            P.emit("pe", lambda e, J=J, i0=i0, bs=bs, sl=sl: e.matmul(
                                bs.t[:, 0:512], kTh.t[:, sl + J * 128: sl + (J + 1) * 128],
                                qTh.t[:, sl + i0: sl + i0 + 512], start=True, stop=True),
                                reads=[kB[h % 2], qB[h % 2]], writes=[bs.buf])
                            eh = (J % 3) * 512
                            eb = expB[J % 3]
                            P.emit("act", lambda e, eh=eh, bs=bs: e.activation(
                                expT.t[:, eh:eh + 512], bs.t[:, 0:512], AF.Exp),
                                reads=[bs.buf], writes=[eb])
                        def em_av(J, vsl=vsl, h=h):
                            eh = (J % 3) * 512
                            eb = expB[J % 3]
                            P.emit("pe", lambda e, J=J, eh=eh, vsl=vsl: e.matmul(
                                P.pb[4].t[0:65, 0:512], vON.t[:, vsl + J * 66: vsl + J * 66 + 65],
                                expT.t[:, eh:eh + 512], start=(J == 0), stop=(J == NT - 1)),
                                reads=[vB[h % 2], eb], writes=[P.pb[4].buf])
                        for J in range(NT):
                            em_s(J)
                            if J >= LA:
                                em_av(J - LA)
                        for J in range(NT - LA, NT):
                            em_av(J)
                        P.emit("dve", lambda e: e.reciprocal(rb.t[0:1, 0:512], P.pb[4].t[64:65, 0:512]),
                               reads=[P.pb[4].buf], writes=[rb.buf])
                        P.emit("pe", lambda e: e.matmul(
                            P.pb[3].t[0:64, 0:512], cmn["onesrow"].t[0:1, 0:64], rb.t[0:1, 0:512],
                            start=True, stop=True),
                            reads=[cmn["onesrow"].buf, rb.buf], writes=[P.pb[3].buf])
                        P.emit("act", lambda e: e.activation(rb.t[:], P.pb[3].t[0:64, 0:512], AF.Copy),
                               reads=[P.pb[3].buf], writes=[rb.buf])
                        P.emit("dve", lambda e, c_att=c_att, r_att=r_att, i0=i0: e.tensor_mul(
                            attnT.t[r_att:r_att + 64, c_att * S + i0: c_att * S + i0 + 512],
                            P.pb[4].t[0:64, 0:512], rb.t[:]),
                            reads=[P.pb[4].buf, rb.buf], writes=[attnT.buf])
                        if icx == 0 and h + 1 < H:
                            emit_build(h + 1)
                P.barrier_no_qs()
                P.flush("l1a_C")

        with contextlib.ExitStack() as pd:
            woutT = P.sbuf("woutT", [128, NCT * 1024], BF16, stack=pd)
            wfD = P.sbuf("wfD", [128, 2 * 1024], F32, stack=pd)
            emit_pe_weight_load_T(P, w["w_out"], 1024, 1024, woutT, cmn, wfD, pd, qload="qa")
            Tc = min(512, S)
            ntc = S // Tc
            xs = P.sbuf("xsD", [128, Tc], F32, stack=pd)
            ls = P.sbuf("lsD", [128, Tc], F32, stack=pd)
            oo = P.sbuf("ooD", [128, Tc], F32, stack=pd)
            for tci in range(ntc):
                t0 = tci * Tc
                for oc in range(NCT):
                    bk = P.pb[4 + (oc % 2)]
                    def mmo(e, oc=oc, t0=t0, bk=bk):
                        i = None
                        for ci in range(NCT):
                            i = e.matmul(bk.t[:, 0:Tc],
                                         woutT.t[:, ci * 1024 + oc * 128: ci * 1024 + oc * 128 + 128],
                                         attnT.t[:, ci * S + t0: ci * S + t0 + Tc],
                                         start=(ci == 0), stop=(ci == NCT - 1))
                        return i
                    P.emit("pe", mmo, reads=[woutT.buf, attnT.buf], writes=[bk.buf])
                    P.dma("qa", lambda e, oc=oc, t0=t0: e.dma_start(
                        xs.t[:, 0:Tc], xin_ap[:, oc * S + t0: oc * S + t0 + Tc]),
                        writes=[xs.buf])
                    P.dma("qa", lambda e, oc=oc, t0=t0: e.dma_start(
                        ls.t[:, 0:Tc], lnp_d.t[:, oc * S + t0: oc * S + t0 + Tc]),
                        reads=[lnp_d.buf], writes=[ls.buf])
                    P.emit("dve", lambda e, bk=bk: e.tensor_add(oo.t[:, 0:Tc], bk.t[:, 0:Tc], ls.t[:, 0:Tc]),
                           reads=[bk.buf, ls.buf], writes=[oo.buf])
                    P.emit("dve", lambda e: e.tensor_add(oo.t[:, 0:Tc], oo.t[:, 0:Tc], xs.t[:, 0:Tc]),
                           reads=[oo.buf, xs.buf], writes=[oo.buf])
                    P.dma("qa", lambda e, oc=oc, t0=t0: e.dma_start(
                        out_ap[:, oc * S + t0: oc * S + t0 + Tc], oo.t[:, 0:Tc]),
                        reads=[oo.buf], writes=[outbuf])
            P.barrier()
            P.flush("l1a_D")



def build_l0attn(P, S, xin_ap, w, out_ap, outbuf):
    NH, DHh = 4, 256
    NT = S // 128
    nic = S // 512
    EPS0 = 1e-8
    ln0_d = P.dram("ln0d", [128, NCT * S], BF16)

    with contextlib.ExitStack() as ph:
        cmn = make_common(P, ph, w["ident"])

        with contextlib.ExitStack() as p0:
            Tc0 = min(512, S)
            xc0 = P.sbuf("xc00", [128, NCT * Tc0], F32, stack=p0)
            lnc0 = P.sbuf("lnc00", [128, NCT * Tc0], BF16, stack=p0)
            aux0 = make_ln_aux(P, Tc0, p0, (P.pb[0], P.pb[1], P.pb[2], P.pb[3]))
            for tci in range(S // Tc0):
                t0 = tci * Tc0
                for c in range(NCT):
                    P.dma("qs", lambda e, c=c, t0=t0: e.dma_start(
                        xc0.t[:, c * Tc0:(c + 1) * Tc0], xin_ap[:, c * S + t0: c * S + t0 + Tc0]),
                        writes=[xc0.buf])
                emit_ln_chunk(P, xc0, lnc0, Tc0, aux0, cmn)
                for c in range(NCT):
                    P.dma("qa", lambda e, c=c, t0=t0: e.dma_start(
                        ln0_d.t[:, c * S + t0: c * S + t0 + Tc0], lnc0.t[:, c * Tc0:(c + 1) * Tc0]),
                        reads=[lnc0.buf], writes=[ln0_d.buf])
            P.barrier_partial()
            P.flush("l0a_ln")

        m_pt_all = P.sbuf("m_pt_all", [128, 4 * (S // 128)], F32, stack=ph)
        cs_sb = P.sbuf("cs_sb", [4, S], F32, stack=ph)
        pm_sb = P.sbuf("pm_sb", [4, S], F32, stack=ph)
        with contextlib.ExitStack() as p1:
            ig_sb = P.sbuf("ig_sb", [4, S], F32, stack=p1)
            fg_sb = P.sbuf("fg_sb", [4, S], F32, stack=p1)
            wisum = P.sbuf("wisum", [4, 3 * 1024], F32, stack=p1)
            wfsum = P.sbuf("wfsum", [4, 3 * 1024], F32, stack=p1)
            P.dma("qs", lambda e: e.dma_start(wisum.t[:], w["wi"]), writes=[wisum.buf])
            P.dma("qs", lambda e: e.dma_start(wfsum.t[:], w["wf"]), writes=[wfsum.buf])
            for wg in (wisum, wfsum):
                P.emit("dve", lambda e, wg=wg: e.tensor_add(
                    wg.t[:, 0:1024], wg.t[:, 0:1024], wg.t[:, 1024:2048]),
                    reads=[wg.buf], writes=[wg.buf])
                P.emit("dve", lambda e, wg=wg: e.tensor_add(
                    wg.t[:, 0:1024], wg.t[:, 0:1024], wg.t[:, 2048:3072]),
                    reads=[wg.buf], writes=[wg.buf])
            wgT = P.sbuf("wgT", [128, NCT * 64], BF16, stack=p1)
            P.emit("dve", lambda e: e.memset(wgT.t[:], 0.0), writes=[wgT.buf])
            for c in range(NCT):
                P.emit("pe", lambda e, c=c: e.transpose(
                    P.pb[7].t[0:128, 0:4], wisum.t[:, c * 128:(c + 1) * 128],
                    cmn["ident"].t[0:4, 0:4]),
                    reads=[wisum.buf, cmn["ident"].buf], writes=[P.pb[7].buf])
                P.emit("dve", lambda e, c=c: e.tensor_copy(
                    wgT.t[:, c * 64: c * 64 + 4], P.pb[7].t[0:128, 0:4]),
                    reads=[P.pb[7].buf], writes=[wgT.buf])
                P.emit("pe", lambda e, c=c: e.transpose(
                    P.pb[7].t[0:128, 0:4], wfsum.t[:, c * 128:(c + 1) * 128],
                    cmn["ident"].t[0:4, 0:4]),
                    reads=[wfsum.buf, cmn["ident"].buf], writes=[P.pb[7].buf])
                P.emit("dve", lambda e, c=c: e.tensor_copy(
                    wgT.t[:, c * 64 + 32: c * 64 + 36], P.pb[7].t[0:128, 0:4]),
                    reads=[P.pb[7].buf], writes=[wgT.buf])
            bicol = P.sbuf("bicol", [4, 1], F32, stack=p1)
            bfcol = P.sbuf("bfcol", [4, 1], F32, stack=p1)
            P.dma("qs", lambda e: e.dma_start(bicol.t[:], w["bi"].unsqueeze(1)), writes=[bicol.buf])
            P.dma("qs", lambda e: e.dma_start(bfcol.t[:], w["bf"].unsqueeze(1)), writes=[bfcol.buf])
            lnch = P.sbuf("lnch1", [128, NCT * 512], BF16, stack=p1)
            for icx in range(nic):
                i0 = icx * 512
                for c in range(NCT):
                    P.dma("qs", lambda e, c=c, i0=i0: e.dma_start(
                        lnch.t[:, c * 512:(c + 1) * 512], ln0_d.t[:, c * S + i0: c * S + i0 + 512]),
                        reads=[ln0_d.buf], writes=[lnch.buf])
                def gmm(e):
                    i = None
                    for c in range(NCT):
                        i = e.matmul(P.pb[4].t[0:64, 0:512], wgT.t[:, c * 64:(c + 1) * 64],
                                     lnch.t[:, c * 512:(c + 1) * 512],
                                     start=(c == 0), stop=(c == NCT - 1))
                    return i
                P.emit("pe", gmm, reads=[wgT.buf, lnch.buf], writes=[P.pb[4].buf])
                P.emit("dve", lambda e, i0=i0: e.tensor_scalar_add(
                    ig_sb.t[:, i0:i0 + 512], P.pb[4].t[0:4, 0:512], bicol.t[:]),
                    reads=[P.pb[4].buf, bicol.buf], writes=[ig_sb.buf])
                P.emit("dve", lambda e, i0=i0: e.tensor_copy(
                    fg_sb.t[:, i0:i0 + 512], P.pb[4].t[32:36, 0:512]),
                    reads=[P.pb[4].buf], writes=[fg_sb.buf])
                P.emit("dve", lambda e, i0=i0: e.tensor_scalar_add(
                    fg_sb.t[:, i0:i0 + 512], fg_sb.t[:, i0:i0 + 512], bfcol.t[:]),
                    reads=[fg_sb.buf, bfcol.buf], writes=[fg_sb.buf])
            ones4 = P.sbuf("ones4", [4, 1], F32, stack=p1)
            P.emit("dve", lambda e: e.memset(ones4.t[:], 1.0), writes=[ones4.buf])
            tmp4 = P.sbuf("tmp4", [4, S], F32, stack=p1)
            P.emit("act", lambda e: e.activation(tmp4.t[:], fg_sb.t[:], AF.Abs),
                   reads=[fg_sb.buf], writes=[tmp4.buf])
            P.emit("act", lambda e: e.activation(tmp4.t[:], tmp4.t[:], AF.Exp, scale=-1.0),
                   reads=[tmp4.buf], writes=[tmp4.buf])
            P.emit("act", lambda e: e.activation(tmp4.t[:], tmp4.t[:], AF.Ln, bias=ones4.t[:]),
                   reads=[tmp4.buf, ones4.buf], writes=[tmp4.buf])
            P.emit("dve", lambda e: e.tensor_scalar_min(fg_sb.t[:], fg_sb.t[:], 0.0),
                   reads=[fg_sb.buf], writes=[fg_sb.buf])
            P.emit("dve", lambda e: e.tensor_sub(fg_sb.t[:], fg_sb.t[:], tmp4.t[:]),
                   reads=[fg_sb.buf, tmp4.buf], writes=[fg_sb.buf])
            P.emit("dve", lambda e: e.tensor_tensor_scan(
                cs_sb.t[:], fg_sb.t[:], fg_sb.t[:], 0.0, ALU.add, ALU.bypass),
                reads=[fg_sb.buf], writes=[cs_sb.buf])
            P.emit("dve", lambda e: e.tensor_sub(ig_sb.t[:], ig_sb.t[:], cs_sb.t[:]),
                   reads=[ig_sb.buf, cs_sb.buf], writes=[ig_sb.buf])
            P.emit("dve", lambda e: e.tensor_tensor_scan(
                pm_sb.t[:], ig_sb.t[:], ig_sb.t[:], -1e30, ALU.max, ALU.bypass),
                reads=[ig_sb.buf], writes=[pm_sb.buf])
            for t in range(NT):
                P.emit("pe", lambda e, t=t: e.transpose(
                    P.pb[7].t[0:128, 0:4], ig_sb.t[0:4, t * 128:(t + 1) * 128],
                    cmn["ident"].t[0:4, 0:4]),
                    reads=[ig_sb.buf, cmn["ident"].buf], writes=[P.pb[7].buf])
                P.emit("dve", lambda e, t=t: e.tensor_copy(
                    m_pt_all.t[:, t:t + 3 * NT + 1:NT], P.pb[7].t[0:128, 0:4]),
                    reads=[P.pb[7].buf], writes=[m_pt_all.buf])
            P.barrier_partial()
            P.flush("l0a_gates")

        with contextlib.ExitStack() as p2:
            gcol = P.sbuf("gcol", [128, NCT], F32, stack=p2)
            with contextlib.ExitStack() as vph:
                emit_vecs_to_cols(P, [w["ln_w"]], gcol, NCT, cmn["ident"], P.pb[7], vph, tag="lw")
                P.barrier_partial()
                P.flush("l0a_lw")
            P.emit("dve", lambda e: e.tensor_scalar_add(gcol.t[:], gcol.t[:], 1.0),
                   reads=[gcol.buf], writes=[gcol.buf])
            wq3 = P.sbuf("wq3", [128, NCT * 128], BF16, stack=p2)
            wk3 = P.sbuf("wk3", [128, NCT * 128], BF16, stack=p2)
            wv3 = P.sbuf("wv3", [128, NCT * 128], BF16, stack=p2)
            wf3 = P.sbuf("wf3", [128, NCT * 128], F32, stack=p2)
            for name, dst in (("wqbd", wq3), ("wkbd", wk3), ("wvbd", wv3)):
                for c in range(NCT):
                    P.dma("qs", lambda e, name=name, c=c: e.dma_start(
                        wf3.t[:, c * 128:(c + 1) * 128], w[name][c]),
                        writes=[wf3.buf])
                P.emit("act", lambda e, dst=dst: e.activation(dst.t[:], wf3.t[:], AF.Copy),
                       reads=[wf3.buf], writes=[dst.buf])
            masks = P.sbuf("maskssb", [128, 4 * 512], F32, stack=p2)
            P.dma("qs", lambda e: e.dma_start(masks.t[:], w["masks"]), writes=[masks.buf])

            ln0h = P.sbuf("ln0h", [128, 2 * S], BF16, stack=p2)
            qTh = P.sbuf("qTh0", [128, 2 * S], BF16, stack=p2)
            kTh = P.sbuf("kTh0", [128, 2 * S], BF16, stack=p2)
            vpl = P.sbuf("vpl", [128, NT * 258], BF16, stack=p2)
            pm_ch = P.sbuf("pm_ch", [1, 2 * 512], F32, stack=p2)
            cs_ch = P.sbuf("cs_ch", [1, 2 * 512], F32, stack=p2)
            eI = P.sbuf("eI", [128, 2 * NT], F32, stack=p2)
            negA = P.sbuf("negA", [128, 2], F32, stack=p2)
            srow = P.sbuf("srow", [1, 2 * 512], F32, stack=p2)
            hraw = P.sbuf("hraw", [128, 2 * 1024], F32, stack=p2)
            pmB = [Buf("pmA"), Buf("pmB")]
            csB = [Buf("csA"), Buf("csB")]
            eIB = [Buf("eIA"), Buf("eIB")]
            nAB = [Buf("nA0"), Buf("nA1")]
            srB = [Buf("srA"), Buf("srB")]
            hrB = [Buf("hrA"), Buf("hrB")]
            pending_ep = []
            cbf = P.sbuf("cbf", [128, 3 * 512], BF16, stack=p2)
            ctmp = P.sbuf("ctmp", [128, 512], F32, stack=p2)
            r512a = P.sbuf("r512a", [1, 512], F32, stack=p2)
            r512b = P.sbuf("r512b", [1, 512], F32, stack=p2)
            rb0 = P.sbuf("rb00", [128, 512], F32, stack=p2)
            hout = P.sbuf("hout", [128, 2 * 512], F32, stack=p2)
            hcast = P.sbuf("hcast", [128, 2 * 512], BF16, stack=p2)
            hsq = P.sbuf("hsq", [128, 2 * 512], BF16, stack=p2)
            srows = P.sbuf("srows", [1, 3 * 512], F32, stack=p2)
            xs0 = P.sbuf("xs00", [128, 512], F32, stack=p2)
            oo0 = P.sbuf("oo00", [128, 512], F32, stack=p2)

            for h in range(NH):
                for dt_ in range(2):
                    P.dma("qs", lambda e, h=h, dt_=dt_: e.dma_start(
                        ln0h.t[:, dt_ * S:(dt_ + 1) * S],
                        ln0_d.t[:, (2 * h + dt_) * S:(2 * h + dt_ + 1) * S]),
                        reads=[ln0_d.buf], writes=[ln0h.buf])
                for dt_ in range(2):
                    for icx in range(nic):
                        i0 = icx * 512
                        for which, wbd, dstT, sc in (("q", wq3, qTh, 1.0), ("k", wk3, kTh, 1.0 / 16.0)):
                            bk = P.pb[5]
                            P.emit("pe", lambda e, h=h, dt_=dt_, i0=i0, wbd=wbd, bk=bk: e.matmul(
                                bk.t[:, 0:512], wbd.t[:, (2 * h + dt_) * 128:(2 * h + dt_ + 1) * 128],
                                ln0h.t[:, dt_ * S + i0: dt_ * S + i0 + 512], start=True, stop=True),
                                reads=[wbd.buf, ln0h.buf], writes=[bk.buf])
                            P.emit("act", lambda e, dstT=dstT, dt_=dt_, i0=i0, sc=sc, bk=bk: e.activation(
                                dstT.t[:, dt_ * S + i0: dt_ * S + i0 + 512], bk.t[:, 0:512],
                                AF.Copy, scale=sc),
                                reads=[bk.buf], writes=[dstT.buf])
                P.emit("dve", lambda e: e.memset(vpl.t[:], 1.0), writes=[vpl.buf])
                for tt in range(NT):
                    for dt_ in range(2):
                        bk = P.pb[5]
                        P.emit("pe", lambda e, h=h, tt=tt, dt_=dt_, bk=bk: e.matmul(
                            bk.t[:, 0:128],
                            ln0h.t[:, dt_ * S + tt * 128: dt_ * S + tt * 128 + 128],
                            wv3.t[:, (2 * h + dt_) * 128:(2 * h + dt_ + 1) * 128],
                            start=True, stop=True),
                            reads=[ln0h.buf, wv3.buf], writes=[bk.buf])
                        P.emit("act", lambda e, tt=tt, dt_=dt_, bk=bk: e.activation(
                            vpl.t[:, tt * 258 + dt_ * 128: tt * 258 + dt_ * 128 + 128],
                            bk.t[:, 0:128], AF.Copy),
                            reads=[bk.buf], writes=[vpl.buf])
                for icx in range(nic):
                    i0 = icx * 512
                    jmax = (i0 + 512) // 128 - 1
                    par = (h * nic + icx) % 2
                    po = par * 512
                    P.dma("qs", lambda e, h=h, i0=i0, po=po: e.dma_start(
                        pm_ch.t[0:1, po:po + 512], pm_sb.t[h:h + 1, i0:i0 + 512]),
                        reads=[pm_sb.buf], writes=[pmB[par]])
                    P.dma("qs", lambda e, h=h, i0=i0, po=po: e.dma_start(
                        cs_ch.t[0:1, po:po + 512], cs_sb.t[h:h + 1, i0:i0 + 512]),
                        reads=[cs_sb.buf], writes=[csB[par]])
                    P.emit("pe", lambda e, po=po: e.matmul(
                        P.pb[7].t[0:128, 0:1], cmn["onesrow"].t[:],
                        pm_ch.t[0:1, po + 511:po + 512], start=True, stop=True),
                        reads=[cmn["onesrow"].buf, pmB[par]], writes=[P.pb[7].buf])
                    P.emit("act", lambda e, par=par: e.activation(
                        negA.t[:, par:par + 1], P.pb[7].t[0:128, 0:1], AF.Copy, scale=-1.0),
                        reads=[P.pb[7].buf], writes=[nAB[par]])
                    P.emit("act", lambda e, jmax=jmax, h=h, par=par: e.activation(
                        eI.t[:, par * NT: par * NT + jmax + 1],
                        m_pt_all.t[:, h * NT: h * NT + jmax + 1],
                        AF.Exp, bias=negA.t[:, par:par + 1]),
                        reads=[m_pt_all.buf, nAB[par]], writes=[eIB[par]])
                    cbfB = [Buf("cbfA"), Buf("cbfB"), Buf("cbfC")]
                    LA0 = 2
                    def em_qk(J, i0=i0, jmax=jmax, par=par):
                        bqk = P.pb[4 + (J % 3)]
                        def mmqk(e, J=J, i0=i0, bqk=bqk):
                            i = None
                            for dt_ in range(2):
                                i = e.matmul(bqk.t[:, 0:512],
                                             kTh.t[:, dt_ * S + J * 128: dt_ * S + J * 128 + 128],
                                             qTh.t[:, dt_ * S + i0: dt_ * S + i0 + 512],
                                             start=(dt_ == 0), stop=(dt_ == 1))
                            return i
                        P.emit("pe", mmqk, reads=[kTh.buf, qTh.buf], writes=[bqk.buf])
                        dJ = J - i0 // 128
                        ch = (J % 3) * 512
                        cB = cbfB[J % 3]
                        if dJ < 0:
                            if J % 2 == 0:
                                P.emit("act", lambda e, J=J, ch=ch, bqk=bqk, par=par: e.activation(
                                    cbf.t[:, ch:ch + 512], bqk.t[:, 0:512], AF.Copy,
                                    scale=eI.t[:, par * NT + J: par * NT + J + 1]),
                                    reads=[bqk.buf, eIB[par]], writes=[cB])
                            else:
                                P.emit("dve", lambda e, J=J, ch=ch, bqk=bqk, par=par: e.tensor_scalar(
                                    cbf.t[:, ch:ch + 512], bqk.t[:, 0:512],
                                    eI.t[:, par * NT + J: par * NT + J + 1], None, ALU.mult),
                                    reads=[bqk.buf, eIB[par]], writes=[cB])
                        else:
                            P.emit("act", lambda e, J=J, bqk=bqk, par=par: e.activation(
                                ctmp.t[:], bqk.t[:, 0:512], AF.Copy,
                                scale=eI.t[:, par * NT + J: par * NT + J + 1]),
                                reads=[bqk.buf, eIB[par]], writes=[ctmp.buf])
                            P.emit("dve", lambda e, dJ=dJ, ch=ch: e.tensor_mul(
                                cbf.t[:, ch:ch + 512], ctmp.t[:], masks.t[:, dJ * 512:(dJ + 1) * 512]),
                                reads=[ctmp.buf, masks.buf], writes=[cB])
                    def em_av3(J, jmax=jmax):
                        ch = (J % 3) * 512
                        cB = cbfB[J % 3]
                        for mi, (mof, mw, bk) in enumerate(((0, 128, P.pb[0]), (128, 128, P.pb[1]),
                                                            (256, 1, P.pb[2]))):
                            P.emit("pe", lambda e, J=J, ch=ch, mof=mof, mw=mw, bk=bk: e.matmul(
                                bk.t[0:mw, 0:512], vpl.t[:, J * 258 + mof: J * 258 + mof + mw],
                                cbf.t[:, ch:ch + 512], start=(J == 0), stop=(J == jmax)),
                                reads=[vpl.buf, cB], writes=[bk.buf])
                    for J in range(jmax + 1):
                        em_qk(J)
                        if J >= LA0:
                            em_av3(J - LA0)
                    for J in range(max(0, jmax + 1 - LA0), jmax + 1):
                        em_av3(J)
                    P.emit("dve", lambda e, po=po: e.tensor_copy(srow.t[0:1, po:po + 512],
                                                                 P.pb[2].t[0:1, 0:512]),
                           reads=[P.pb[2].buf], writes=[srB[par]])
                    P.emit("dve", lambda e, par=par: e.tensor_copy(hraw.t[:, par * 1024: par * 1024 + 512],
                                                                   P.pb[0].t[:, 0:512]),
                           reads=[P.pb[0].buf], writes=[hrB[par]])
                    P.emit("dve", lambda e, par=par: e.tensor_copy(hraw.t[:, par * 1024 + 512: par * 1024 + 1024],
                                                                   P.pb[1].t[:, 0:512]),
                           reads=[P.pb[1].buf], writes=[hrB[par]])

                    def epilogue(h=h, i0=i0, par=par, po=po):
                        hof = par * 1024
                        P.emit("act", lambda e: e.activation(
                            r512a.t[:], srow.t[0:1, po:po + 512], AF.Abs),
                            reads=[srB[par]], writes=[r512a.buf])
                        P.emit("act", lambda e: e.activation(
                            r512b.t[:], cs_ch.t[0:1, po:po + 512], AF.Exp, scale=-1.0,
                            bias=negA.t[0:1, par:par + 1]),
                            reads=[csB[par], nAB[par]], writes=[r512b.buf])
                        P.emit("dve", lambda e: e.tensor_max(r512a.t[:], r512a.t[:], r512b.t[:]),
                               reads=[r512a.buf, r512b.buf], writes=[r512a.buf])
                        P.emit("act", lambda e: e.activation(
                            r512b.t[:], pm_ch.t[0:1, po:po + 512], AF.Exp, bias=negA.t[0:1, par:par + 1]),
                            reads=[pmB[par], nAB[par]], writes=[r512b.buf])
                        P.emit("dve", lambda e: e.scalar_tensor_tensor(
                            r512a.t[:], r512b.t[:], EPS0, r512a.t[:], ALU.mult, ALU.add),
                            reads=[r512a.buf, r512b.buf], writes=[r512a.buf])
                        P.emit("dve", lambda e: e.reciprocal(r512a.t[:], r512a.t[:]),
                               reads=[r512a.buf], writes=[r512a.buf])
                        P.emit("pe", lambda e: e.matmul(
                            P.pb[7].t[0:128, 0:512], cmn["onesrow"].t[:], r512a.t[:],
                            start=True, stop=True),
                            reads=[cmn["onesrow"].buf, r512a.buf], writes=[P.pb[7].buf])
                        P.emit("act", lambda e: e.activation(rb0.t[:], P.pb[7].t[0:128, 0:512], AF.Copy),
                               reads=[P.pb[7].buf], writes=[rb0.buf])
                        P.emit("dve", lambda e: e.tensor_mul(hout.t[:, 0:512],
                                                             hraw.t[:, hof:hof + 512], rb0.t[:]),
                               reads=[hrB[par], rb0.buf], writes=[hout.buf])
                        P.emit("dve", lambda e: e.tensor_mul(hout.t[:, 512:1024],
                                                             hraw.t[:, hof + 512:hof + 1024], rb0.t[:]),
                               reads=[hrB[par], rb0.buf], writes=[hout.buf])
                        for ti in range(2):
                            sl = slice(ti * 512, ti * 512 + 512)
                            P.emit("act", lambda e, sl=sl: e.activation(hcast.t[:, sl], hout.t[:, sl], AF.Copy),
                                   reads=[hout.buf], writes=[hcast.buf])
                            P.emit("act", lambda e, sl=sl: e.activation(hsq.t[:, sl], hout.t[:, sl], AF.Square),
                                   reads=[hout.buf], writes=[hsq.buf])
                            P.emit("pe", lambda e, sl=sl, ti=ti: e.matmul(
                                P.pb[3].t[0:1, 0:512], cmn["onesb"].t[:], hcast.t[:, sl],
                                start=(ti == 0), stop=(ti == 1)),
                                reads=[hcast.buf, cmn["onesb"].buf], writes=[P.pb[3].buf])
                            P.emit("pe", lambda e, sl=sl, ti=ti: e.matmul(
                                P.pb[3].t[32:33, 0:512], cmn["onesb"].t[:], hsq.t[:, sl],
                                start=(ti == 0), stop=(ti == 1)),
                                reads=[hsq.buf, cmn["onesb"].buf], writes=[P.pb[3].buf])
                        mr = srows.t[0:1, 0:512]; qr = srows.t[0:1, 512:1024]; vr = srows.t[0:1, 1024:1536]
                        P.emit("dve", lambda e: e.tensor_scalar_mul(mr, P.pb[3].t[0:1, 0:512], 1.0 / DHh),
                               reads=[P.pb[3].buf], writes=[srows.buf])
                        P.emit("dve", lambda e: e.tensor_scalar_mul(qr, P.pb[3].t[32:33, 0:512], 1.0 / DHh),
                               reads=[P.pb[3].buf], writes=[srows.buf])
                        P.emit("dve", lambda e: e.tensor_mul(vr, mr, mr), reads=[srows.buf], writes=[srows.buf])
                        P.emit("dve", lambda e: e.tensor_sub(vr, qr, vr), reads=[srows.buf], writes=[srows.buf])
                        P.emit("act", lambda e: e.activation(vr, vr, AF.Sqrt, bias=cmn["eps1"].t[0:1, 0:1]),
                               reads=[srows.buf, cmn["eps1"].buf], writes=[srows.buf])
                        P.emit("dve", lambda e: e.reciprocal(vr, vr), reads=[srows.buf], writes=[srows.buf])
                        P.emit("pe", lambda e: e.matmul(P.pb[3].t[0:128, 0:512], cmn["onesrow"].t[:], mr,
                                                        start=True, stop=True),
                               reads=[srows.buf, cmn["onesrow"].buf], writes=[P.pb[3].buf])
                        P.emit("pe", lambda e: e.matmul(P.pb[7].t[0:128, 0:512], cmn["onesrow"].t[:], vr,
                                                        start=True, stop=True),
                               reads=[srows.buf, cmn["onesrow"].buf], writes=[P.pb[7].buf])
                        for ti in range(2):
                            ct = 2 * h + ti
                            sl = slice(ti * 512, ti * 512 + 512)
                            P.dma("qs", lambda e, ct=ct, i0=i0: e.dma_start(
                                xs0.t[:], xin_ap[:, ct * S + i0: ct * S + i0 + 512]),
                                writes=[xs0.buf])
                            P.emit("dve", lambda e, sl=sl: e.tensor_sub(oo0.t[:], hout.t[:, sl], P.pb[3].t[0:128, 0:512]),
                                   reads=[hout.buf, P.pb[3].buf], writes=[oo0.buf])
                            P.emit("dve", lambda e: e.tensor_mul(oo0.t[:], oo0.t[:], P.pb[7].t[0:128, 0:512]),
                                   reads=[oo0.buf, P.pb[7].buf], writes=[oo0.buf])
                            P.emit("dve", lambda e, ct=ct: e.tensor_scalar(
                                oo0.t[:], oo0.t[:], gcol.t[:, ct:ct + 1], None, ALU.mult),
                                reads=[oo0.buf, gcol.buf], writes=[oo0.buf])
                            P.emit("dve", lambda e: e.tensor_add(oo0.t[:], oo0.t[:], xs0.t[:]),
                                   reads=[oo0.buf, xs0.buf], writes=[oo0.buf])
                            P.dma("qa", lambda e, ct=ct, i0=i0: e.dma_start(
                                out_ap[:, ct * S + i0: ct * S + i0 + 512], oo0.t[:]),
                                reads=[oo0.buf], writes=[outbuf])

                    if pending_ep:
                        pending_ep.pop(0)()
                    pending_ep.append(epilogue)
            while pending_ep:
                pending_ep.pop(0)()
            P.barrier_partial()
            P.flush("l0a_main")


from concourse.bass_utils import run_bass_kernel_spmd

S_FULL = 4096
NCORES = 8


def _consts(S):
    c = {}
    c["ident"] = np.eye(128, dtype=np.float32)
    sel8 = np.zeros((8, 8 * 128), np.float32)
    for e in range(8):
        sel8[e, e * 128:(e + 1) * 128] = 1.0
    c["sel8"] = sel8
    masks = np.zeros((128, 4 * 512), np.float32)
    ii = np.arange(512)
    for dJ in range(4):
        for jj in range(128):
            masks[jj, dJ * 512:(dJ + 1) * 512] = (ii - dJ * 128 >= jj).astype(np.float32)
    c["masks"] = masks
    dim = 64
    inv_freq = 1.0 / 10000.0 ** (np.arange(0, dim, 2, dtype=np.float32) / dim)
    t = np.arange(S, dtype=np.float32)
    freqs = np.outer(t, inv_freq)
    cos = np.concatenate([np.cos(freqs), np.cos(freqs)], -1)[:, ::2]
    sin = np.concatenate([np.sin(freqs), np.sin(freqs)], -1)[:, ::2]
    A = np.zeros((64, S), np.float32)
    Bm = np.zeros((64, S), np.float32)
    for cc in range(32):
        A[2 * cc] = cos[:, cc]
        A[2 * cc + 1] = cos[:, cc]
        Bm[2 * cc] = -sin[:, cc]
        Bm[2 * cc + 1] = sin[:, cc]
    c["ropeA"] = A
    c["ropeB"] = Bm
    sw = np.zeros((64, 64), np.float32)
    for i in range(64):
        sw[i ^ 1, i] = 1.0
    c["swap64"] = sw
    return c


def _bd_chunks(wmat):
    out = np.zeros((8, 128, 128), np.float32)
    wr = wmat.reshape(8, 32, 4, 4)
    gg, o_, i_ = np.meshgrid(np.arange(32), np.arange(4), np.arange(4), indexing="ij")
    rows = (4 * gg + i_).ravel()
    cols = (4 * gg + o_).ravel()
    for cchunk in range(8):
        out[cchunk, rows, cols] = wr[cchunk][gg.ravel(), o_.ravel(), i_.ravel()]
    return out


def _decl(nc, ins, name, arr, dtype=F32):
    ap = nc.dram_tensor(name, list(arr.shape), dtype, kind="ExternalInput").ap()
    ins[name] = np.ascontiguousarray(arr)
    return ap


def _build_k0(S, con, weights):
    nc = bass.Bass("TRN2", target_bir_lowering=False)
    ins = {}
    xin = nc.dram_tensor("xin", [128, 8 * S], F32, kind="ExternalInput").ap()
    w = {}
    w["ident"] = _decl(nc, ins, "ident", con["ident"])
    w["masks"] = _decl(nc, ins, "masks", con["masks"])
    w["wqbd"] = _decl(nc, ins, "wqbd", _bd_chunks(weights["L0_attn_wq"]))
    w["wkbd"] = _decl(nc, ins, "wkbd", _bd_chunks(weights["L0_attn_wk"]))
    w["wvbd"] = _decl(nc, ins, "wvbd", _bd_chunks(weights["L0_attn_wv"]))
    for nm, key in (("wi", "L0_attn_wi"), ("bi", "L0_attn_bi"), ("wf", "L0_attn_wf"),
                    ("bf", "L0_attn_bf"), ("ln_w", "L0_attn_ln_w")):
        w[nm] = _decl(nc, ins, nm, weights[key])
    y = nc.dram_tensor("y", [128, 8 * S], F32, kind="ExternalOutput").ap()
    P = Prog(nc)
    build_l0attn(P, S, xin, w, y, Buf("y"))
    P.finish()
    return nc, ins


def _build_k1(S, con, weights):
    nc = bass.Bass("TRN2", target_bir_lowering=False)
    ins = {}
    xin = nc.dram_tensor("xin", [128, 8 * S], F32, kind="ExternalInput").ap()
    w = {}
    w["ident"] = _decl(nc, ins, "ident", con["ident"])
    w["sel8"] = _decl(nc, ins, "sel8", con["sel8"])
    for nm, key in (("Wg", "L0_ffn_Wg"), ("bg", "L0_ffn_bg"), ("W1", "L0_ffn_W1"),
                    ("b1", "L0_ffn_b1"), ("W2", "L0_ffn_W2"), ("b2", "L0_ffn_b2")):
        w[nm] = _decl(nc, ins, nm, weights[key])
    y = nc.dram_tensor("y", [128, 8 * S], F32, kind="ExternalOutput").ap()
    P = Prog(nc)
    build_l0ffn(P, S, xin, w, y, Buf("y"))
    P.finish()
    return nc, ins


def _build_k2(S, con, weights):
    nc = bass.Bass("TRN2", target_bir_lowering=False)
    ins = {}
    xin = nc.dram_tensor("xin", [128, 8 * S], F32, kind="ExternalInput").ap()
    w = {}
    w["ident"] = _decl(nc, ins, "ident", con["ident"])
    w["ropeA"] = _decl(nc, ins, "ropeA", con["ropeA"])
    w["ropeB"] = _decl(nc, ins, "ropeB", con["ropeB"])
    w["swap64"] = _decl(nc, ins, "swap64", con["swap64"])
    for nm in ("w_down_q", "w_down_kv", "w_up_q", "w_up_k", "w_up_v", "w_out",
               "ln_scale", "ln_bias"):
        w[nm] = _decl(nc, ins, nm, weights["L1_attn_" + nm])
    y = nc.dram_tensor("y", [128, 8 * S], F32, kind="ExternalOutput").ap()
    P = Prog(nc)
    build_l1attn(P, S, xin, w, y, Buf("y"))
    P.finish()
    return nc, ins


def _build_k3(S, con, weights):
    nc = bass.Bass("TRN2", target_bir_lowering=False)
    ins = {}
    xin = nc.dram_tensor("xin", [128, 8 * S], F32, kind="ExternalInput").ap()
    w = {}
    w["ident"] = _decl(nc, ins, "ident", con["ident"])
    for nm in ("w1_w", "w1_s", "w2_w", "w2_s", "h1_w", "h1_s", "h2_w", "h2_s",
               "w3_w", "w3_s"):
        w[nm] = _decl(nc, ins, nm, weights["L1_ffn_" + nm])
    y = nc.dram_tensor("y", [S, D], F32, kind="ExternalOutput").ap()
    P = Prog(nc)
    build_l1ffn(P, S, xin, w, "tok", y, Buf("y"))
    P.finish()
    return nc, ins


def kernel_multi(**inputs):
    S = S_FULL
    x = np.asarray(inputs["L0_attn_x"], dtype=np.float32)
    con = _consts(S)
    xf = [np.ascontiguousarray(x[i].reshape(S, 8, 128).transpose(2, 1, 0).reshape(128, 8 * S))
          for i in range(NCORES)]

    nc0, ins0 = _build_k0(S, con, inputs)
    maps = [dict(ins0, xin=xf[i]) for i in range(NCORES)]
    r = run_bass_kernel_spmd(nc0, maps, core_ids=list(range(NCORES)))
    cur = [r.results[i]["y"] for i in range(NCORES)]

    nc1, ins1 = _build_k1(S, con, inputs)
    maps = [dict(ins1, xin=cur[i]) for i in range(NCORES)]
    r = run_bass_kernel_spmd(nc1, maps, core_ids=list(range(NCORES)))
    cur = [r.results[i]["y"] for i in range(NCORES)]

    nc2, ins2 = _build_k2(S, con, inputs)
    maps = [dict(ins2, xin=cur[i]) for i in range(NCORES)]
    r = run_bass_kernel_spmd(nc2, maps, core_ids=list(range(NCORES)))
    cur = [r.results[i]["y"] for i in range(NCORES)]

    nc3, ins3 = _build_k3(S, con, inputs)
    maps = [dict(ins3, xin=cur[i]) for i in range(NCORES)]
    r = run_bass_kernel_spmd(nc3, maps, core_ids=list(range(NCORES)))
    out = np.stack([r.results[i]["y"] for i in range(NCORES)], axis=0)
    return out.astype(np.float32)


def _build_fused(S, con, weights):
    nc = bass.Bass("TRN2", target_bir_lowering=False)
    ins = {}
    xin = nc.dram_tensor("xin", [128, 8 * S], F32, kind="ExternalInput").ap()
    w = {}
    w["ident"] = _decl(nc, ins, "ident", con["ident"])
    w["masks"] = _decl(nc, ins, "masks", con["masks"])
    w["sel8"] = _decl(nc, ins, "sel8", con["sel8"])
    w["ropeA"] = _decl(nc, ins, "ropeA", con["ropeA"])
    w["ropeB"] = _decl(nc, ins, "ropeB", con["ropeB"])
    w["swap64"] = _decl(nc, ins, "swap64", con["swap64"])
    w["wqbd"] = _decl(nc, ins, "wqbd", _bd_chunks(weights["L0_attn_wq"]))
    w["wkbd"] = _decl(nc, ins, "wkbd", _bd_chunks(weights["L0_attn_wk"]))
    w["wvbd"] = _decl(nc, ins, "wvbd", _bd_chunks(weights["L0_attn_wv"]))
    for nm, key in (("wi", "L0_attn_wi"), ("bi", "L0_attn_bi"), ("wf", "L0_attn_wf"),
                    ("bf", "L0_attn_bf"), ("ln_w", "L0_attn_ln_w"),
                    ("Wg", "L0_ffn_Wg"), ("bg", "L0_ffn_bg"), ("W1", "L0_ffn_W1"),
                    ("b1", "L0_ffn_b1"), ("W2", "L0_ffn_W2"), ("b2", "L0_ffn_b2")):
        w[nm] = _decl(nc, ins, nm, weights[key])
    for nm in ("w_down_q", "w_down_kv", "w_up_q", "w_up_k", "w_up_v", "w_out",
               "ln_scale", "ln_bias"):
        w[nm] = _decl(nc, ins, nm, weights["L1_attn_" + nm])
    for nm in ("w1_w", "w1_s", "w2_w", "w2_s", "h1_w", "h1_s", "h2_w", "h2_s",
               "w3_w", "w3_s"):
        w[nm] = _decl(nc, ins, nm, weights["L1_ffn_" + nm])
    a1 = nc.dram_tensor("a1", [128, 8 * S], F32).ap()
    a2 = nc.dram_tensor("a2", [128, 8 * S], F32).ap()
    a3 = nc.dram_tensor("a3", [128, 8 * S], F32).ap()
    y = nc.dram_tensor("y", [S, D], F32, kind="ExternalOutput").ap()
    P = Prog(nc)
    b1 = Buf("a1"); b2 = Buf("a2"); b3 = Buf("a3")
    import contextlib as _ctx
    ps0 = _ctx.ExitStack()
    wt = make_prep_bufs(P, 1024, stack=ps0)
    pre0 = emit_l0ffn_prep(P, w, wt)
    build_l0attn(P, S, xin, w, a1, b1)
    pre1bf = emit_l1ffn_cast(P, w, wt)
    build_l0ffn(P, S, a1, w, a2, b2, prep=pre0)
    ps0.close()
    ps1 = _ctx.ExitStack()
    tt = {"t": P.sbuf("wprep_t", [128, 1 * 1024], BF16, stack=ps1), "nslot": 1}
    pre1 = emit_l1ffn_transpose(P, pre1bf, tt, ow=1024)
    build_l1attn(P, S, a2, w, a3, b3)
    ps1.close()
    build_l1ffn(P, S, a3, w, "tok", y, Buf("y"), prep=pre1)
    P.finish()
    return nc, ins


def kernel(**inputs):
    S = S_FULL
    x = np.asarray(inputs["L0_attn_x"], dtype=np.float32)
    con = _consts(S)
    xf = [np.ascontiguousarray(x[i].reshape(S, 8, 128).transpose(2, 1, 0).reshape(128, 8 * S))
          for i in range(NCORES)]
    nc, ins = _build_fused(S, con, inputs)
    maps = [dict(ins, xin=xf[i]) for i in range(NCORES)]
    r = run_bass_kernel_spmd(nc, maps, core_ids=list(range(NCORES)))
    return np.stack([r.results[i]["y"] for i in range(NCORES)], axis=0).astype(np.float32)


# version 28 (final)
# speedup vs baseline: 1.0274x; 1.0274x over previous
import contextlib
import numpy as np
import concourse.bass as bass
import concourse.mybir as mybir

F32 = mybir.dt.float32
BF16 = mybir.dt.bfloat16
AF = mybir.ActivationFunctionType
ALU = mybir.AluOpType
AX = mybir.AxisListType

ENGINES = ("pe", "act", "dve", "pool", "sp")
DMAQ = {"qs": "sp", "qa": "act", "qp": "pool"}
NLANES = 4
LANES = [f"{q}{l}" for q in DMAQ for l in range(NLANES)]


class Buf:
    __slots__ = ("name", "w", "r")

    def __init__(self, name):
        self.name = name
        self.w = {}
        self.r = {}


class TB:
    __slots__ = ("t", "buf")

    def __init__(self, t, buf):
        self.t = t
        self.buf = buf

    def __getitem__(self, key):
        return self.t[key]


PREP_SEMS = ("pool",) + tuple(f"qp{l}" for l in range(NLANES))
QS_SEMS = tuple(f"qs{l}" for l in range(NLANES))
HARD_FLUSH_TAGS = set()


class Prog:
    def barrier_partial(self):
        self.barrier(exclude=PREP_SEMS)

    def barrier_no_qs(self):
        self.barrier(exclude=PREP_SEMS + QS_SEMS)

    def __init__(self, nc):
        self.nc = nc
        self.stack = contextlib.ExitStack()
        self.sems = {}
        for name in list(ENGINES) + LANES:
            self.sems[name] = self.stack.enter_context(nc.semaphore(f"sem_{name}"))
        self.counts = {name: 0 for name in list(ENGINES) + LANES}
        self.qrr = {q: 0 for q in DMAQ}
        self.last_wait = {}
        self.ops = {e: [] for e in ENGINES}
        self.n_ops = 0
        self.pb = [self.sbuf_like_psum(i) for i in range(8)]

    def sbuf_like_psum(self, i):
        t = self.stack.enter_context(self.nc.psum_tensor(f"pb{i}", [128, 512], F32))
        return TB(t, Buf(f"pb{i}"))

    def sbuf(self, name, shape, dtype, stack=None, **kw):
        st = stack if stack is not None else self.stack
        self.uid = getattr(self, "uid", 0) + 1
        name = f"{name}_u{self.uid}"
        t = st.enter_context(self.nc.sbuf_tensor(name, list(shape), dtype, **kw))
        return TB(t, Buf(name))

    def dram(self, name, shape, dtype, kind="Internal"):
        t = self.nc.dram_tensor(name, list(shape), dtype, kind=kind)
        return TB(t.ap(), Buf(name))

    def _wait_list(self, eng_stream, own_sem, reads, writes):
        need = {}
        def add(s, v):
            if need.get(s, 0) < v:
                need[s] = v
        for b in reads:
            for s, v in b.w.items():
                add(s, v)
        for b in writes:
            for s, v in b.w.items():
                if s != own_sem or s in ENGINES:
                    add(s, v)
            for s, v in b.r.items():
                add(s, v)
        waits = []
        for s, v in need.items():
            if s == "pe" and eng_stream == "pe":
                continue
            assert v <= self.counts[s] * (16 if s not in ENGINES else 1), \
                f"wait on future token {s}={v}"
            key = (eng_stream, s)
            if self.last_wait.get(key, 0) >= v:
                continue
            self.last_wait[key] = v
            waits.append((self.sems[s], v))
        return waits

    def emit(self, eng, fn, reads=(), writes=(), inc=1):
        assert eng in ENGINES
        waits = self._wait_list(eng, eng, reads, writes)
        self.counts[eng] += inc
        tok = (eng, self.counts[eng])
        self.ops[eng].append((waits, fn, self.sems[eng], inc))
        self.n_ops += 1
        for b in reads:
            b.r[tok[0]] = tok[1]
        for b in writes:
            b.w = {tok[0]: tok[1]}
            b.r = {}
        return tok

    def dma(self, q, fn, reads=(), writes=()):
        eng = DMAQ[q]
        lane = f"{q}{self.qrr[q]}"
        self.qrr[q] = (self.qrr[q] + 1) % NLANES
        waits = self._wait_list(eng, lane, reads, writes)
        prev = self.counts[lane] * 16
        key = (eng, lane)
        if prev > 0 and self.last_wait.get(key, 0) < prev:
            self.last_wait[key] = prev
            waits.append((self.sems[lane], prev))
        self.counts[lane] += 1
        tok = (lane, self.counts[lane] * 16)
        self.ops[eng].append((waits, fn, self.sems[lane], 16))
        self.n_ops += 1
        for b in reads:
            b.r[tok[0]] = tok[1]
        for b in writes:
            b.w = {tok[0]: tok[1]}
            b.r = {}
        return tok

    def barrier(self, exclude=()):
        snapshot = [(s, self.counts[s] * (16 if s not in ENGINES else 1))
                    for s in list(ENGINES) + LANES if s not in exclude]
        for e in ENGINES:
            if e in exclude:
                continue
            waits = []
            for s, v in snapshot:
                if v == 0:
                    continue
                key = (e, s)
                if self.last_wait.get(key, 0) >= v:
                    continue
                self.last_wait[key] = v
                waits.append((self.sems[s], v))
            if waits:
                self.ops[e].append((waits, None, None, 0))

    def flush(self, name=None, final=False):
        if not final and name not in HARD_FLUSH_TAGS:
            return
        ops = self.ops
        self.ops = {e: [] for e in ENGINES}
        live = [e for e in ENGINES if ops[e]]
        if not live:
            return

        def runner(oplist):
            def run(eng):
                for waits, fn, sem, amt in oplist:
                    for s, v in waits:
                        eng.wait_ge(s, v)
                    if fn is not None:
                        inst = fn(eng)
                        if sem is not None:
                            assert inst is not None, "op fn must return its last instruction"
                            inst.then_inc(sem, amt)
            return run

        with self.nc.Block(name, no_gpsimd_drain=True) as block:
            for e in live:
                getattr(block, {"pe": "tensor", "act": "scalar", "dve": "vector",
                                "pool": "gpsimd", "sp": "sync"}[e])(runner(ops[e]))

    def finish(self):
        self.flush(final=True)
        self.stack.close()


"""Block builders for the mk_seq_preln_d2 megakernel. Feature-major dataflow:
activations [128 ch-part, S tok] per 128-channel tile (tile c = channels c*128..c*128+127,
stored at cols [c*S, (c+1)*S)); matmul lhsT = W[in,out] tiles, rhs = actT,
out psum [out_ch, tok]."""
import contextlib
import numpy as np

D = 1024
NCT = D // 128
HID = 4096
NHT = HID // 128
LN_EPS = 1e-5



def emit_vecs_to_cols(P, vec_aps, dst, NT, ident, bank, stack, tag="", q="qs"):
    nvec = len(vec_aps)
    assert nvec <= 32
    rows = P.sbuf(f"vrows{tag}", [nvec, NT * 128], F32, stack=stack)
    for i, v in enumerate(vec_aps):
        P.dma(q, lambda e, i=i, v=v: e.dma_start(rows.t[i:i + 1, :], v.unsqueeze(0)),
              writes=[rows.buf])
    for t in range(NT):
        P.emit("pe", lambda e, t=t: e.transpose(
            bank.t[0:128, 0:nvec], rows.t[0:nvec, t * 128:(t + 1) * 128],
            ident.t[0:nvec, 0:nvec]),
            reads=[rows.buf, ident.buf], writes=[bank.buf])
        P.emit("dve", lambda e, t=t: e.tensor_copy(
            dst.t[:, t:t + (nvec - 1) * NT + 1:NT], bank.t[0:128, 0:nvec]),
            reads=[bank.buf], writes=[dst.buf])


def make_ln_aux(P, Tc, stack, banks):
    return {
        "rows": P.sbuf("lnrows", [1, 3 * Tc], F32, stack=stack),
        "castbuf": P.sbuf("lncast", [128, 2 * Tc], BF16, stack=stack),
        "sqbuf": P.sbuf("lnsq", [128, 2 * Tc], BF16, stack=stack),
        "tmpf": P.sbuf("lntmp", [128, Tc], F32, stack=stack),
        "banks": banks,
    }


def emit_ln_chunk(P, xc, lnc, Tc, aux, cmn, scale_bias=None):
    onesb, onesrow = cmn["onesb"], cmn["onesrow"]
    rows, castb, sqb = aux["rows"], aux["castbuf"], aux["sqbuf"]
    bm, bq, bmb, brb = aux["banks"]
    for c in range(NCT):
        sl = slice(c * Tc, (c + 1) * Tc)
        dsl = slice((c % 2) * Tc, (c % 2) * Tc + Tc)
        P.emit("act", lambda e, sl=sl, dsl=dsl: e.activation(castb.t[:, dsl], xc.t[:, sl], AF.Copy),
               reads=[xc.buf], writes=[castb.buf])
        P.emit("act", lambda e, sl=sl, dsl=dsl: e.activation(sqb.t[:, dsl], xc.t[:, sl], AF.Square),
               reads=[xc.buf], writes=[sqb.buf])
        P.emit("pe", lambda e, c=c, dsl=dsl: e.matmul(
            bm.t[0:1, 0:Tc], onesb.t[:], castb.t[:, dsl], start=(c == 0), stop=(c == NCT - 1)),
            reads=[castb.buf, onesb.buf], writes=[bm.buf])
        P.emit("pe", lambda e, c=c, dsl=dsl: e.matmul(
            bq.t[0:1, 0:Tc], onesb.t[:], sqb.t[:, dsl], start=(c == 0), stop=(c == NCT - 1)),
            reads=[sqb.buf, onesb.buf], writes=[bq.buf])
    mrow = rows.t[0:1, 0:Tc]
    qrow = rows.t[0:1, Tc:2 * Tc]
    vrow = rows.t[0:1, 2 * Tc:3 * Tc]
    P.emit("dve", lambda e: e.tensor_scalar_mul(mrow, bm.t[0:1, 0:Tc], 1.0 / D),
           reads=[bm.buf], writes=[rows.buf])
    P.emit("dve", lambda e: e.tensor_scalar_mul(qrow, bq.t[0:1, 0:Tc], 1.0 / D),
           reads=[bq.buf], writes=[rows.buf])
    P.emit("dve", lambda e: e.tensor_mul(vrow, mrow, mrow), reads=[rows.buf], writes=[rows.buf])
    P.emit("dve", lambda e: e.tensor_sub(vrow, qrow, vrow), reads=[rows.buf], writes=[rows.buf])
    P.emit("act", lambda e: e.activation(vrow, vrow, AF.Sqrt, bias=cmn["eps1"].t[0:1, 0:1]),
           reads=[rows.buf, cmn["eps1"].buf], writes=[rows.buf])
    P.emit("dve", lambda e: e.reciprocal(vrow, vrow), reads=[rows.buf], writes=[rows.buf])
    P.emit("pe", lambda e: e.matmul(bmb.t[0:128, 0:Tc], onesrow.t[:], mrow, start=True, stop=True),
           reads=[rows.buf, onesrow.buf], writes=[bmb.buf])
    P.emit("pe", lambda e: e.matmul(brb.t[0:128, 0:Tc], onesrow.t[:], vrow, start=True, stop=True),
           reads=[rows.buf, onesrow.buf], writes=[brb.buf])
    tmpf = aux["tmpf"]
    for c in range(NCT):
        sl = slice(c * Tc, (c + 1) * Tc)
        P.emit("dve", lambda e, sl=sl: e.tensor_sub(tmpf.t[:, 0:Tc], xc.t[:, sl], bmb.t[0:128, 0:Tc]),
               reads=[xc.buf, bmb.buf], writes=[tmpf.buf])
        P.emit("dve", lambda e, sl=sl: e.tensor_mul(lnc.t[:, sl], tmpf.t[:, 0:Tc], brb.t[0:128, 0:Tc]),
               reads=[tmpf.buf, brb.buf], writes=[lnc.buf])


def emit_weight_cast(P, src_ap, dst_tb, O, I, wt, qload="qs", ecast="act", qstore="qa", cw=None):
    cw = min(cw or I, I)
    wf, wb = wt["f"], wt["b"]
    fB = wt.setdefault("fb", [Buf("wcf0"), Buf("wcf1")])
    bB = wt.setdefault("bb", [Buf("wcb0"), Buf("wcb1")])
    k = 0
    for r in range(O // 128):
        for cc in range(I // cw):
            h = (k % 2) * cw
            fb, bb = fB[k % 2], bB[k % 2]
            k += 1
            P.dma(qload, lambda e, r=r, cc=cc, h=h: e.dma_start(
                wf.t[:, h:h + cw], src_ap[r * 128:(r + 1) * 128, cc * cw:(cc + 1) * cw]),
                writes=[fb])
            if ecast in ("pool", "dve"):
                P.emit(ecast, lambda e, h=h: e.tensor_copy(wb.t[:, h:h + cw], wf.t[:, h:h + cw]),
                       reads=[fb], writes=[bb])
            else:
                P.emit(ecast, lambda e, h=h: e.activation(wb.t[:, h:h + cw], wf.t[:, h:h + cw], AF.Copy),
                       reads=[fb], writes=[bb])
            P.dma(qstore, lambda e, r=r, cc=cc, h=h: e.dma_start(
                dst_tb.t[r * 128:(r + 1) * 128, cc * cw:(cc + 1) * cw], wb.t[:, h:h + cw]),
                reads=[bb], writes=[dst_tb.buf])


def emit_weight_transpose(P, src_tb, dst_tb, O, I, tt, qload="qs", qstore="qs", ow=2048):
    tB = tt.setdefault("bufs", [Buf("wtt0"), Buf("wtt1")])
    ttt = tt["t"]
    ow = min(ow, O)
    nslot = tt.get("nslot", 2)
    k = 0
    for ci in range(I // 128):
        for oc in range(O // ow):
            h = (k % nslot) * ow
            tb = tB[k % nslot]
            k += 1
            P.dma(qload, lambda e, ci=ci, oc=oc, h=h: e.dma_start(
                ttt.t[:, h:h + ow],
                src_tb.t[oc * ow:(oc + 1) * ow, ci * 128:(ci + 1) * 128], transpose=True),
                reads=[src_tb.buf], writes=[tb])
            P.dma(qstore, lambda e, ci=ci, oc=oc, h=h: e.dma_start(
                dst_tb.t[ci * 128:(ci + 1) * 128, oc * ow:(oc + 1) * ow], ttt.t[:, h:h + ow]),
                reads=[tb], writes=[dst_tb.buf])


def make_common(P, stack, ident_ap, q="qs"):
    cmn = {}
    cmn["ident"] = P.sbuf("identC", [128, 128], F32, stack=stack)
    P.dma(q, lambda e: e.dma_start(cmn["ident"].t[:], ident_ap), writes=[cmn["ident"].buf])
    cmn["onesb"] = P.sbuf("onesbC", [128, 1], BF16, stack=stack)
    cmn["onesrow"] = P.sbuf("onesrowC", [1, 128], F32, stack=stack)
    P.emit("dve", lambda e: e.memset(cmn["onesb"].t[:], 1.0), writes=[cmn["onesb"].buf])
    P.emit("dve", lambda e: e.memset(cmn["onesrow"].t[:], 1.0), writes=[cmn["onesrow"].buf])
    cmn["eps1"] = P.sbuf("eps1C", [1, 1], F32, stack=stack)
    P.emit("dve", lambda e: e.memset(cmn["eps1"].t[:], LN_EPS), writes=[cmn["eps1"].buf])
    return cmn



def make_prep_bufs(P, cw=1024, stack=None):
    return {"f": P.sbuf("wprep_f", [128, 2 * cw], F32, stack=stack),
            "b": P.sbuf("wprep_b", [128, 2 * cw], BF16, stack=stack)}


def emit_l1ffn_cast(P, w, wt, qload="qp", ecast="pool", qstore="qp", cw=1024):
    if wt is None:
        wt = make_prep_bufs(P, cw)
    bf = {}
    for nm, O, I in (("w1", HID, D), ("w2", HID, D), ("h1", HID, HID),
                     ("h2", HID, HID), ("w3", D, HID)):
        bf[nm] = P.dram(nm + "bf", [O, I], BF16)
        emit_weight_cast(P, w[nm + "_w"], bf[nm], O, I, wt, qload, ecast, qstore, cw)
    return bf


def emit_l1ffn_transpose(P, bf, tt, qload="qs", qstore="qs", ow=2048):
    if tt is None:
        tt = {"t": P.sbuf("wprep_t", [128, 2 * ow], BF16)}
    pre = {}
    for nm, O, I in (("w1", HID, D), ("w2", HID, D), ("h1", HID, HID),
                     ("h2", HID, HID), ("w3", D, HID)):
        pre[nm + "T"] = P.dram(nm + "T", [I, O], BF16)
        emit_weight_transpose(P, bf[nm], pre[nm + "T"], O, I, tt, qload, qstore, ow)
    return pre


def build_l1ffn(P, S, xin_ap, w, out_kind, out_ap, outbuf, prep=True):
    Tc = min(512, S)
    ntc = S // Tc
    WQ = 512

    if prep is True:
        pre = emit_l1ffn_cast(P, w, None)
        pre = emit_l1ffn_transpose(P, pre, None)
    else:
        pre = prep
    w1T, w2T, h1T, h2T, w3T = (pre[k] for k in ("w1T", "w2T", "h1T", "h2T", "w3T"))

    with contextlib.ExitStack() as ph:
        cmn = make_common(P, ph, w["ident"])
        scol = P.sbuf("scol", [128, 4 * NHT], F32, stack=ph)
        s3col = P.sbuf("s3col", [128, NCT], F32, stack=ph)
        with contextlib.ExitStack() as vph:
            emit_vecs_to_cols(P, [w["w1_s"], w["w2_s"], w["h1_s"], w["h2_s"]], scol, NHT,
                              cmn["ident"], P.pb[7], vph, tag="s4")
            emit_vecs_to_cols(P, [w["w3_s"]], s3col, NCT, cmn["ident"], P.pb[7], vph, tag="s3")
            P.barrier_partial()
            P.flush("l1ffn_scales")

        xc = P.sbuf("xc", [128, NCT * Tc], F32, stack=ph)
        lnc = P.sbuf("lnc", [128, NCT * Tc], BF16, stack=ph)
        av = P.sbuf("av", [128, NHT * Tc], BF16, stack=ph)
        bv = P.sbuf("bv", [128, NHT * Tc], BF16, stack=ph)
        WQA = 256
        wa = P.sbuf("wa", [128, 2 * NCT * WQA], BF16, stack=ph)
        wb = P.sbuf("wb", [128, 2 * NCT * WQA], BF16, stack=ph)
        wh = P.sbuf("wh", [128, 2 * NHT * WQ], BF16, stack=ph)
        t1f = P.sbuf("t1f", [128, Tc], F32, stack=ph)
        xo = P.sbuf("xo", [128, Tc], F32, stack=ph)
        ytile = P.sbuf("ytile", [128, 2 * 128], F32, stack=ph)
        aux = make_ln_aux(P, Tc, ph, (P.pb[0], P.pb[1], P.pb[2], P.pb[3]))
        waB = [Buf("waA"), Buf("waB")]
        wbB = [Buf("wbA"), Buf("wbB")]
        whB = [Buf("whA"), Buf("whB")]

        def win_load(q, srcT, dst_t, half_off, nin_t, wnd, bufW, wq=WQ):
            src = srcT.t.rearrange("(c p) o -> p c o", p=128)[:, :, wnd * wq:(wnd + 1) * wq]
            dst = dst_t[:, half_off:half_off + nin_t * wq].rearrange("p (c o) -> p c o", c=nin_t)
            P.dma(q, lambda e, s=src, d=dst: e.dma_start(d, s), reads=[srcT.buf], writes=[bufW])

        def hid_stage(inp, srcT, nin_t, nout_t, epilogue):
            for wnd in range(nout_t * 128 // WQ):
                half = (wnd % 2) * NHT * WQ
                bufW = whB[wnd % 2]
                win_load("qs", srcT, wh.t, half, nin_t, wnd, bufW)
                for ot in range(WQ // 128):
                    og = wnd * (WQ // 128) + ot
                    bk = P.pb[4 + (og % 2)]
                    def mm(e, half=half, ot=ot, bk=bk, nin_t=nin_t, inp=inp):
                        i = None
                        for ci in range(nin_t):
                            i = e.matmul(bk.t[:, 0:Tc],
                                         wh.t[:, half + ci * WQ + ot * 128: half + ci * WQ + ot * 128 + 128],
                                         inp.t[:, ci * Tc:(ci + 1) * Tc],
                                         start=(ci == 0), stop=(ci == nin_t - 1))
                        return i
                    P.emit("pe", mm, reads=[bufW, inp.buf], writes=[bk.buf])
                    epilogue(og, bk)

        for tci in range(ntc):
            t0 = tci * Tc
            for c in range(NCT):
                P.dma("qa", lambda e, c=c, t0=t0: e.dma_start(
                    xc.t[:, c * Tc:(c + 1) * Tc], xin_ap[:, c * S + t0: c * S + t0 + Tc]),
                    writes=[xc.buf])
            emit_ln_chunk(P, xc, lnc, Tc, aux, cmn)

            for wnd in range(HID // WQA):
                half = (wnd % 2) * NCT * WQA
                bufA, bufB = waB[wnd % 2], wbB[wnd % 2]
                win_load("qs", w1T, wa.t, half, NCT, wnd, bufA, WQA)
                win_load("qa", w2T, wb.t, half, NCT, wnd, bufB, WQA)
                for ho in range(WQA // 128):
                    hog = wnd * (WQA // 128) + ho
                    bA, bB = P.pb[4], P.pb[5]
                    def mmw(e, wtile, half, ho, bk):
                        i = None
                        for ci in range(NCT):
                            i = e.matmul(bk.t[:, 0:Tc],
                                         wtile[:, half + ci * WQA + ho * 128: half + ci * WQA + ho * 128 + 128],
                                         lnc.t[:, ci * Tc:(ci + 1) * Tc],
                                         start=(ci == 0), stop=(ci == NCT - 1))
                        return i
                    P.emit("pe", lambda e, half=half, ho=ho: mmw(e, wa.t, half, ho, bA),
                           reads=[bufA, lnc.buf], writes=[bA.buf])
                    P.emit("act", lambda e, hog=hog: e.activation(
                        t1f.t[:, 0:Tc], bA.t[:, 0:Tc], AF.Silu,
                        scale=scol.t[:, 0 * NHT + hog:0 * NHT + hog + 1]),
                        reads=[bA.buf, scol.buf], writes=[t1f.buf])
                    P.emit("pe", lambda e, half=half, ho=ho: mmw(e, wb.t, half, ho, bB),
                           reads=[bufB, lnc.buf], writes=[bB.buf])
                    P.emit("dve", lambda e, hog=hog: e.scalar_tensor_tensor(
                        av.t[:, hog * Tc:(hog + 1) * Tc], bB.t[:, 0:Tc],
                        scol.t[:, 1 * NHT + hog:1 * NHT + hog + 1], t1f.t[:, 0:Tc],
                        ALU.mult, ALU.mult),
                        reads=[bB.buf, scol.buf, t1f.buf], writes=[av.buf])

            def ep_gelu(dst, sidx):
                def ep(og, bk):
                    P.emit("act", lambda e, og=og, bk=bk: e.activation(
                        dst.t[:, og * Tc:(og + 1) * Tc], bk.t[:, 0:Tc], AF.Gelu,
                        scale=scol.t[:, sidx + og:sidx + og + 1]),
                        reads=[bk.buf, scol.buf], writes=[dst.buf])
                return ep

            hid_stage(av, h1T, NHT, NHT, ep_gelu(bv, 2 * NHT))
            hid_stage(bv, h2T, NHT, NHT, ep_gelu(av, 3 * NHT))

            def ep_out(og, bk):
                P.emit("dve", lambda e, og=og, bk=bk: e.scalar_tensor_tensor(
                    xo.t[:, 0:Tc], bk.t[:, 0:Tc], s3col.t[:, og:og + 1],
                    xc.t[:, og * Tc:(og + 1) * Tc], ALU.mult, ALU.add),
                    reads=[bk.buf, s3col.buf, xc.buf], writes=[xo.buf])
                if out_kind == "feat":
                    P.dma("qa", lambda e, og=og, t0=t0: e.dma_start(
                        out_ap[:, og * S + t0: og * S + t0 + Tc], xo.t[:, 0:Tc]),
                        reads=[xo.buf], writes=[outbuf])
                else:
                    for tb in range(Tc // 128):
                        yh = (tb % 2) * 128
                        P.emit("pe", lambda e, tb=tb: e.transpose(
                            P.pb[6].t[0:128, 0:128], xo.t[:, tb * 128:(tb + 1) * 128],
                            cmn["ident"].t[:]),
                            reads=[xo.buf, cmn["ident"].buf], writes=[P.pb[6].buf])
                        P.emit("dve", lambda e, yh=yh: e.tensor_copy(
                            ytile.t[:, yh:yh + 128], P.pb[6].t[0:128, 0:128]),
                            reads=[P.pb[6].buf], writes=[ytile.buf])
                        P.dma("qa", lambda e, og=og, tb=tb, yh=yh, t0=t0: e.dma_start(
                            out_ap[t0 + tb * 128: t0 + (tb + 1) * 128, og * 128:(og + 1) * 128],
                            ytile.t[:, yh:yh + 128]),
                            reads=[ytile.buf], writes=[outbuf])

            hid_stage(av, w3T, NHT, NCT, ep_out)
        P.barrier()
        P.flush("l1ffn_main")



def emit_l0ffn_prep(P, w, wt, qload="qp", ecast="pool", qstore="qp", cw=1024):
    NE = 8
    W1b = P.dram("W1b", [NE, D, HID], BF16)
    W2b = P.dram("W2b", [NE, HID, D], BF16)
    W1e = [TB(W1b.t[e], Buf(f"W1b{e}")) for e in range(NE)]
    W2e = [TB(W2b.t[e], Buf(f"W2b{e}")) for e in range(NE)]
    if wt is None:
        wt = make_prep_bufs(P, cw)
    for e in range(NE):
        emit_weight_cast(P, w["W1"][e], W1e[e], D, HID, wt, qload, ecast, qstore, cw)
        emit_weight_cast(P, w["W2"][e], W2e[e], HID, D, wt, qload, ecast, qstore, cw)
    return (W1e, W2e)


def build_l0ffn(P, S, xin_ap, w, out_ap, outbuf, prep=True):
    Tc = min(512, S)
    ntc = S // Tc
    NE = 8
    if prep is True:
        pre = emit_l0ffn_prep(P, w, None)
    else:
        pre = prep
    W1e, W2e = pre

    with contextlib.ExitStack() as ph:
        cmn = make_common(P, ph, w["ident"])
        b1col = P.sbuf("b1col", [128, NE * NHT], F32, stack=ph)
        with contextlib.ExitStack() as vph:
            emit_vecs_to_cols(P, [w["b1"][e] for e in range(NE)], b1col, NHT,
                              cmn["ident"], P.pb[7], vph, tag="b1")
            P.barrier_partial()
            P.flush("l0ffn_b1")
        b2sb = P.sbuf("b2sb", [NE, D], F32, stack=ph)
        P.dma("qs", lambda e: e.dma_start(b2sb.t[:], w["b2"]), writes=[b2sb.buf])
        bgcol = P.sbuf("bgcol", [NE, 1], F32, stack=ph)
        P.dma("qs", lambda e: e.dma_start(bgcol.t[:], w["bg"].unsqueeze(1)), writes=[bgcol.buf])
        wgsb = P.sbuf("wgsb", [128, NCT * NE], BF16, stack=ph)
        wgf = P.sbuf("wgf", [128, NCT * NE], F32, stack=ph)
        for c in range(NCT):
            P.dma("qs", lambda e, c=c: e.dma_start(
                wgf.t[:, c * NE:(c + 1) * NE], w["Wg"][c * 128:(c + 1) * 128, :]),
                writes=[wgf.buf])
        P.emit("act", lambda e: e.activation(wgsb.t[:], wgf.t[:], AF.Copy),
               reads=[wgf.buf], writes=[wgsb.buf])
        sel8 = P.sbuf("sel8sb", [NE, NE * 128], F32, stack=ph)
        P.dma("qs", lambda e: e.dma_start(sel8.t[:], w["sel8"]), writes=[sel8.buf])

        xc = P.sbuf("xc0", [128, NCT * Tc], F32, stack=ph)
        lnc = P.sbuf("lnc0", [128, NCT * Tc], BF16, stack=ph)
        hbuf = P.sbuf("hbuf", [128, NHT * Tc], BF16, stack=ph)
        acc = P.sbuf("acc0", [128, NCT * Tc], F32, stack=ph)
        w1win = P.sbuf("w1win", [128, 2 * NCT * 512], BF16, stack=ph)
        w2win = P.sbuf("w2win", [128, 2 * NHT * 256], BF16, stack=ph)
        lgsb = P.sbuf("lgsb", [NE, Tc], F32, stack=ph)
        ohw = P.sbuf("ohw", [128, 33], F32, stack=ph)
        elbuf = P.sbuf("elbuf", [128, NE], F32, stack=ph)
        rmaxn = P.sbuf("rmaxn", [128, 1], F32, stack=ph)
        maskT = P.sbuf("maskT", [NE, Tc], F32, stack=ph)
        wrow = P.sbuf("wrow", [1, Tc], F32, stack=ph)
        t1f = P.sbuf("t1f0", [128, 2 * Tc], F32, stack=ph)
        tmpo = P.sbuf("tmpo", [128, 2 * Tc], F32, stack=ph)
        aux = make_ln_aux(P, Tc, ph, (P.pb[0], P.pb[1], P.pb[2], P.pb[3]))
        P.emit("dve", lambda e: e.memset(ohw.t[:], 0.0), writes=[ohw.buf])
        w1B = [Buf("w1A"), Buf("w1B")]
        w2B = [Buf("w2A"), Buf("w2B")]
        t1B = [Buf("t1A"), Buf("t1B")]
        tmB = [Buf("tmA"), Buf("tmB")]

        for tci in range(ntc):
            t0 = tci * Tc
            for c in range(NCT):
                P.dma("qs", lambda e, c=c, t0=t0: e.dma_start(
                    xc.t[:, c * Tc:(c + 1) * Tc], xin_ap[:, c * S + t0: c * S + t0 + Tc]),
                    writes=[xc.buf])
            emit_ln_chunk(P, xc, lnc, Tc, aux, cmn)

            def gmm(e):
                i = None
                for c in range(NCT):
                    i = e.matmul(P.pb[4].t[0:NE, 0:Tc], wgsb.t[:, c * NE:(c + 1) * NE],
                                 lnc.t[:, c * Tc:(c + 1) * Tc], start=(c == 0), stop=(c == NCT - 1))
                return i
            P.emit("pe", gmm, reads=[wgsb.buf, lnc.buf], writes=[P.pb[4].buf])
            P.emit("dve", lambda e: e.tensor_scalar_add(lgsb.t[:], P.pb[4].t[0:NE, 0:Tc],
                                                        bgcol.t[:]),
                   reads=[P.pb[4].buf, bgcol.buf], writes=[lgsb.buf])
            for tt in range(Tc // 128):
                P.emit("pe", lambda e, tt=tt: e.transpose(
                    P.pb[5].t[0:128, 0:NE], lgsb.t[:, tt * 128:(tt + 1) * 128],
                    cmn["ident"].t[0:NE, 0:NE]),
                    reads=[lgsb.buf, cmn["ident"].buf], writes=[P.pb[5].buf])
                P.emit("dve", lambda e: e.tensor_reduce(
                    rmaxn.t[:], P.pb[5].t[0:128, 0:NE], AX.X, ALU.max, negate=True),
                    reads=[P.pb[5].buf], writes=[rmaxn.buf])
                P.emit("act", lambda e: e.activation(elbuf.t[:], P.pb[5].t[0:128, 0:NE],
                                                     AF.Exp, bias=rmaxn.t[:]),
                       reads=[P.pb[5].buf, rmaxn.buf], writes=[elbuf.buf])
                P.emit("dve", lambda e: e.tensor_scalar(
                    ohw.t[:, 0:NE], P.pb[5].t[0:128, 0:NE], rmaxn.t[:], 0.0,
                    ALU.add, ALU.is_ge),
                    reads=[P.pb[5].buf, rmaxn.buf], writes=[ohw.buf])
                P.emit("dve", lambda e: e.tensor_reduce(
                    ohw.t[:, 32:33], elbuf.t[:], AX.X, ALU.add),
                    reads=[elbuf.buf], writes=[ohw.buf])
                P.emit("dve", lambda e: e.reciprocal(ohw.t[:, 32:33], ohw.t[:, 32:33]),
                       reads=[ohw.buf], writes=[ohw.buf])
                P.emit("pe", lambda e: e.transpose(
                    P.pb[6].t[0:33, 0:128], ohw.t[:], cmn["ident"].t[:]),
                    reads=[ohw.buf, cmn["ident"].buf], writes=[P.pb[6].buf])
                P.emit("dve", lambda e, tt=tt: e.tensor_copy(
                    maskT.t[:, tt * 128:(tt + 1) * 128], P.pb[6].t[0:NE, 0:128]),
                    reads=[P.pb[6].buf], writes=[maskT.buf])
                P.emit("dve", lambda e, tt=tt: e.tensor_copy(
                    wrow.t[0:1, tt * 128:(tt + 1) * 128], P.pb[6].t[32:33, 0:128]),
                    reads=[P.pb[6].buf], writes=[wrow.buf])

            for e_i in range(NE):
                P.emit("pe", lambda e, e_i=e_i: e.matmul(
                    P.pb[2].t[:, 0:Tc], sel8.t[:, e_i * 128:(e_i + 1) * 128],
                    maskT.t[0:NE, 0:Tc], start=True, stop=True),
                    reads=[sel8.buf, maskT.buf], writes=[P.pb[2].buf])
                for hw in range(HID // 512):
                    half = (hw % 2) * NCT * 512
                    bw = w1B[hw % 2]
                    src = W1e[e_i].t.rearrange("(c p) o -> p c o", p=128)[:, :, hw * 512:(hw + 1) * 512]
                    dst = w1win.t[:, half:half + NCT * 512].rearrange("p (c o) -> p c o", c=NCT)
                    P.dma("qs", lambda e, s=src, d=dst: e.dma_start(d, s),
                          reads=[W1e[e_i].buf], writes=[bw])
                    for ho in range(4):
                        hog = hw * 4 + ho
                        bk = P.pb[4 + (ho % 2)]
                        def mm1(e, half=half, ho=ho, bk=bk):
                            i = None
                            for ci in range(NCT):
                                i = e.matmul(bk.t[:, 0:Tc],
                                             w1win.t[:, half + ci * 512 + ho * 128: half + ci * 512 + ho * 128 + 128],
                                             lnc.t[:, ci * Tc:(ci + 1) * Tc],
                                             start=(ci == 0), stop=(ci == NCT - 1))
                            return i
                        P.emit("pe", mm1, reads=[bw, lnc.buf], writes=[bk.buf])
                        tof = (hog % 2) * Tc
                        tb1 = t1B[hog % 2]
                        P.emit("act", lambda e, e_i=e_i, hog=hog, bk=bk, tof=tof: e.activation(
                            t1f.t[:, tof:tof + Tc], bk.t[:, 0:Tc], AF.Silu,
                            bias=b1col.t[:, e_i * NHT + hog:e_i * NHT + hog + 1]),
                            reads=[bk.buf, b1col.buf], writes=[tb1])
                        P.emit("dve", lambda e, hog=hog, tof=tof: e.tensor_mul(
                            hbuf.t[:, hog * Tc:(hog + 1) * Tc], t1f.t[:, tof:tof + Tc], P.pb[2].t[:, 0:Tc]),
                            reads=[tb1, P.pb[2].buf], writes=[hbuf.buf])
                for ow in range(D // 256):
                    w2h = (ow % 2) * NHT * 256
                    w2b_ = w2B[ow % 2]
                    src = W2e[e_i].t.rearrange("(c p) o -> p c o", p=128)[:, :, ow * 256:(ow + 1) * 256]
                    dst = w2win.t[:, w2h:w2h + NHT * 256].rearrange("p (c o) -> p c o", c=NHT)
                    P.dma("qa", lambda e, s=src, d=dst: e.dma_start(d, s),
                          reads=[W2e[e_i].buf], writes=[w2b_])
                    for oc in range(2):
                        og = ow * 2 + oc
                        bk = P.pb[4 + (oc % 2)]
                        def mm2(e, oc=oc, bk=bk, w2h=w2h):
                            i = None
                            for ci in range(NHT):
                                i = e.matmul(bk.t[:, 0:Tc],
                                             w2win.t[:, w2h + ci * 256 + oc * 128: w2h + ci * 256 + oc * 128 + 128],
                                             hbuf.t[:, ci * Tc:(ci + 1) * Tc],
                                             start=(ci == 0), stop=(ci == NHT - 1))
                            return i
                        P.emit("pe", mm2, reads=[w2b_, hbuf.buf], writes=[bk.buf])
                        if e_i == 0:
                            P.emit("dve", lambda e, og=og, bk=bk: e.tensor_copy(
                                acc.t[:, og * Tc:(og + 1) * Tc], bk.t[:, 0:Tc]),
                                reads=[bk.buf], writes=[acc.buf])
                        else:
                            P.emit("dve", lambda e, og=og, bk=bk: e.tensor_add(
                                acc.t[:, og * Tc:(og + 1) * Tc], acc.t[:, og * Tc:(og + 1) * Tc],
                                bk.t[:, 0:Tc]),
                                reads=[bk.buf, acc.buf], writes=[acc.buf])

            P.emit("pe", lambda e: e.matmul(P.pb[3].t[:, 0:Tc], cmn["onesrow"].t[:],
                                            wrow.t[0:1, 0:Tc], start=True, stop=True),
                   reads=[cmn["onesrow"].buf, wrow.buf], writes=[P.pb[3].buf])
            for og in range(NCT):
                bk = P.pb[4 + (og % 2)]
                tof = (og % 2) * Tc
                tb = tmB[og % 2]
                P.emit("pe", lambda e, og=og, bk=bk: e.matmul(
                    bk.t[:, 0:Tc], b2sb.t[:, og * 128:(og + 1) * 128], maskT.t[0:NE, 0:Tc],
                    start=True, stop=True),
                    reads=[b2sb.buf, maskT.buf], writes=[bk.buf])
                P.emit("dve", lambda e, og=og, bk=bk, tof=tof: e.tensor_add(
                    tmpo.t[:, tof:tof + Tc], acc.t[:, og * Tc:(og + 1) * Tc], bk.t[:, 0:Tc]),
                    reads=[acc.buf, bk.buf], writes=[tb])
                P.emit("dve", lambda e, tof=tof: e.tensor_mul(tmpo.t[:, tof:tof + Tc], tmpo.t[:, tof:tof + Tc],
                                                     P.pb[3].t[:, 0:Tc]),
                       reads=[tb, P.pb[3].buf], writes=[tb])
                P.emit("dve", lambda e, og=og, tof=tof: e.tensor_add(
                    tmpo.t[:, tof:tof + Tc], tmpo.t[:, tof:tof + Tc], xc.t[:, og * Tc:(og + 1) * Tc]),
                    reads=[tb, xc.buf], writes=[tb])
                P.dma("qa", lambda e, og=og, t0=t0, tof=tof: e.dma_start(
                    out_ap[:, og * S + t0: og * S + t0 + Tc], tmpo.t[:, tof:tof + Tc]),
                    reads=[tb], writes=[outbuf])
        P.barrier()
        P.flush("l0ffn_main")



def emit_pe_weight_load_T(P, src_ap, O, I, dst, cmn, wf, stack, tag="", qload="qs"):
    nO, nI = O // 128, I // 128
    for r in range(nO):
        P.dma(qload, lambda e, r=r: e.dma_start(wf.t[:, (r % 2) * I:(r % 2) * I + I],
                                                src_ap[r * 128:(r + 1) * 128, :]),
              writes=[wf.buf])
        for ci in range(nI):
            P.emit("pe", lambda e, r=r, ci=ci: e.transpose(
                P.pb[7].t[0:128, 0:128],
                wf.t[:, (r % 2) * I + ci * 128:(r % 2) * I + ci * 128 + 128],
                cmn["ident"].t[:]),
                reads=[wf.buf, cmn["ident"].buf], writes=[P.pb[7].buf])
            P.emit("act", lambda e, r=r, ci=ci: e.activation(
                dst.t[:, ci * O + r * 128: ci * O + r * 128 + 128],
                P.pb[7].t[0:128, 0:128], AF.Copy),
                reads=[P.pb[7].buf], writes=[dst.buf])


def build_l1attn(P, S, xin_ap, w, out_ap, outbuf):
    H, DH = 16, 64
    DQL = 512
    NQT = DQL // 128
    lnp_d = P.dram("lnp", [128, NCT * S], F32)

    with contextlib.ExitStack() as ph:
        cmn = make_common(P, ph, w["ident"], q="qa")
        attnT = P.sbuf("attnT", [128, NCT * S], BF16, stack=ph)

        with contextlib.ExitStack() as bc:
            cqT = P.sbuf("cqT", [128, NQT * S], BF16, stack=bc)
            ckvT = P.sbuf("ckvT", [128, NQT * S], BF16, stack=bc)

            with contextlib.ExitStack() as pa:
                wdqT = P.sbuf("wdqT", [128, NCT * DQL], BF16, stack=pa)
                wdkT = P.sbuf("wdkT", [128, NCT * DQL], BF16, stack=pa)
                with contextlib.ExitStack() as pwa:
                    wfA = P.sbuf("wfA", [128, 2 * 1024], F32, stack=pwa)
                    emit_pe_weight_load_T(P, w["w_down_q"], DQL, D, wdqT, cmn, wfA, pwa, qload="qa")
                    emit_pe_weight_load_T(P, w["w_down_kv"], DQL, D, wdkT, cmn, wfA, pwa, qload="qa")
                    P.barrier_no_qs()
                    P.flush("l1a_wd")
                sbcol = P.sbuf("sbcol", [128, 2 * NCT], F32, stack=pa)
                with contextlib.ExitStack() as vph:
                    emit_vecs_to_cols(P, [w["ln_scale"], w["ln_bias"]], sbcol, NCT,
                                      cmn["ident"], P.pb[7], vph, tag="sb", q="qa")
                    P.barrier_no_qs()
                    P.flush("l1a_sb")
                Tc = min(512, S)
                ntc = S // Tc
                xc = P.sbuf("xcA", [128, NCT * Tc], F32, stack=pa)
                lnc = P.sbuf("lncA", [128, NCT * Tc], BF16, stack=pa)
                lnpc = P.sbuf("lnpcA", [128, Tc], F32, stack=pa)
                aux = make_ln_aux(P, Tc, pa, (P.pb[0], P.pb[1], P.pb[2], P.pb[3]))
                rows2, cb2, sq2 = aux["rows"], aux["castbuf"], aux["sqbuf"]
                lnpf = P.sbuf("lnpf", [128, NCT * Tc], F32, stack=pa)

                for tci in range(ntc):
                    t0 = tci * Tc
                    for c in range(NCT):
                        P.dma("qa", lambda e, c=c, t0=t0: e.dma_start(
                            xc.t[:, c * Tc:(c + 1) * Tc], xin_ap[:, c * S + t0: c * S + t0 + Tc]),
                            writes=[xc.buf])
                    onesb, onesrow = cmn["onesb"], cmn["onesrow"]
                    rows, castb, sqb = aux["rows"], aux["castbuf"], aux["sqbuf"]
                    bm, bq, bmb, brb = aux["banks"]
                    for c in range(NCT):
                        sl = slice(c * Tc, (c + 1) * Tc)
                        dsl = slice((c % 2) * Tc, (c % 2) * Tc + Tc)
                        P.emit("act", lambda e, sl=sl, dsl=dsl: e.activation(castb.t[:, dsl], xc.t[:, sl], AF.Copy),
                               reads=[xc.buf], writes=[castb.buf])
                        P.emit("act", lambda e, sl=sl, dsl=dsl: e.activation(sqb.t[:, dsl], xc.t[:, sl], AF.Square),
                               reads=[xc.buf], writes=[sqb.buf])
                        P.emit("pe", lambda e, c=c, dsl=dsl: e.matmul(
                            bm.t[0:1, 0:Tc], onesb.t[:], castb.t[:, dsl], start=(c == 0), stop=(c == NCT - 1)),
                            reads=[castb.buf, onesb.buf], writes=[bm.buf])
                        P.emit("pe", lambda e, c=c, dsl=dsl: e.matmul(
                            bq.t[0:1, 0:Tc], onesb.t[:], sqb.t[:, dsl], start=(c == 0), stop=(c == NCT - 1)),
                            reads=[sqb.buf, onesb.buf], writes=[bq.buf])
                    mrow = rows.t[0:1, 0:Tc]; qrow = rows.t[0:1, Tc:2 * Tc]; vrow = rows.t[0:1, 2 * Tc:3 * Tc]
                    P.emit("dve", lambda e: e.tensor_scalar_mul(mrow, bm.t[0:1, 0:Tc], 1.0 / D),
                           reads=[bm.buf], writes=[rows.buf])
                    P.emit("dve", lambda e: e.tensor_scalar_mul(qrow, bq.t[0:1, 0:Tc], 1.0 / D),
                           reads=[bq.buf], writes=[rows.buf])
                    P.emit("dve", lambda e: e.tensor_mul(vrow, mrow, mrow), reads=[rows.buf], writes=[rows.buf])
                    P.emit("dve", lambda e: e.tensor_sub(vrow, qrow, vrow), reads=[rows.buf], writes=[rows.buf])
                    P.emit("act", lambda e: e.activation(vrow, vrow, AF.Sqrt, bias=cmn["eps1"].t[0:1, 0:1]),
                           reads=[rows.buf, cmn["eps1"].buf], writes=[rows.buf])
                    P.emit("dve", lambda e: e.reciprocal(vrow, vrow), reads=[rows.buf], writes=[rows.buf])
                    P.emit("pe", lambda e: e.matmul(bmb.t[0:128, 0:Tc], onesrow.t[:], mrow, start=True, stop=True),
                           reads=[rows.buf, onesrow.buf], writes=[bmb.buf])
                    P.emit("pe", lambda e: e.matmul(brb.t[0:128, 0:Tc], onesrow.t[:], vrow, start=True, stop=True),
                           reads=[rows.buf, onesrow.buf], writes=[brb.buf])
                    for c in range(NCT):
                        sl = slice(c * Tc, (c + 1) * Tc)
                        P.emit("dve", lambda e, sl=sl: e.tensor_sub(lnpc.t[:, 0:Tc], xc.t[:, sl], bmb.t[0:128, 0:Tc]),
                               reads=[xc.buf, bmb.buf], writes=[lnpc.buf])
                        P.emit("dve", lambda e, sl=sl: e.tensor_mul(lnpf.t[:, sl], lnpc.t[:, 0:Tc], brb.t[0:128, 0:Tc]),
                               reads=[lnpc.buf, brb.buf], writes=[lnpf.buf])
                        P.dma("qa", lambda e, c=c, t0=t0, sl=sl: e.dma_start(
                            lnp_d.t[:, c * S + t0: c * S + t0 + Tc], lnpf.t[:, sl]),
                            reads=[lnpf.buf], writes=[lnp_d.buf])
                    for c in range(NCT):
                        sl = slice(c * Tc, (c + 1) * Tc)
                        dsl = slice((c % 2) * Tc, (c % 2) * Tc + Tc)
                        P.emit("act", lambda e, sl=sl, dsl=dsl: e.activation(cb2.t[:, dsl], lnpf.t[:, sl], AF.Copy),
                               reads=[lnpf.buf], writes=[cb2.buf])
                        P.emit("act", lambda e, sl=sl, dsl=dsl: e.activation(sq2.t[:, dsl], lnpf.t[:, sl], AF.Square),
                               reads=[lnpf.buf], writes=[sq2.buf])
                        P.emit("pe", lambda e, c=c, dsl=dsl: e.matmul(
                            bm.t[0:1, 0:Tc], onesb.t[:], cb2.t[:, dsl], start=(c == 0), stop=(c == NCT - 1)),
                            reads=[cb2.buf, onesb.buf], writes=[bm.buf])
                        P.emit("pe", lambda e, c=c, dsl=dsl: e.matmul(
                            bq.t[0:1, 0:Tc], onesb.t[:], sq2.t[:, dsl], start=(c == 0), stop=(c == NCT - 1)),
                            reads=[sq2.buf, onesb.buf], writes=[bq.buf])
                    mrow2 = rows2.t[0:1, 0:Tc]; qrow2 = rows2.t[0:1, Tc:2 * Tc]; vrow2 = rows2.t[0:1, 2 * Tc:3 * Tc]
                    P.emit("dve", lambda e: e.tensor_scalar_mul(mrow2, bm.t[0:1, 0:Tc], 1.0 / D),
                           reads=[bm.buf], writes=[rows2.buf])
                    P.emit("dve", lambda e: e.tensor_scalar_mul(qrow2, bq.t[0:1, 0:Tc], 1.0 / D),
                           reads=[bq.buf], writes=[rows2.buf])
                    P.emit("dve", lambda e: e.tensor_mul(vrow2, mrow2, mrow2), reads=[rows2.buf], writes=[rows2.buf])
                    P.emit("dve", lambda e: e.tensor_sub(vrow2, qrow2, vrow2), reads=[rows2.buf], writes=[rows2.buf])
                    P.emit("act", lambda e: e.activation(vrow2, vrow2, AF.Sqrt, bias=cmn["eps1"].t[0:1, 0:1]),
                           reads=[rows2.buf, cmn["eps1"].buf], writes=[rows2.buf])
                    P.emit("dve", lambda e: e.reciprocal(vrow2, vrow2), reads=[rows2.buf], writes=[rows2.buf])
                    P.emit("pe", lambda e: e.matmul(bmb.t[0:128, 0:Tc], onesrow.t[:], mrow2, start=True, stop=True),
                           reads=[rows2.buf, onesrow.buf], writes=[bmb.buf])
                    P.emit("pe", lambda e: e.matmul(brb.t[0:128, 0:Tc], onesrow.t[:], vrow2, start=True, stop=True),
                           reads=[rows2.buf, onesrow.buf], writes=[brb.buf])
                    tmpf = aux["tmpf"]
                    for c in range(NCT):
                        sl = slice(c * Tc, (c + 1) * Tc)
                        P.emit("dve", lambda e, sl=sl: e.tensor_sub(tmpf.t[:, 0:Tc], lnpf.t[:, sl], bmb.t[0:128, 0:Tc]),
                               reads=[lnpf.buf, bmb.buf], writes=[tmpf.buf])
                        P.emit("dve", lambda e, sl=sl: e.tensor_mul(tmpf.t[:, 0:Tc], tmpf.t[:, 0:Tc], brb.t[0:128, 0:Tc]),
                               reads=[tmpf.buf, brb.buf], writes=[tmpf.buf])
                        P.emit("dve", lambda e, c=c, sl=sl: e.tensor_scalar(
                            lnc.t[:, sl], tmpf.t[:, 0:Tc],
                            sbcol.t[:, 0 * NCT + c:0 * NCT + c + 1],
                            sbcol.t[:, 1 * NCT + c:1 * NCT + c + 1],
                            ALU.mult, ALU.add),
                            reads=[tmpf.buf, sbcol.buf], writes=[lnc.buf])
                    for proj, wT, dstT in (("q", wdqT, cqT), ("kv", wdkT, ckvT)):
                        for oc in range(NQT):
                            bk = P.pb[4 + (oc % 2)]
                            def mmd(e, wT=wT, oc=oc, bk=bk):
                                i = None
                                for ci in range(NCT):
                                    i = e.matmul(bk.t[:, 0:Tc],
                                                 wT.t[:, ci * DQL + oc * 128: ci * DQL + oc * 128 + 128],
                                                 lnc.t[:, ci * Tc:(ci + 1) * Tc],
                                                 start=(ci == 0), stop=(ci == NCT - 1))
                                return i
                            P.emit("pe", mmd, reads=[wT.buf, lnc.buf], writes=[bk.buf])
                            P.emit("act", lambda e, dstT=dstT, oc=oc, t0=t0, bk=bk: e.activation(
                                dstT.t[:, oc * S + t0: oc * S + t0 + Tc], bk.t[:, 0:Tc], AF.Copy),
                                reads=[bk.buf], writes=[dstT.buf])
                P.barrier_no_qs()
                P.flush("l1a_A")

            with contextlib.ExitStack() as pc:
                wuqT = P.sbuf("wuqT", [128, NQT * 1024], BF16, stack=pc)
                wukT = P.sbuf("wukT", [128, NQT * 1024], BF16, stack=pc)
                wuvT = P.sbuf("wuvT", [128, NQT * 1024], BF16, stack=pc)
                with contextlib.ExitStack() as pw:
                    wfC = P.sbuf("wfC", [128, 2 * 512], F32, stack=pw)
                    emit_pe_weight_load_T(P, w["w_up_q"], 1024, DQL, wuqT, cmn, wfC, pw, qload="qa")
                    emit_pe_weight_load_T(P, w["w_up_k"], 1024, DQL, wukT, cmn, wfC, pw, qload="qa")
                    emit_pe_weight_load_T(P, w["w_up_v"], 1024, DQL, wuvT, cmn, wfC, pw, qload="qa")
                    P.barrier_no_qs()
                    P.flush("l1a_wu")
                ropeA = P.sbuf("ropeAsb", [64, 512], F32, stack=pc)
                ropeB = P.sbuf("ropeBsb", [64, 512], F32, stack=pc)
                swap64 = P.sbuf("swap64sb", [64, 64], BF16, stack=pc)
                swf = P.sbuf("swf", [64, 64], F32, stack=pc)
                P.dma("qa", lambda e: e.dma_start(swf.t[:], w["swap64"]), writes=[swf.buf])
                P.emit("act", lambda e: e.activation(swap64.t[:], swf.t[:], AF.Copy),
                       reads=[swf.buf], writes=[swap64.buf])
                qTh = P.sbuf("qTh", [64, 2 * S], BF16, stack=pc)
                kTh = P.sbuf("kTh", [64, 2 * S], BF16, stack=pc)
                NT = S // 128
                nic = S // 512
                vON = P.sbuf("vON", [128, 2 * NT * 66], BF16, stack=pc)
                qbf = P.sbuf("qbf", [64, 512], BF16, stack=pc)
                rt1 = P.sbuf("rt1", [64, 512], F32, stack=pc)
                rt2 = rb_shared = None
                expT = P.sbuf("expT", [128, 3 * 512], BF16, stack=pc)
                rb = P.sbuf("rb", [64, 512], F32, stack=pc)
                rt2 = rb
                qB = [Buf("qThA"), Buf("qThB")]
                kB = [Buf("kThA"), Buf("kThB")]
                vB = [Buf("vONA"), Buf("vONB")]

                def emit_build(h):
                    sl = (h % 2) * S
                    vsl = (h % 2) * NT * 66
                    for icx in range(nic):
                        for which, wupT, dstT, dB in (("q", wuqT, qTh, qB), ("k", wukT, kTh, kB)):
                            i0 = icx * 512
                            if which == "q":
                                P.dma("qa", lambda e, i0=i0: e.dma_start(
                                    ropeA.t[:], w["ropeA"][:, i0:i0 + 512]),
                                    writes=[ropeA.buf])
                                P.dma("qa", lambda e, i0=i0: e.dma_start(
                                    ropeB.t[:], w["ropeB"][:, i0:i0 + 512]),
                                    writes=[ropeB.buf])
                            bk = P.pb[1]
                            def mmq(e, wupT=wupT, h=h, i0=i0, bk=bk):
                                i = None
                                for ci in range(NQT):
                                    i = e.matmul(bk.t[0:64, 0:512],
                                                 wupT.t[:, ci * 1024 + h * 64: ci * 1024 + h * 64 + 64],
                                                 (cqT if wupT is wuqT else ckvT).t[:, ci * S + i0: ci * S + i0 + 512],
                                                 start=(ci == 0), stop=(ci == NQT - 1))
                                return i
                            P.emit("pe", mmq, reads=[wupT.buf, (cqT if which == "q" else ckvT).buf],
                                   writes=[bk.buf])
                            sc = 1.0 if which == "q" else 0.125
                            P.emit("act", lambda e, bk=bk, sc=sc: e.activation(
                                qbf.t[:], bk.t[0:64, 0:512], AF.Copy, scale=sc),
                                reads=[bk.buf], writes=[qbf.buf])
                            P.emit("pe", lambda e: e.matmul(
                                P.pb[2].t[0:64, 0:512], swap64.t[:], qbf.t[:], start=True, stop=True),
                                reads=[swap64.buf, qbf.buf], writes=[P.pb[2].buf])
                            P.emit("dve", lambda e: e.tensor_mul(
                                rt1.t[:], qbf.t[:], ropeA.t[:]),
                                reads=[qbf.buf, ropeA.buf], writes=[rt1.buf])
                            P.emit("dve", lambda e: e.tensor_mul(
                                rt2.t[:], P.pb[2].t[0:64, 0:512], ropeB.t[:]),
                                reads=[P.pb[2].buf, ropeB.buf], writes=[rt2.buf])
                            P.emit("dve", lambda e, dstT=dstT, i0=i0, sl=sl, dB=dB, h=h: e.tensor_add(
                                dstT.t[:, sl + i0: sl + i0 + 512], rt1.t[:], rt2.t[:]),
                                reads=[rt1.buf, rt2.buf], writes=[dB[h % 2]])
                    P.emit("dve", lambda e, vsl=vsl: e.memset(vON.t[:, vsl:vsl + NT * 66], 1.0),
                           writes=[vB[h % 2]])
                    for tt in range(NT):
                        bk = P.pb[1]
                        def mmv(e, h=h, tt=tt, bk=bk):
                            i = None
                            for ci in range(NQT):
                                i = e.matmul(bk.t[:, 0:64],
                                             ckvT.t[:, ci * S + tt * 128: ci * S + tt * 128 + 128],
                                             wuvT.t[:, ci * 1024 + h * 64: ci * 1024 + h * 64 + 64],
                                             start=(ci == 0), stop=(ci == NQT - 1))
                            return i
                        P.emit("pe", mmv, reads=[ckvT.buf, wuvT.buf], writes=[bk.buf])
                        P.emit("act", lambda e, tt=tt, bk=bk, vsl=vsl, h=h: e.activation(
                            vON.t[:, vsl + tt * 66: vsl + tt * 66 + 64], bk.t[:, 0:64], AF.Copy),
                            reads=[bk.buf], writes=[vB[h % 2]])

                emit_build(0)
                for h in range(H):
                    c_att = h // 2
                    r_att = (h % 2) * 64
                    sl = (h % 2) * S
                    vsl = (h % 2) * NT * 66
                    expB = [Buf("expA"), Buf("expB"), Buf("expC")]
                    LA = 2
                    for icx in range(nic):
                        i0 = icx * 512
                        def em_s(J, i0=i0, sl=sl, h=h):
                            bs = P.pb[5 + (J % 3)]
                            P.emit("pe", lambda e, J=J, i0=i0, bs=bs, sl=sl: e.matmul(
                                bs.t[:, 0:512], kTh.t[:, sl + J * 128: sl + (J + 1) * 128],
                                qTh.t[:, sl + i0: sl + i0 + 512], start=True, stop=True),
                                reads=[kB[h % 2], qB[h % 2]], writes=[bs.buf])
                            eh = (J % 3) * 512
                            eb = expB[J % 3]
                            P.emit("act", lambda e, eh=eh, bs=bs: e.activation(
                                expT.t[:, eh:eh + 512], bs.t[:, 0:512], AF.Exp),
                                reads=[bs.buf], writes=[eb])
                        def em_av(J, vsl=vsl, h=h):
                            eh = (J % 3) * 512
                            eb = expB[J % 3]
                            P.emit("pe", lambda e, J=J, eh=eh, vsl=vsl: e.matmul(
                                P.pb[4].t[0:65, 0:512], vON.t[:, vsl + J * 66: vsl + J * 66 + 65],
                                expT.t[:, eh:eh + 512], start=(J == 0), stop=(J == NT - 1)),
                                reads=[vB[h % 2], eb], writes=[P.pb[4].buf])
                        for J in range(NT):
                            em_s(J)
                            if J >= LA:
                                em_av(J - LA)
                        for J in range(NT - LA, NT):
                            em_av(J)
                        P.emit("dve", lambda e: e.reciprocal(rb.t[0:1, 0:512], P.pb[4].t[64:65, 0:512]),
                               reads=[P.pb[4].buf], writes=[rb.buf])
                        P.emit("pe", lambda e: e.matmul(
                            P.pb[3].t[0:64, 0:512], cmn["onesrow"].t[0:1, 0:64], rb.t[0:1, 0:512],
                            start=True, stop=True),
                            reads=[cmn["onesrow"].buf, rb.buf], writes=[P.pb[3].buf])
                        P.emit("act", lambda e: e.activation(rb.t[:], P.pb[3].t[0:64, 0:512], AF.Copy),
                               reads=[P.pb[3].buf], writes=[rb.buf])
                        P.emit("dve", lambda e, c_att=c_att, r_att=r_att, i0=i0: e.tensor_mul(
                            attnT.t[r_att:r_att + 64, c_att * S + i0: c_att * S + i0 + 512],
                            P.pb[4].t[0:64, 0:512], rb.t[:]),
                            reads=[P.pb[4].buf, rb.buf], writes=[attnT.buf])
                        if icx == 0 and h + 1 < H:
                            emit_build(h + 1)
                P.barrier_no_qs()
                P.flush("l1a_C")

        with contextlib.ExitStack() as pd:
            woutT = P.sbuf("woutT", [128, NCT * 1024], BF16, stack=pd)
            wfD = P.sbuf("wfD", [128, 2 * 1024], F32, stack=pd)
            emit_pe_weight_load_T(P, w["w_out"], 1024, 1024, woutT, cmn, wfD, pd, qload="qa")
            Tc = min(512, S)
            ntc = S // Tc
            xs = P.sbuf("xsD", [128, Tc], F32, stack=pd)
            ls = P.sbuf("lsD", [128, Tc], F32, stack=pd)
            oo = P.sbuf("ooD", [128, Tc], F32, stack=pd)
            for tci in range(ntc):
                t0 = tci * Tc
                for oc in range(NCT):
                    bk = P.pb[4 + (oc % 2)]
                    def mmo(e, oc=oc, t0=t0, bk=bk):
                        i = None
                        for ci in range(NCT):
                            i = e.matmul(bk.t[:, 0:Tc],
                                         woutT.t[:, ci * 1024 + oc * 128: ci * 1024 + oc * 128 + 128],
                                         attnT.t[:, ci * S + t0: ci * S + t0 + Tc],
                                         start=(ci == 0), stop=(ci == NCT - 1))
                        return i
                    P.emit("pe", mmo, reads=[woutT.buf, attnT.buf], writes=[bk.buf])
                    P.dma("qa", lambda e, oc=oc, t0=t0: e.dma_start(
                        xs.t[:, 0:Tc], xin_ap[:, oc * S + t0: oc * S + t0 + Tc]),
                        writes=[xs.buf])
                    P.dma("qa", lambda e, oc=oc, t0=t0: e.dma_start(
                        ls.t[:, 0:Tc], lnp_d.t[:, oc * S + t0: oc * S + t0 + Tc]),
                        reads=[lnp_d.buf], writes=[ls.buf])
                    P.emit("dve", lambda e, bk=bk: e.tensor_add(oo.t[:, 0:Tc], bk.t[:, 0:Tc], ls.t[:, 0:Tc]),
                           reads=[bk.buf, ls.buf], writes=[oo.buf])
                    P.emit("dve", lambda e: e.tensor_add(oo.t[:, 0:Tc], oo.t[:, 0:Tc], xs.t[:, 0:Tc]),
                           reads=[oo.buf, xs.buf], writes=[oo.buf])
                    P.dma("qa", lambda e, oc=oc, t0=t0: e.dma_start(
                        out_ap[:, oc * S + t0: oc * S + t0 + Tc], oo.t[:, 0:Tc]),
                        reads=[oo.buf], writes=[outbuf])
            P.barrier()
            P.flush("l1a_D")



def build_l0attn(P, S, xin_ap, w, out_ap, outbuf):
    NH, DHh = 4, 256
    NT = S // 128
    nic = S // 512
    EPS0 = 1e-8
    ln0_d = P.dram("ln0d", [128, NCT * S], BF16)

    with contextlib.ExitStack() as ph:
        cmn = make_common(P, ph, w["ident"])

        with contextlib.ExitStack() as p0:
            Tc0 = min(512, S)
            xc0 = P.sbuf("xc00", [128, NCT * Tc0], F32, stack=p0)
            lnc0 = P.sbuf("lnc00", [128, NCT * Tc0], BF16, stack=p0)
            aux0 = make_ln_aux(P, Tc0, p0, (P.pb[0], P.pb[1], P.pb[2], P.pb[3]))
            for tci in range(S // Tc0):
                t0 = tci * Tc0
                for c in range(NCT):
                    P.dma("qs", lambda e, c=c, t0=t0: e.dma_start(
                        xc0.t[:, c * Tc0:(c + 1) * Tc0], xin_ap[:, c * S + t0: c * S + t0 + Tc0]),
                        writes=[xc0.buf])
                emit_ln_chunk(P, xc0, lnc0, Tc0, aux0, cmn)
                for c in range(NCT):
                    P.dma("qa", lambda e, c=c, t0=t0: e.dma_start(
                        ln0_d.t[:, c * S + t0: c * S + t0 + Tc0], lnc0.t[:, c * Tc0:(c + 1) * Tc0]),
                        reads=[lnc0.buf], writes=[ln0_d.buf])
            P.barrier_partial()
            P.flush("l0a_ln")

        m_pt_all = P.sbuf("m_pt_all", [128, 4 * (S // 128)], F32, stack=ph)
        cs_sb = P.sbuf("cs_sb", [4, S], F32, stack=ph)
        pm_sb = P.sbuf("pm_sb", [4, S], F32, stack=ph)
        with contextlib.ExitStack() as p1:
            ig_sb = P.sbuf("ig_sb", [4, S], F32, stack=p1)
            fg_sb = P.sbuf("fg_sb", [4, S], F32, stack=p1)
            wisum = P.sbuf("wisum", [4, 3 * 1024], F32, stack=p1)
            wfsum = P.sbuf("wfsum", [4, 3 * 1024], F32, stack=p1)
            P.dma("qs", lambda e: e.dma_start(wisum.t[:], w["wi"]), writes=[wisum.buf])
            P.dma("qs", lambda e: e.dma_start(wfsum.t[:], w["wf"]), writes=[wfsum.buf])
            for wg in (wisum, wfsum):
                P.emit("dve", lambda e, wg=wg: e.tensor_add(
                    wg.t[:, 0:1024], wg.t[:, 0:1024], wg.t[:, 1024:2048]),
                    reads=[wg.buf], writes=[wg.buf])
                P.emit("dve", lambda e, wg=wg: e.tensor_add(
                    wg.t[:, 0:1024], wg.t[:, 0:1024], wg.t[:, 2048:3072]),
                    reads=[wg.buf], writes=[wg.buf])
            wgT = P.sbuf("wgT", [128, NCT * 64], BF16, stack=p1)
            P.emit("dve", lambda e: e.memset(wgT.t[:], 0.0), writes=[wgT.buf])
            for c in range(NCT):
                P.emit("pe", lambda e, c=c: e.transpose(
                    P.pb[7].t[0:128, 0:4], wisum.t[:, c * 128:(c + 1) * 128],
                    cmn["ident"].t[0:4, 0:4]),
                    reads=[wisum.buf, cmn["ident"].buf], writes=[P.pb[7].buf])
                P.emit("dve", lambda e, c=c: e.tensor_copy(
                    wgT.t[:, c * 64: c * 64 + 4], P.pb[7].t[0:128, 0:4]),
                    reads=[P.pb[7].buf], writes=[wgT.buf])
                P.emit("pe", lambda e, c=c: e.transpose(
                    P.pb[7].t[0:128, 0:4], wfsum.t[:, c * 128:(c + 1) * 128],
                    cmn["ident"].t[0:4, 0:4]),
                    reads=[wfsum.buf, cmn["ident"].buf], writes=[P.pb[7].buf])
                P.emit("dve", lambda e, c=c: e.tensor_copy(
                    wgT.t[:, c * 64 + 32: c * 64 + 36], P.pb[7].t[0:128, 0:4]),
                    reads=[P.pb[7].buf], writes=[wgT.buf])
            bicol = P.sbuf("bicol", [4, 1], F32, stack=p1)
            bfcol = P.sbuf("bfcol", [4, 1], F32, stack=p1)
            P.dma("qs", lambda e: e.dma_start(bicol.t[:], w["bi"].unsqueeze(1)), writes=[bicol.buf])
            P.dma("qs", lambda e: e.dma_start(bfcol.t[:], w["bf"].unsqueeze(1)), writes=[bfcol.buf])
            lnch = P.sbuf("lnch1", [128, NCT * 512], BF16, stack=p1)
            for icx in range(nic):
                i0 = icx * 512
                for c in range(NCT):
                    P.dma("qs", lambda e, c=c, i0=i0: e.dma_start(
                        lnch.t[:, c * 512:(c + 1) * 512], ln0_d.t[:, c * S + i0: c * S + i0 + 512]),
                        reads=[ln0_d.buf], writes=[lnch.buf])
                def gmm(e):
                    i = None
                    for c in range(NCT):
                        i = e.matmul(P.pb[4].t[0:64, 0:512], wgT.t[:, c * 64:(c + 1) * 64],
                                     lnch.t[:, c * 512:(c + 1) * 512],
                                     start=(c == 0), stop=(c == NCT - 1))
                    return i
                P.emit("pe", gmm, reads=[wgT.buf, lnch.buf], writes=[P.pb[4].buf])
                P.emit("dve", lambda e, i0=i0: e.tensor_scalar_add(
                    ig_sb.t[:, i0:i0 + 512], P.pb[4].t[0:4, 0:512], bicol.t[:]),
                    reads=[P.pb[4].buf, bicol.buf], writes=[ig_sb.buf])
                P.emit("dve", lambda e, i0=i0: e.tensor_copy(
                    fg_sb.t[:, i0:i0 + 512], P.pb[4].t[32:36, 0:512]),
                    reads=[P.pb[4].buf], writes=[fg_sb.buf])
                P.emit("dve", lambda e, i0=i0: e.tensor_scalar_add(
                    fg_sb.t[:, i0:i0 + 512], fg_sb.t[:, i0:i0 + 512], bfcol.t[:]),
                    reads=[fg_sb.buf, bfcol.buf], writes=[fg_sb.buf])
            ones4 = P.sbuf("ones4", [4, 1], F32, stack=p1)
            P.emit("dve", lambda e: e.memset(ones4.t[:], 1.0), writes=[ones4.buf])
            tmp4 = P.sbuf("tmp4", [4, S], F32, stack=p1)
            P.emit("act", lambda e: e.activation(tmp4.t[:], fg_sb.t[:], AF.Abs),
                   reads=[fg_sb.buf], writes=[tmp4.buf])
            P.emit("act", lambda e: e.activation(tmp4.t[:], tmp4.t[:], AF.Exp, scale=-1.0),
                   reads=[tmp4.buf], writes=[tmp4.buf])
            P.emit("act", lambda e: e.activation(tmp4.t[:], tmp4.t[:], AF.Ln, bias=ones4.t[:]),
                   reads=[tmp4.buf, ones4.buf], writes=[tmp4.buf])
            P.emit("dve", lambda e: e.tensor_scalar_min(fg_sb.t[:], fg_sb.t[:], 0.0),
                   reads=[fg_sb.buf], writes=[fg_sb.buf])
            P.emit("dve", lambda e: e.tensor_sub(fg_sb.t[:], fg_sb.t[:], tmp4.t[:]),
                   reads=[fg_sb.buf, tmp4.buf], writes=[fg_sb.buf])
            P.emit("dve", lambda e: e.tensor_tensor_scan(
                cs_sb.t[:], fg_sb.t[:], fg_sb.t[:], 0.0, ALU.add, ALU.bypass),
                reads=[fg_sb.buf], writes=[cs_sb.buf])
            P.emit("dve", lambda e: e.tensor_sub(ig_sb.t[:], ig_sb.t[:], cs_sb.t[:]),
                   reads=[ig_sb.buf, cs_sb.buf], writes=[ig_sb.buf])
            P.emit("dve", lambda e: e.tensor_tensor_scan(
                pm_sb.t[:], ig_sb.t[:], ig_sb.t[:], -1e30, ALU.max, ALU.bypass),
                reads=[ig_sb.buf], writes=[pm_sb.buf])
            for t in range(NT):
                P.emit("pe", lambda e, t=t: e.transpose(
                    P.pb[7].t[0:128, 0:4], ig_sb.t[0:4, t * 128:(t + 1) * 128],
                    cmn["ident"].t[0:4, 0:4]),
                    reads=[ig_sb.buf, cmn["ident"].buf], writes=[P.pb[7].buf])
                P.emit("dve", lambda e, t=t: e.tensor_copy(
                    m_pt_all.t[:, t:t + 3 * NT + 1:NT], P.pb[7].t[0:128, 0:4]),
                    reads=[P.pb[7].buf], writes=[m_pt_all.buf])
            P.barrier_partial()
            P.flush("l0a_gates")

        with contextlib.ExitStack() as p2:
            gcol = P.sbuf("gcol", [128, NCT], F32, stack=p2)
            with contextlib.ExitStack() as vph:
                emit_vecs_to_cols(P, [w["ln_w"]], gcol, NCT, cmn["ident"], P.pb[7], vph, tag="lw")
                P.barrier_partial()
                P.flush("l0a_lw")
            P.emit("dve", lambda e: e.tensor_scalar_add(gcol.t[:], gcol.t[:], 1.0),
                   reads=[gcol.buf], writes=[gcol.buf])
            wq3 = P.sbuf("wq3", [128, NCT * 128], BF16, stack=p2)
            wk3 = P.sbuf("wk3", [128, NCT * 128], BF16, stack=p2)
            wv3 = P.sbuf("wv3", [128, NCT * 128], BF16, stack=p2)
            wf3 = P.sbuf("wf3", [128, NCT * 128], F32, stack=p2)
            for name, dst in (("wqbd", wq3), ("wkbd", wk3), ("wvbd", wv3)):
                for c in range(NCT):
                    P.dma("qs", lambda e, name=name, c=c: e.dma_start(
                        wf3.t[:, c * 128:(c + 1) * 128], w[name][c]),
                        writes=[wf3.buf])
                P.emit("act", lambda e, dst=dst: e.activation(dst.t[:], wf3.t[:], AF.Copy),
                       reads=[wf3.buf], writes=[dst.buf])
            masks = P.sbuf("maskssb", [128, 4 * 512], F32, stack=p2)
            P.dma("qs", lambda e: e.dma_start(masks.t[:], w["masks"]), writes=[masks.buf])

            ln0h = P.sbuf("ln0h", [128, 2 * S], BF16, stack=p2)
            qTh = P.sbuf("qTh0", [128, 2 * S], BF16, stack=p2)
            kTh = P.sbuf("kTh0", [128, 2 * S], BF16, stack=p2)
            vpl = P.sbuf("vpl", [128, NT * 258], BF16, stack=p2)
            pm_ch = P.sbuf("pm_ch", [1, 2 * 512], F32, stack=p2)
            cs_ch = P.sbuf("cs_ch", [1, 2 * 512], F32, stack=p2)
            eI = P.sbuf("eI", [128, 2 * NT], F32, stack=p2)
            negA = P.sbuf("negA", [128, 2], F32, stack=p2)
            srow = P.sbuf("srow", [1, 2 * 512], F32, stack=p2)
            hraw = P.sbuf("hraw", [128, 2 * 1024], F32, stack=p2)
            pmB = [Buf("pmA"), Buf("pmB")]
            csB = [Buf("csA"), Buf("csB")]
            eIB = [Buf("eIA"), Buf("eIB")]
            nAB = [Buf("nA0"), Buf("nA1")]
            srB = [Buf("srA"), Buf("srB")]
            hrB = [Buf("hrA"), Buf("hrB")]
            pending_ep = []
            cbf = P.sbuf("cbf", [128, 3 * 512], BF16, stack=p2)
            ctmp = P.sbuf("ctmp", [128, 512], F32, stack=p2)
            r512a = P.sbuf("r512a", [1, 512], F32, stack=p2)
            r512b = P.sbuf("r512b", [1, 512], F32, stack=p2)
            rb0 = P.sbuf("rb00", [128, 512], F32, stack=p2)
            hout = P.sbuf("hout", [128, 2 * 512], F32, stack=p2)
            hcast = P.sbuf("hcast", [128, 2 * 512], BF16, stack=p2)
            hsq = P.sbuf("hsq", [128, 2 * 512], BF16, stack=p2)
            srows = P.sbuf("srows", [1, 3 * 512], F32, stack=p2)
            xs0 = P.sbuf("xs00", [128, 512], F32, stack=p2)
            oo0 = P.sbuf("oo00", [128, 512], F32, stack=p2)

            for h in range(NH):
                for dt_ in range(2):
                    P.dma("qs", lambda e, h=h, dt_=dt_: e.dma_start(
                        ln0h.t[:, dt_ * S:(dt_ + 1) * S],
                        ln0_d.t[:, (2 * h + dt_) * S:(2 * h + dt_ + 1) * S]),
                        reads=[ln0_d.buf], writes=[ln0h.buf])
                for dt_ in range(2):
                    for icx in range(nic):
                        i0 = icx * 512
                        for which, wbd, dstT, sc in (("q", wq3, qTh, 1.0), ("k", wk3, kTh, 1.0 / 16.0)):
                            bk = P.pb[5]
                            P.emit("pe", lambda e, h=h, dt_=dt_, i0=i0, wbd=wbd, bk=bk: e.matmul(
                                bk.t[:, 0:512], wbd.t[:, (2 * h + dt_) * 128:(2 * h + dt_ + 1) * 128],
                                ln0h.t[:, dt_ * S + i0: dt_ * S + i0 + 512], start=True, stop=True),
                                reads=[wbd.buf, ln0h.buf], writes=[bk.buf])
                            P.emit("act", lambda e, dstT=dstT, dt_=dt_, i0=i0, sc=sc, bk=bk: e.activation(
                                dstT.t[:, dt_ * S + i0: dt_ * S + i0 + 512], bk.t[:, 0:512],
                                AF.Copy, scale=sc),
                                reads=[bk.buf], writes=[dstT.buf])
                P.emit("dve", lambda e: e.memset(vpl.t[:], 1.0), writes=[vpl.buf])
                for tt in range(NT):
                    for dt_ in range(2):
                        bk = P.pb[5]
                        P.emit("pe", lambda e, h=h, tt=tt, dt_=dt_, bk=bk: e.matmul(
                            bk.t[:, 0:128],
                            ln0h.t[:, dt_ * S + tt * 128: dt_ * S + tt * 128 + 128],
                            wv3.t[:, (2 * h + dt_) * 128:(2 * h + dt_ + 1) * 128],
                            start=True, stop=True),
                            reads=[ln0h.buf, wv3.buf], writes=[bk.buf])
                        P.emit("act", lambda e, tt=tt, dt_=dt_, bk=bk: e.activation(
                            vpl.t[:, tt * 258 + dt_ * 128: tt * 258 + dt_ * 128 + 128],
                            bk.t[:, 0:128], AF.Copy),
                            reads=[bk.buf], writes=[vpl.buf])
                for icx in range(nic):
                    i0 = icx * 512
                    jmax = (i0 + 512) // 128 - 1
                    par = (h * nic + icx) % 2
                    po = par * 512
                    P.dma("qs", lambda e, h=h, i0=i0, po=po: e.dma_start(
                        pm_ch.t[0:1, po:po + 512], pm_sb.t[h:h + 1, i0:i0 + 512]),
                        reads=[pm_sb.buf], writes=[pmB[par]])
                    P.dma("qs", lambda e, h=h, i0=i0, po=po: e.dma_start(
                        cs_ch.t[0:1, po:po + 512], cs_sb.t[h:h + 1, i0:i0 + 512]),
                        reads=[cs_sb.buf], writes=[csB[par]])
                    P.emit("pe", lambda e, po=po: e.matmul(
                        P.pb[7].t[0:128, 0:1], cmn["onesrow"].t[:],
                        pm_ch.t[0:1, po + 511:po + 512], start=True, stop=True),
                        reads=[cmn["onesrow"].buf, pmB[par]], writes=[P.pb[7].buf])
                    P.emit("act", lambda e, par=par: e.activation(
                        negA.t[:, par:par + 1], P.pb[7].t[0:128, 0:1], AF.Copy, scale=-1.0),
                        reads=[P.pb[7].buf], writes=[nAB[par]])
                    P.emit("act", lambda e, jmax=jmax, h=h, par=par: e.activation(
                        eI.t[:, par * NT: par * NT + jmax + 1],
                        m_pt_all.t[:, h * NT: h * NT + jmax + 1],
                        AF.Exp, bias=negA.t[:, par:par + 1]),
                        reads=[m_pt_all.buf, nAB[par]], writes=[eIB[par]])
                    cbfB = [Buf("cbfA"), Buf("cbfB"), Buf("cbfC")]
                    LA0 = 2
                    def em_qk(J, i0=i0, jmax=jmax, par=par):
                        bqk = P.pb[4 + (J % 3)]
                        def mmqk(e, J=J, i0=i0, bqk=bqk):
                            i = None
                            for dt_ in range(2):
                                i = e.matmul(bqk.t[:, 0:512],
                                             kTh.t[:, dt_ * S + J * 128: dt_ * S + J * 128 + 128],
                                             qTh.t[:, dt_ * S + i0: dt_ * S + i0 + 512],
                                             start=(dt_ == 0), stop=(dt_ == 1))
                            return i
                        P.emit("pe", mmqk, reads=[kTh.buf, qTh.buf], writes=[bqk.buf])
                        dJ = J - i0 // 128
                        ch = (J % 3) * 512
                        cB = cbfB[J % 3]
                        if dJ < 0:
                            if J % 2 == 0:
                                P.emit("act", lambda e, J=J, ch=ch, bqk=bqk, par=par: e.activation(
                                    cbf.t[:, ch:ch + 512], bqk.t[:, 0:512], AF.Copy,
                                    scale=eI.t[:, par * NT + J: par * NT + J + 1]),
                                    reads=[bqk.buf, eIB[par]], writes=[cB])
                            else:
                                P.emit("dve", lambda e, J=J, ch=ch, bqk=bqk, par=par: e.tensor_scalar(
                                    cbf.t[:, ch:ch + 512], bqk.t[:, 0:512],
                                    eI.t[:, par * NT + J: par * NT + J + 1], None, ALU.mult),
                                    reads=[bqk.buf, eIB[par]], writes=[cB])
                        else:
                            P.emit("act", lambda e, J=J, bqk=bqk, par=par: e.activation(
                                ctmp.t[:], bqk.t[:, 0:512], AF.Copy,
                                scale=eI.t[:, par * NT + J: par * NT + J + 1]),
                                reads=[bqk.buf, eIB[par]], writes=[ctmp.buf])
                            P.emit("dve", lambda e, dJ=dJ, ch=ch: e.tensor_mul(
                                cbf.t[:, ch:ch + 512], ctmp.t[:], masks.t[:, dJ * 512:(dJ + 1) * 512]),
                                reads=[ctmp.buf, masks.buf], writes=[cB])
                    def em_av3(J, jmax=jmax):
                        ch = (J % 3) * 512
                        cB = cbfB[J % 3]
                        for mi, (mof, mw, bk) in enumerate(((0, 128, P.pb[0]), (128, 128, P.pb[1]),
                                                            (256, 1, P.pb[2]))):
                            P.emit("pe", lambda e, J=J, ch=ch, mof=mof, mw=mw, bk=bk: e.matmul(
                                bk.t[0:mw, 0:512], vpl.t[:, J * 258 + mof: J * 258 + mof + mw],
                                cbf.t[:, ch:ch + 512], start=(J == 0), stop=(J == jmax)),
                                reads=[vpl.buf, cB], writes=[bk.buf])
                    for J in range(jmax + 1):
                        em_qk(J)
                        if J >= LA0:
                            em_av3(J - LA0)
                    for J in range(max(0, jmax + 1 - LA0), jmax + 1):
                        em_av3(J)
                    P.emit("dve", lambda e, po=po: e.tensor_copy(srow.t[0:1, po:po + 512],
                                                                 P.pb[2].t[0:1, 0:512]),
                           reads=[P.pb[2].buf], writes=[srB[par]])
                    P.emit("dve", lambda e, par=par: e.tensor_copy(hraw.t[:, par * 1024: par * 1024 + 512],
                                                                   P.pb[0].t[:, 0:512]),
                           reads=[P.pb[0].buf], writes=[hrB[par]])
                    P.emit("dve", lambda e, par=par: e.tensor_copy(hraw.t[:, par * 1024 + 512: par * 1024 + 1024],
                                                                   P.pb[1].t[:, 0:512]),
                           reads=[P.pb[1].buf], writes=[hrB[par]])

                    def epilogue(h=h, i0=i0, par=par, po=po):
                        hof = par * 1024
                        P.emit("act", lambda e: e.activation(
                            r512a.t[:], srow.t[0:1, po:po + 512], AF.Abs),
                            reads=[srB[par]], writes=[r512a.buf])
                        P.emit("act", lambda e: e.activation(
                            r512b.t[:], cs_ch.t[0:1, po:po + 512], AF.Exp, scale=-1.0,
                            bias=negA.t[0:1, par:par + 1]),
                            reads=[csB[par], nAB[par]], writes=[r512b.buf])
                        P.emit("dve", lambda e: e.tensor_max(r512a.t[:], r512a.t[:], r512b.t[:]),
                               reads=[r512a.buf, r512b.buf], writes=[r512a.buf])
                        P.emit("act", lambda e: e.activation(
                            r512b.t[:], pm_ch.t[0:1, po:po + 512], AF.Exp, bias=negA.t[0:1, par:par + 1]),
                            reads=[pmB[par], nAB[par]], writes=[r512b.buf])
                        P.emit("dve", lambda e: e.scalar_tensor_tensor(
                            r512a.t[:], r512b.t[:], EPS0, r512a.t[:], ALU.mult, ALU.add),
                            reads=[r512a.buf, r512b.buf], writes=[r512a.buf])
                        P.emit("dve", lambda e: e.reciprocal(r512a.t[:], r512a.t[:]),
                               reads=[r512a.buf], writes=[r512a.buf])
                        P.emit("pe", lambda e: e.matmul(
                            P.pb[7].t[0:128, 0:512], cmn["onesrow"].t[:], r512a.t[:],
                            start=True, stop=True),
                            reads=[cmn["onesrow"].buf, r512a.buf], writes=[P.pb[7].buf])
                        P.emit("act", lambda e: e.activation(rb0.t[:], P.pb[7].t[0:128, 0:512], AF.Copy),
                               reads=[P.pb[7].buf], writes=[rb0.buf])
                        P.emit("dve", lambda e: e.tensor_mul(hout.t[:, 0:512],
                                                             hraw.t[:, hof:hof + 512], rb0.t[:]),
                               reads=[hrB[par], rb0.buf], writes=[hout.buf])
                        P.emit("dve", lambda e: e.tensor_mul(hout.t[:, 512:1024],
                                                             hraw.t[:, hof + 512:hof + 1024], rb0.t[:]),
                               reads=[hrB[par], rb0.buf], writes=[hout.buf])
                        for ti in range(2):
                            sl = slice(ti * 512, ti * 512 + 512)
                            P.emit("act", lambda e, sl=sl: e.activation(hcast.t[:, sl], hout.t[:, sl], AF.Copy),
                                   reads=[hout.buf], writes=[hcast.buf])
                            P.emit("act", lambda e, sl=sl: e.activation(hsq.t[:, sl], hout.t[:, sl], AF.Square),
                                   reads=[hout.buf], writes=[hsq.buf])
                            P.emit("pe", lambda e, sl=sl, ti=ti: e.matmul(
                                P.pb[3].t[0:1, 0:512], cmn["onesb"].t[:], hcast.t[:, sl],
                                start=(ti == 0), stop=(ti == 1)),
                                reads=[hcast.buf, cmn["onesb"].buf], writes=[P.pb[3].buf])
                            P.emit("pe", lambda e, sl=sl, ti=ti: e.matmul(
                                P.pb[3].t[32:33, 0:512], cmn["onesb"].t[:], hsq.t[:, sl],
                                start=(ti == 0), stop=(ti == 1)),
                                reads=[hsq.buf, cmn["onesb"].buf], writes=[P.pb[3].buf])
                        mr = srows.t[0:1, 0:512]; qr = srows.t[0:1, 512:1024]; vr = srows.t[0:1, 1024:1536]
                        P.emit("dve", lambda e: e.tensor_scalar_mul(mr, P.pb[3].t[0:1, 0:512], 1.0 / DHh),
                               reads=[P.pb[3].buf], writes=[srows.buf])
                        P.emit("dve", lambda e: e.tensor_scalar_mul(qr, P.pb[3].t[32:33, 0:512], 1.0 / DHh),
                               reads=[P.pb[3].buf], writes=[srows.buf])
                        P.emit("dve", lambda e: e.tensor_mul(vr, mr, mr), reads=[srows.buf], writes=[srows.buf])
                        P.emit("dve", lambda e: e.tensor_sub(vr, qr, vr), reads=[srows.buf], writes=[srows.buf])
                        P.emit("act", lambda e: e.activation(vr, vr, AF.Sqrt, bias=cmn["eps1"].t[0:1, 0:1]),
                               reads=[srows.buf, cmn["eps1"].buf], writes=[srows.buf])
                        P.emit("dve", lambda e: e.reciprocal(vr, vr), reads=[srows.buf], writes=[srows.buf])
                        P.emit("pe", lambda e: e.matmul(P.pb[3].t[0:128, 0:512], cmn["onesrow"].t[:], mr,
                                                        start=True, stop=True),
                               reads=[srows.buf, cmn["onesrow"].buf], writes=[P.pb[3].buf])
                        P.emit("pe", lambda e: e.matmul(P.pb[7].t[0:128, 0:512], cmn["onesrow"].t[:], vr,
                                                        start=True, stop=True),
                               reads=[srows.buf, cmn["onesrow"].buf], writes=[P.pb[7].buf])
                        for ti in range(2):
                            ct = 2 * h + ti
                            sl = slice(ti * 512, ti * 512 + 512)
                            P.dma("qs", lambda e, ct=ct, i0=i0: e.dma_start(
                                xs0.t[:], xin_ap[:, ct * S + i0: ct * S + i0 + 512]),
                                writes=[xs0.buf])
                            P.emit("dve", lambda e, sl=sl: e.tensor_sub(oo0.t[:], hout.t[:, sl], P.pb[3].t[0:128, 0:512]),
                                   reads=[hout.buf, P.pb[3].buf], writes=[oo0.buf])
                            P.emit("dve", lambda e: e.tensor_mul(oo0.t[:], oo0.t[:], P.pb[7].t[0:128, 0:512]),
                                   reads=[oo0.buf, P.pb[7].buf], writes=[oo0.buf])
                            P.emit("dve", lambda e, ct=ct: e.tensor_scalar(
                                oo0.t[:], oo0.t[:], gcol.t[:, ct:ct + 1], None, ALU.mult),
                                reads=[oo0.buf, gcol.buf], writes=[oo0.buf])
                            P.emit("dve", lambda e: e.tensor_add(oo0.t[:], oo0.t[:], xs0.t[:]),
                                   reads=[oo0.buf, xs0.buf], writes=[oo0.buf])
                            P.dma("qa", lambda e, ct=ct, i0=i0: e.dma_start(
                                out_ap[:, ct * S + i0: ct * S + i0 + 512], oo0.t[:]),
                                reads=[oo0.buf], writes=[outbuf])

                    if pending_ep:
                        pending_ep.pop(0)()
                    pending_ep.append(epilogue)
            while pending_ep:
                pending_ep.pop(0)()
            P.barrier_partial()
            P.flush("l0a_main")


from concourse.bass_utils import run_bass_kernel_spmd

S_FULL = 4096
NCORES = 8


def _consts(S):
    c = {}
    c["ident"] = np.eye(128, dtype=np.float32)
    sel8 = np.zeros((8, 8 * 128), np.float32)
    for e in range(8):
        sel8[e, e * 128:(e + 1) * 128] = 1.0
    c["sel8"] = sel8
    masks = np.zeros((128, 4 * 512), np.float32)
    ii = np.arange(512)
    for dJ in range(4):
        for jj in range(128):
            masks[jj, dJ * 512:(dJ + 1) * 512] = (ii - dJ * 128 >= jj).astype(np.float32)
    c["masks"] = masks
    dim = 64
    inv_freq = 1.0 / 10000.0 ** (np.arange(0, dim, 2, dtype=np.float32) / dim)
    t = np.arange(S, dtype=np.float32)
    freqs = np.outer(t, inv_freq)
    cos = np.concatenate([np.cos(freqs), np.cos(freqs)], -1)[:, ::2]
    sin = np.concatenate([np.sin(freqs), np.sin(freqs)], -1)[:, ::2]
    A = np.zeros((64, S), np.float32)
    Bm = np.zeros((64, S), np.float32)
    for cc in range(32):
        A[2 * cc] = cos[:, cc]
        A[2 * cc + 1] = cos[:, cc]
        Bm[2 * cc] = -sin[:, cc]
        Bm[2 * cc + 1] = sin[:, cc]
    c["ropeA"] = A
    c["ropeB"] = Bm
    sw = np.zeros((64, 64), np.float32)
    for i in range(64):
        sw[i ^ 1, i] = 1.0
    c["swap64"] = sw
    return c


def _bd_chunks(wmat):
    out = np.zeros((8, 128, 128), np.float32)
    wr = wmat.reshape(8, 32, 4, 4)
    gg, o_, i_ = np.meshgrid(np.arange(32), np.arange(4), np.arange(4), indexing="ij")
    rows = (4 * gg + i_).ravel()
    cols = (4 * gg + o_).ravel()
    for cchunk in range(8):
        out[cchunk, rows, cols] = wr[cchunk][gg.ravel(), o_.ravel(), i_.ravel()]
    return out


def _decl(nc, ins, name, arr, dtype=F32):
    ap = nc.dram_tensor(name, list(arr.shape), dtype, kind="ExternalInput").ap()
    ins[name] = np.ascontiguousarray(arr)
    return ap


def _build_k0(S, con, weights):
    nc = bass.Bass("TRN2", target_bir_lowering=False)
    ins = {}
    xin = nc.dram_tensor("xin", [128, 8 * S], F32, kind="ExternalInput").ap()
    w = {}
    w["ident"] = _decl(nc, ins, "ident", con["ident"])
    w["masks"] = _decl(nc, ins, "masks", con["masks"])
    w["wqbd"] = _decl(nc, ins, "wqbd", _bd_chunks(weights["L0_attn_wq"]))
    w["wkbd"] = _decl(nc, ins, "wkbd", _bd_chunks(weights["L0_attn_wk"]))
    w["wvbd"] = _decl(nc, ins, "wvbd", _bd_chunks(weights["L0_attn_wv"]))
    for nm, key in (("wi", "L0_attn_wi"), ("bi", "L0_attn_bi"), ("wf", "L0_attn_wf"),
                    ("bf", "L0_attn_bf"), ("ln_w", "L0_attn_ln_w")):
        w[nm] = _decl(nc, ins, nm, weights[key])
    y = nc.dram_tensor("y", [128, 8 * S], F32, kind="ExternalOutput").ap()
    P = Prog(nc)
    build_l0attn(P, S, xin, w, y, Buf("y"))
    P.finish()
    return nc, ins


def _build_k1(S, con, weights):
    nc = bass.Bass("TRN2", target_bir_lowering=False)
    ins = {}
    xin = nc.dram_tensor("xin", [128, 8 * S], F32, kind="ExternalInput").ap()
    w = {}
    w["ident"] = _decl(nc, ins, "ident", con["ident"])
    w["sel8"] = _decl(nc, ins, "sel8", con["sel8"])
    for nm, key in (("Wg", "L0_ffn_Wg"), ("bg", "L0_ffn_bg"), ("W1", "L0_ffn_W1"),
                    ("b1", "L0_ffn_b1"), ("W2", "L0_ffn_W2"), ("b2", "L0_ffn_b2")):
        w[nm] = _decl(nc, ins, nm, weights[key])
    y = nc.dram_tensor("y", [128, 8 * S], F32, kind="ExternalOutput").ap()
    P = Prog(nc)
    build_l0ffn(P, S, xin, w, y, Buf("y"))
    P.finish()
    return nc, ins


def _build_k2(S, con, weights):
    nc = bass.Bass("TRN2", target_bir_lowering=False)
    ins = {}
    xin = nc.dram_tensor("xin", [128, 8 * S], F32, kind="ExternalInput").ap()
    w = {}
    w["ident"] = _decl(nc, ins, "ident", con["ident"])
    w["ropeA"] = _decl(nc, ins, "ropeA", con["ropeA"])
    w["ropeB"] = _decl(nc, ins, "ropeB", con["ropeB"])
    w["swap64"] = _decl(nc, ins, "swap64", con["swap64"])
    for nm in ("w_down_q", "w_down_kv", "w_up_q", "w_up_k", "w_up_v", "w_out",
               "ln_scale", "ln_bias"):
        w[nm] = _decl(nc, ins, nm, weights["L1_attn_" + nm])
    y = nc.dram_tensor("y", [128, 8 * S], F32, kind="ExternalOutput").ap()
    P = Prog(nc)
    build_l1attn(P, S, xin, w, y, Buf("y"))
    P.finish()
    return nc, ins


def _build_k3(S, con, weights):
    nc = bass.Bass("TRN2", target_bir_lowering=False)
    ins = {}
    xin = nc.dram_tensor("xin", [128, 8 * S], F32, kind="ExternalInput").ap()
    w = {}
    w["ident"] = _decl(nc, ins, "ident", con["ident"])
    for nm in ("w1_w", "w1_s", "w2_w", "w2_s", "h1_w", "h1_s", "h2_w", "h2_s",
               "w3_w", "w3_s"):
        w[nm] = _decl(nc, ins, nm, weights["L1_ffn_" + nm])
    y = nc.dram_tensor("y", [S, D], F32, kind="ExternalOutput").ap()
    P = Prog(nc)
    build_l1ffn(P, S, xin, w, "tok", y, Buf("y"))
    P.finish()
    return nc, ins


def kernel_multi(**inputs):
    S = S_FULL
    x = np.asarray(inputs["L0_attn_x"], dtype=np.float32)
    con = _consts(S)
    xf = [np.ascontiguousarray(x[i].reshape(S, 8, 128).transpose(2, 1, 0).reshape(128, 8 * S))
          for i in range(NCORES)]

    nc0, ins0 = _build_k0(S, con, inputs)
    maps = [dict(ins0, xin=xf[i]) for i in range(NCORES)]
    r = run_bass_kernel_spmd(nc0, maps, core_ids=list(range(NCORES)))
    cur = [r.results[i]["y"] for i in range(NCORES)]

    nc1, ins1 = _build_k1(S, con, inputs)
    maps = [dict(ins1, xin=cur[i]) for i in range(NCORES)]
    r = run_bass_kernel_spmd(nc1, maps, core_ids=list(range(NCORES)))
    cur = [r.results[i]["y"] for i in range(NCORES)]

    nc2, ins2 = _build_k2(S, con, inputs)
    maps = [dict(ins2, xin=cur[i]) for i in range(NCORES)]
    r = run_bass_kernel_spmd(nc2, maps, core_ids=list(range(NCORES)))
    cur = [r.results[i]["y"] for i in range(NCORES)]

    nc3, ins3 = _build_k3(S, con, inputs)
    maps = [dict(ins3, xin=cur[i]) for i in range(NCORES)]
    r = run_bass_kernel_spmd(nc3, maps, core_ids=list(range(NCORES)))
    out = np.stack([r.results[i]["y"] for i in range(NCORES)], axis=0)
    return out.astype(np.float32)


def _build_fused(S, con, weights):
    nc = bass.Bass("TRN2", target_bir_lowering=False)
    ins = {}
    xin = nc.dram_tensor("xin", [128, 8 * S], F32, kind="ExternalInput").ap()
    w = {}
    w["ident"] = _decl(nc, ins, "ident", con["ident"])
    w["masks"] = _decl(nc, ins, "masks", con["masks"])
    w["sel8"] = _decl(nc, ins, "sel8", con["sel8"])
    w["ropeA"] = _decl(nc, ins, "ropeA", con["ropeA"])
    w["ropeB"] = _decl(nc, ins, "ropeB", con["ropeB"])
    w["swap64"] = _decl(nc, ins, "swap64", con["swap64"])
    w["wqbd"] = _decl(nc, ins, "wqbd", _bd_chunks(weights["L0_attn_wq"]))
    w["wkbd"] = _decl(nc, ins, "wkbd", _bd_chunks(weights["L0_attn_wk"]))
    w["wvbd"] = _decl(nc, ins, "wvbd", _bd_chunks(weights["L0_attn_wv"]))
    for nm, key in (("wi", "L0_attn_wi"), ("bi", "L0_attn_bi"), ("wf", "L0_attn_wf"),
                    ("bf", "L0_attn_bf"), ("ln_w", "L0_attn_ln_w"),
                    ("Wg", "L0_ffn_Wg"), ("bg", "L0_ffn_bg"), ("W1", "L0_ffn_W1"),
                    ("b1", "L0_ffn_b1"), ("W2", "L0_ffn_W2"), ("b2", "L0_ffn_b2")):
        w[nm] = _decl(nc, ins, nm, weights[key])
    for nm in ("w_down_q", "w_down_kv", "w_up_q", "w_up_k", "w_up_v", "w_out",
               "ln_scale", "ln_bias"):
        w[nm] = _decl(nc, ins, nm, weights["L1_attn_" + nm])
    for nm in ("w1_w", "w1_s", "w2_w", "w2_s", "h1_w", "h1_s", "h2_w", "h2_s",
               "w3_w", "w3_s"):
        w[nm] = _decl(nc, ins, nm, weights["L1_ffn_" + nm])
    a1 = nc.dram_tensor("a1", [128, 8 * S], F32).ap()
    a2 = nc.dram_tensor("a2", [128, 8 * S], F32).ap()
    a3 = nc.dram_tensor("a3", [128, 8 * S], F32).ap()
    y = nc.dram_tensor("y", [S, D], F32, kind="ExternalOutput").ap()
    P = Prog(nc)
    b1 = Buf("a1"); b2 = Buf("a2"); b3 = Buf("a3")
    import contextlib as _ctx
    ps0 = _ctx.ExitStack()
    wt = make_prep_bufs(P, 2048, stack=ps0)
    pre0 = emit_l0ffn_prep(P, w, wt, cw=2048)
    build_l0attn(P, S, xin, w, a1, b1)
    pre1bf = emit_l1ffn_cast(P, w, wt, cw=2048)
    build_l0ffn(P, S, a1, w, a2, b2, prep=pre0)
    ps0.close()
    ps1 = _ctx.ExitStack()
    tt = {"t": P.sbuf("wprep_t", [128, 1 * 1024], BF16, stack=ps1), "nslot": 1}
    pre1 = emit_l1ffn_transpose(P, pre1bf, tt, ow=1024)
    build_l1attn(P, S, a2, w, a3, b3)
    ps1.close()
    build_l1ffn(P, S, a3, w, "tok", y, Buf("y"), prep=pre1)
    P.finish()
    return nc, ins


def kernel(**inputs):
    S = S_FULL
    x = np.asarray(inputs["L0_attn_x"], dtype=np.float32)
    con = _consts(S)
    xf = [np.ascontiguousarray(x[i].reshape(S, 8, 128).transpose(2, 1, 0).reshape(128, 8 * S))
          for i in range(NCORES)]
    nc, ins = _build_fused(S, con, inputs)
    maps = [dict(ins, xin=xf[i]) for i in range(NCORES)]
    r = run_bass_kernel_spmd(nc, maps, core_ids=list(range(NCORES)))
    return np.stack([r.results[i]["y"] for i in range(NCORES)], axis=0).astype(np.float32)
